# version 01 (final)
# baseline (speedup 1.0000x reference)
import math
import numpy as np
import ml_dtypes
import concourse.bass as bass
import concourse.mybir as mybir
from concourse.bass_utils import run_bass_kernel_spmd

F32 = mybir.dt.float32
BF16 = mybir.dt.bfloat16
U8 = mybir.dt.uint8
I32 = mybir.dt.int32
AF = mybir.ActivationFunctionType
ALU = mybir.AluOpType
AX = mybir.AxisListType

BF = ml_dtypes.bfloat16
T = 2048
D = 1024
NCORES = 8
GR = 512

ENGINES = ("tensor", "vector", "scalar", "gpsimd", "sync")
DMA_QUEUES = ("sync", "gpsimd", "scalar")


class Node:
    __slots__ = ("eng", "fn", "rd", "wr", "dma", "idx", "deps", "sem_id",
                 "inc_val", "tag", "lane")

    def __init__(self, eng, fn, rd, wr, dma, idx, tag):
        self.eng, self.fn, self.rd, self.wr, self.dma, self.idx = \
            eng, fn, rd, wr, dma, idx
        self.deps = None
        self.inc_val = None
        self.tag = tag


def _iv(name, lo, hi):
    return [(name, b) for b in range(int(lo) // GR, (int(hi) + GR - 1) // GR)]


class KB:
    def __init__(self):
        self.nodes = []
        self.same_engine_edges = False

    def op(self, eng, fn, rd=(), wr=(), dma=False, tag=""):
        n = Node(eng, fn, list(rd), list(wr), dma, len(self.nodes), tag)
        self.nodes.append(n)
        return n

    def dma(self, queue, fn, rd=(), wr=(), tag=""):
        return self.op(queue, fn, rd, wr, dma=True, tag=tag)

    def resolve(self):
        writer = {}
        readers = {}
        for n in self.nodes:
            deps = {}
            rk = []
            for (nm, lo, hi) in n.rd:
                rk += _iv(nm, lo, hi)
            wk = []
            for (nm, lo, hi) in n.wr:
                wk += _iv(nm, lo, hi)
            for key in rk:
                w = writer.get(key)
                if w is not None:
                    deps[w.idx] = w
            for key in wk:
                w = writer.get(key)
                if w is not None:
                    deps[w.idx] = w
                for r in readers.get(key, ()):
                    deps[r.idx] = r
            n.deps = []
            for d in deps.values():
                if d.eng == n.eng == "tensor" and not d.dma and not n.dma:
                    continue
                n.deps.append(d)
            for key in rk:
                readers.setdefault(key, []).append(n)
            for key in wk:
                writer[key] = n
                readers[key] = [n]

    def emit(self, nc, extra_final=()):
        self.resolve()
        for n in self.nodes:
            for d in n.deps:
                d.sem_id = True
        NLANES = 6
        ccnt = {e: 0 for e in ENGINES}
        dcnt = {e: 0 for e in ENGINES}
        lanecnt = {}
        for n in self.nodes:
            if n.dma:
                lane = dcnt[n.eng] % NLANES
                dcnt[n.eng] += 1
                n.lane = lane
                key = (n.eng, lane)
                lanecnt[key] = lanecnt.get(key, 0) + 1
                n.inc_val = 16 * lanecnt[key]
            elif getattr(n, "sem_id", None):
                ccnt[n.eng] += 1
                n.inc_val = ccnt[n.eng]
        sems = {}
        import contextlib
        stack = contextlib.ExitStack()
        for e in ENGINES:
            sems[("c", e)] = stack.enter_context(nc.semaphore(f"sem_c_{e}"))
            if dcnt[e]:
                for l in range(NLANES):
                    sems[("d", e, l)] = stack.enter_context(nc.semaphore(f"sem_d_{e}{l}"))
        self._sems = sems
        self._stack = stack

        per_eng = {e: [] for e in ENGINES}
        for n in self.nodes:
            per_eng[n.eng].append(n)

        def mk_section(eng_name):
            nodes = per_eng[eng_name]

            def section(eng):
                waited = {}
                for n in nodes:
                    for d in n.deps:
                        k = ("d", d.eng, d.lane) if d.dma else ("c", d.eng)
                        v = d.inc_val
                        if waited.get(k, 0) >= v:
                            continue
                        eng.wait_ge(sems[k], v)
                        waited[k] = v
                    if n.dma and n.inc_val > 16:
                        k = ("d", n.eng, n.lane)
                        if waited.get(k, 0) < n.inc_val - 16:
                            eng.wait_ge(sems[k], n.inc_val - 16)
                            waited[k] = n.inc_val - 16
                    inst = n.fn(eng)
                    if n.inc_val is not None:
                        k = ("d", n.eng, n.lane) if n.dma else ("c", n.eng)
                        if inst is None:
                            raise RuntimeError(f"node {n.tag} needs inc but fn returned None")
                        inst.then_inc(sems[k], 16 if n.dma else 1)
                if eng_name == "sync":
                    for (tag, *rest), v in sorted(lanecnt.items(), key=str):
                        pass
                    for key, v in sorted(lanecnt.items(), key=str):
                        eng.wait_ge(sems[("d", key[0], key[1])], 16 * v)

            return section

        with nc.Block() as block:
            block.tensor(mk_section("tensor"))
            block.vector(mk_section("vector"))
            block.scalar(mk_section("scalar"))
            block.gpsimd(mk_section("gpsimd"))
            block.sync(mk_section("sync"))
        stack.close()


class Reg:
    def __init__(self, tensor, name, off, nbytes):
        self.t, self.name, self.off, self.nbytes = tensor, name, off, nbytes

    def ap(self, dtype=BF16, cols=None, coff=0, p0=0, pn=128):
        sz = mybir.dt.size(dtype)
        base = self.off // sz
        a = self.t.ap().bitcast(dtype)
        if cols is None:
            cols = self.nbytes // sz - coff
        return a[p0:p0 + pn, base + coff: base + coff + cols]

    def key(self, dtype=None, cols=None, coff=0):
        if dtype is None:
            return (self.name, self.off, self.off + self.nbytes)
        sz = mybir.dt.size(dtype)
        if cols is None:
            return (self.name, self.off + coff * sz, self.off + self.nbytes)
        return (self.name, self.off + coff * sz, self.off + (coff + cols) * sz)


class Arena:
    def __init__(self, tensor, name, nbytes):
        self.t, self.name, self.size = tensor, name, nbytes
        self.free_list = [(0, nbytes)]
        self.allocs = {}

    def alloc(self, tag, nbytes, top=False):
        nbytes = (nbytes + 511) & ~511
        fl = list(enumerate(self.free_list))
        if top:
            fl = fl[::-1]
        for i, (off, sz) in fl:
            if sz >= nbytes:
                if top:
                    r = Reg(self.t, self.name, off + sz - nbytes, nbytes)
                    if sz == nbytes:
                        self.free_list.pop(i)
                    else:
                        self.free_list[i] = (off, sz - nbytes)
                else:
                    r = Reg(self.t, self.name, off, nbytes)
                    if sz == nbytes:
                        self.free_list.pop(i)
                    else:
                        self.free_list[i] = (off + nbytes, sz - nbytes)
                if tag in self.allocs:
                    raise RuntimeError(f"dup alloc {tag}")
                self.allocs[tag] = r
                return r
        raise RuntimeError(f"arena overflow: {tag} needs {nbytes}, free={self.free_list}")

    def free(self, *tags):
        for t in tags:
            r = self.allocs.pop(t)
            self.free_list.append((r.off, r.nbytes))
        self.free_list.sort()
        out = []
        for off, sz in self.free_list:
            if out and out[-1][0] + out[-1][1] == off:
                out[-1] = (out[-1][0], out[-1][1] + sz)
            else:
                out.append((off, sz))
        self.free_list = [tuple(x) for x in out]

    def mark(self):
        return (list(self.free_list), dict(self.allocs))

    def reset(self, m):
        self.free_list, self.allocs = list(m[0]), dict(m[1])

    def used(self):
        return self.size - sum(sz for _, sz in self.free_list)


class Env:
    pass


DEBUG_OUTS = []


def mk_env(nc, names_dram):
    import contextlib
    env = Env()
    env.nc = nc
    env.stack = contextlib.ExitStack()
    env.dram = {}
    for nm, (shape, dt, kind) in names_dram.items():
        env.dram[nm] = nc.dram_tensor(nm, list(shape), dt, kind=kind).ap()
    env.XR_t = env.stack.enter_context(nc.sbuf_tensor("XR", [128, 8 * T], F32))
    env.XR = Reg(env.XR_t, "XR", 0, 8 * T * 4)
    ABYTES = 133 * 1024
    env.A_t = env.stack.enter_context(nc.sbuf_tensor("AR", [128, ABYTES // 2], BF16))
    env.arena = Arena(env.A_t, "AR", ABYTES)
    env.ST_t = env.stack.enter_context(nc.sbuf_tensor("STT", [128, 2048], F32))
    env.ST = Arena(env.ST_t, "STT", 8192)
    env.CN_t = env.stack.enter_context(nc.sbuf_tensor("CNT", [128, 1024], BF16))
    env.CN = Arena(env.CN_t, "CNT", 2048)
    env.ident = env.CN.alloc("ident", 256)
    env.ones = env.CN.alloc("ones", 1024)
    env.FC_t = env.stack.enter_context(nc.sbuf_tensor("FCT", [128, 64], F32))
    env.fvals = []
    env.PS = []
    for i in range(8):
        t = env.stack.enter_context(nc.psum_tensor(f"PS{i}", [128, 512], F32))
        env.PS.append(Reg(t, f"PS{i}", 0, 2048))
    return env


def load_consts(b, env):
    d = env.dram
    b.dma("sync", lambda e: e.dma_start(out=env.ident.ap(BF16, 128), in_=d["c_ident"][:, :]),
          rd=[("c_ident", 0, 1)], wr=[env.ident.key()], tag="ld_ident")
    b.dma("sync", lambda e: e.dma_start(out=env.ones.ap(BF16, 1), in_=d["c_ones_bf"][:, :]),
          rd=[("c_ones_bf", 0, 1)], wr=[(env.ones.name, env.ones.off, env.ones.off + 2)], tag="ld_ones_bf")
    b.dma("sync", lambda e: e.dma_start(out=env.ones.ap(F32, 128, coff=4, pn=1), in_=d["c_ones_row"][:, :]),
          rd=[("c_ones_row", 0, 1)], wr=[(env.ones.name, env.ones.off + 16, env.ones.off + 528)], tag="ld_ones_row")
    b.dma("sync", lambda e: e.dma_start(out=env.ones.ap(BF16, 128, coff=264, pn=1), in_=d["c_ones_row_bf"][:, :]),
          rd=[("c_ones_row_bf", 0, 1)], wr=[(env.ones.name, env.ones.off + 528, env.ones.off + 784)], tag="ld_ones_row_bf")


def ones_col_bf(env):
    return env.ones.ap(BF16, 1)


def ones_row_f32(env):
    return env.ones.ap(F32, 128, coff=4, pn=1)


def onesrow_key(env):
    return (env.ones.name, env.ones.off + 16, env.ones.off + 528)


def onescf_key(env):
    return (env.ones.name, env.ones.off + 4, env.ones.off + 8)


def ones_row_bf(env):
    return env.ones.ap(BF16, 128, coff=264, pn=1)


def onesrowbf_key(env):
    return (env.ones.name, env.ones.off + 528, env.ones.off + 784)


def ident_bf(env):
    return env.ident.ap(BF16, 128)


def fc(env, v, pn=128):
    v = float(v)
    if v in env.fvals:
        i = env.fvals.index(v)
    else:
        env.fvals.append(v)
        i = len(env.fvals) - 1
        if i >= 64:
            raise RuntimeError("fconst pool full")
    return env.FC_t.ap()[0:pn, i:i + 1]


def fck(env, v):
    v = float(v)
    if v not in env.fvals:
        env.fvals.append(v)
    i = env.fvals.index(v)
    return ("FCT", i * 4, i * 4 + 4)


def emit_ln(b, env, XN, tag="", src=None, srcdt=None, keep=None):
    nc = env.nc
    XR, PS, A = env.XR, env.PS, env.arena
    if src is not None:
        XR = src
    SDT = F32 if srcdt is None else srcdt
    onc = env.ones.ap(F32, 1, coff=1) if SDT == F32 else ones_col_bf(env)
    onck = onescf_key(env) if SDT == F32 else (env.ones.name, env.ones.off, env.ones.off + 2)
    onr = ones_row_f32(env)
    onrk = onesrow_key(env)
    m0 = A.mark()
    sq = A.alloc(tag + "sq", T * 4)
    rows = keep if keep is not None else A.alloc(tag + "rows", 2 * 2048 * 4)
    def row(i):
        return rows.ap(F32, T, coff=i * T, pn=1)
    def rowk(i):
        return (rows.name, rows.off + i * T * 4, rows.off + (i + 1) * T * 4)
    for f in range(8):
        fap = XR.ap(SDT, T, coff=f * T)
        fk = XR.key(SDT, T, coff=f * T)
        b.op("scalar", (lambda fap=fap: lambda e: e.activation(out=sq.ap(F32, T), in_=fap, func=AF.Square, bias=0.0, scale=1.0))(),
             rd=[fk], wr=[sq.key()], tag=tag + f"sq{f}")
        for ch in range(4):
            xap = XR.ap(SDT, 512, coff=f * T + ch * 512)
            sap = sq.ap(F32, 512, coff=ch * 512)
            b.op("tensor", (lambda xap=xap, ch=ch, f=f: lambda e: e.matmul(PS[ch].ap(F32, 512, pn=1), onc, xap, start=(f == 0), stop=(f == 7)))(),
                 rd=[onck, XR.key(SDT, 512, coff=f * T + ch * 512)], wr=[PS[ch].key()], tag=tag + f"summ{f}_{ch}")
            b.op("tensor", (lambda sap=sap, ch=ch, f=f: lambda e: e.matmul(PS[4 + ch].ap(F32, 512, pn=1), env.ones.ap(F32, 1, coff=1), sap, start=(f == 0), stop=(f == 7)))(),
                 rd=[onescf_key(env), sq.key(F32, 512, coff=ch * 512)], wr=[PS[4 + ch].key()], tag=tag + f"ssmm{f}_{ch}")
    for ch in range(4):
        b.op("scalar", (lambda ch=ch: lambda e: e.activation(out=row(0)[:, ch * 512:(ch + 1) * 512], in_=PS[ch].ap(F32, 512, pn=1), func=AF.Copy, bias=0.0, scale=1.0 / D))(),
             rd=[PS[ch].key()], wr=[rowk(0)], tag=tag + f"muev{ch}")
        b.op("scalar", (lambda ch=ch: lambda e: e.activation(out=row(1)[:, ch * 512:(ch + 1) * 512], in_=PS[4 + ch].ap(F32, 512, pn=1), func=AF.Copy, bias=0.0, scale=1.0 / D))(),
             rd=[PS[4 + ch].key()], wr=[rowk(1)], tag=tag + f"ssev{ch}")
    sqr = sq.ap(F32, T, pn=1)
    sqrk = (sq.name, sq.off, sq.off + T * 4)
    b.op("vector", lambda e: e.tensor_tensor(out=sqr, in0=row(0), in1=row(0), op=ALU.mult),
         rd=[rowk(0)], wr=[sqrk], tag=tag + "mu2")
    b.op("vector", lambda e: e.tensor_tensor(out=row(1), in0=row(1), in1=sqr, op=ALU.subtract),
         rd=[rowk(1), sqrk], wr=[rowk(1)], tag=tag + "var")
    b.op("scalar", lambda e: e.activation(out=row(1), in_=row(1), func=AF.Ln, bias=fc(env, 1e-5, 1), scale=1.0),
         rd=[rowk(1), fck(env, 1e-5)], wr=[rowk(1)], tag=tag + "lnv")
    b.op("scalar", lambda e: e.activation(out=row(1), in_=row(1), func=AF.Exp, bias=0.0, scale=fc(env, -0.5, 1)),
         rd=[rowk(1), fck(env, -0.5)], wr=[rowk(1)], tag=tag + "rstd")
    mubc = A.alloc(tag + "mubc", T * 4)
    rsbc = A.alloc(tag + "rsbc", T * 4)
    for ch in range(4):
        b.op("tensor", (lambda ch=ch: lambda e: e.matmul(PS[ch].ap(F32, 512), onr, row(0)[:, ch * 512:(ch + 1) * 512], start=True, stop=True))(),
             rd=[onrk, rowk(0)], wr=[PS[ch].key()], tag=tag + f"mubc{ch}")
        b.op("tensor", (lambda ch=ch: lambda e: e.matmul(PS[4 + ch].ap(F32, 512), onr, row(1)[:, ch * 512:(ch + 1) * 512], start=True, stop=True))(),
             rd=[onrk, rowk(1)], wr=[PS[4 + ch].key()], tag=tag + f"rsbc{ch}")
        b.op("scalar", (lambda ch=ch: lambda e: e.activation(out=mubc.ap(F32, 512, coff=ch * 512), in_=PS[ch].ap(F32, 512), func=AF.Copy, bias=0.0, scale=1.0))(),
             rd=[PS[ch].key()], wr=[mubc.key(F32, 512, coff=ch * 512)], tag=tag + f"mubcev{ch}")
        b.op("scalar", (lambda ch=ch: lambda e: e.activation(out=rsbc.ap(F32, 512, coff=ch * 512), in_=PS[4 + ch].ap(F32, 512), func=AF.Copy, bias=0.0, scale=1.0))(),
             rd=[PS[4 + ch].key()], wr=[rsbc.key(F32, 512, coff=ch * 512)], tag=tag + f"rsbcev{ch}")
    for f in range(8):
        b.op("vector", (lambda f=f: lambda e: e.tensor_tensor(out=sq.ap(F32, T), in0=XR.ap(SDT, T, coff=f * T), in1=mubc.ap(F32, T), op=ALU.subtract))(),
             rd=[XR.key(SDT, T, coff=f * T), mubc.key()], wr=[sq.key()], tag=tag + f"xs{f}")
        b.op("vector", (lambda f=f: lambda e: e.tensor_tensor(out=XN.ap(BF16, T, coff=f * T), in0=sq.ap(F32, T), in1=rsbc.ap(F32, T), op=ALU.mult))(),
             rd=[sq.key(), rsbc.key()], wr=[XN.key(BF16, T, coff=f * T)], tag=tag + f"xn{f}")
    A.reset(m0)


def g_gemm(b, env, *, name, KT, MT, Wsrc, rhs, evac, wslots, banks=(0, 1, 2, 3, 4, 5), mgroup=None):
    PS = env.PS
    nb = len(banks)
    if mgroup is None:
        mgroup = nb
    mg0 = 0
    di = 0
    while mg0 < MT:
        mgn = min(mgroup, MT - mg0)
        for kt in range(KT):
            sl = wslots[di % len(wslots)]
            di += 1
            src_ap, src_key = Wsrc(kt, mg0, mgn)
            nbytes = mgn * 128 * 2
            b.dma("gpsimd", (lambda src_ap=src_ap, sl=sl, nbytes=nbytes: lambda e: e.dma_start(out=sl.ap(BF16, nbytes // 2), in_=src_ap))(),
                  rd=[src_key], wr=[sl.key(BF16, nbytes // 2)], tag=f"{name}_w{kt}_{mg0}")
            rap, rkey = rhs(kt)
            nn = rap.shape[-1]
            for j in range(mgn):
                bk = banks[j]
                b.op("tensor", (lambda sl=sl, j=j, rap=rap, bk=bk, kt=kt, KT=KT, nn=nn: lambda e: e.matmul(PS[bk].ap(F32, nn), sl.ap(BF16, 128, coff=j * 128), rap, start=(kt == 0), stop=(kt == KT - 1)))(),
                     rd=[sl.key(BF16, 128, coff=j * 128), rkey], wr=[PS[bk].key()], tag=f"{name}_mm{kt}_{mg0 + j}")
        for j in range(mgn):
            evac(mg0 + j, PS[banks[j]])
        mg0 += mgn


def bcast_scalar(b, env, dram_name, reg, coff=0):
    src = env.dram[dram_name].to_broadcast([128, 1])
    b.dma("sync", lambda e: e.dma_start(out=reg.ap(F32, 1, coff=coff), in_=src),
          rd=[(dram_name, 0, 1)], wr=[reg.key(F32, 1, coff=coff)], tag=f"bs_{dram_name}")


def emit_load_xr(b, env):
    src = env.dram["xin"].rearrange("(f p) t -> p f t", p=128)
    dst = env.XR.ap(F32, 8 * T).rearrange("p (f t) -> p f t", f=8)
    b.dma("sync", lambda e: e.dma_start(out=dst, in_=src),
          rd=[("xin", 0, 1)], wr=[env.XR.key()], tag="ld_x")


def emit_store_xr(b, env):
    dst = env.dram["xout"].rearrange("(f p) t -> p f t", p=128)
    src = env.XR.ap(F32, 8 * T).rearrange("p (f t) -> p f t", f=8)
    b.dma("sync", lambda e: e.dma_start(out=dst, in_=src),
          rd=[env.XR.key()], wr=[("xout", 0, 1)], tag="st_x")


def dram_l0f():
    return {
        "l0f_k0": ((2048, 4096), U8, "ExternalInput"),
        "l0f_k1": ((4096, 4096), U8, "ExternalInput"),
        "l0f_k2": ((4096, 1024), U8, "ExternalInput"),
        "l0f_s0": ((1, 1), F32, "ExternalInput"),
        "l0f_s1": ((1, 1), F32, "ExternalInput"),
        "l0f_s2": ((1, 1), F32, "ExternalInput"),
        "l0f_w0eff": ((1024, 4096), BF16, "Internal"),
        "l0f_w1eff": ((4096, 4096), BF16, "Internal"),
        "l0f_w2eff": ((4096, 1024), BF16, "Internal"),
    }


def host_l0f(inp):
    return {
        "l0f_k0": np.ascontiguousarray(inp["L0_ffn_k0"]).view(np.uint8),
        "l0f_k1": np.ascontiguousarray(inp["L0_ffn_k1"]).view(np.uint8),
        "l0f_k2": np.ascontiguousarray(inp["L0_ffn_k2"]).view(np.uint8),
        "l0f_s0": np.asarray(inp["L0_ffn_s0"], np.float32).reshape(1, 1),
        "l0f_s1": np.asarray(inp["L0_ffn_s1"], np.float32).reshape(1, 1),
        "l0f_s2": np.asarray(inp["L0_ffn_s2"], np.float32).reshape(1, 1),
    }


def emit_bool_convert(b, env, srcname, dstname, K, M, scale, bias, pair=None):
    A = env.arena
    m0 = A.mark()
    ca = A.alloc("cva", M)
    cf = A.alloc("cvf", M * 2)
    if pair is not None:
        cb = A.alloc("cvb", M)
        cg = A.alloc("cvg", M * 2)
    d = env.dram
    for kt in range(K // 128):
        b.dma("sync", (lambda kt=kt: lambda e: e.dma_start(out=ca.ap(U8, M), in_=d[srcname][kt * 128:(kt + 1) * 128, :]))(),
              rd=[(srcname, 0, 1)], wr=[ca.key()], tag=f"cv_{srcname}{kt}a")
        if pair is None:
            b.op("scalar", (lambda: lambda e: e.activation(out=cf.ap(BF16, M), in_=ca.ap(U8, M), func=AF.Copy, bias=bias, scale=scale))(),
                 rd=[ca.key()], wr=[cf.key()], tag=f"cv_{srcname}{kt}f")
        else:
            b.dma("sync", (lambda kt=kt: lambda e: e.dma_start(out=cb.ap(U8, M), in_=d[srcname][pair + kt * 128: pair + (kt + 1) * 128, :]))(),
                  rd=[(srcname, 0, 1)], wr=[cb.key()], tag=f"cv_{srcname}{kt}b")
            b.op("scalar", (lambda: lambda e: e.activation(out=cf.ap(BF16, M), in_=ca.ap(U8, M), func=AF.Copy, bias=0.0, scale=scale))(),
                 rd=[ca.key()], wr=[cf.key()], tag=f"cv_{srcname}{kt}f")
            b.op("scalar", (lambda: lambda e: e.activation(out=cg.ap(BF16, M), in_=cb.ap(U8, M), func=AF.Copy, bias=0.0, scale=scale))(),
                 rd=[cb.key()], wr=[cg.key()], tag=f"cv_{srcname}{kt}g")
            b.op("vector", (lambda: lambda e: e.tensor_tensor(out=cf.ap(BF16, M), in0=cf.ap(BF16, M), in1=cg.ap(BF16, M), op=ALU.subtract))(),
                 rd=[cf.key(), cg.key()], wr=[cf.key()], tag=f"cv_{srcname}{kt}s")
        b.dma("sync", (lambda kt=kt: lambda e: e.dma_start(out=d[dstname][kt * 128:(kt + 1) * 128, :], in_=cf.ap(BF16, M)))(),
              rd=[cf.key()], wr=[(dstname, 0, 1)], tag=f"cv_{srcname}{kt}o")
    A.reset(m0)


def emit_l0f(b, env):
    A, d = env.arena, env.dram
    m0 = A.mark()
    sc = env.ST.alloc("l0f_sc", 512)
    bcast_scalar(b, env, "l0f_s0", sc, 0)
    bcast_scalar(b, env, "l0f_s1", sc, 1)
    bcast_scalar(b, env, "l0f_s2", sc, 2)
    s0 = sc.ap(F32, 1, coff=0); s1 = sc.ap(F32, 1, coff=1); s2 = sc.ap(F32, 1, coff=2)
    emit_bool_convert(b, env, "l0f_k0", "l0f_w0eff", 1024, 4096, 2.0, 0.0, pair=1024)
    emit_bool_convert(b, env, "l0f_k1", "l0f_w1eff", 4096, 4096, 4.0, -2.0)
    emit_bool_convert(b, env, "l0f_k2", "l0f_w2eff", 4096, 1024, 4.0, -2.0)
    XN = A.alloc("l0f_xn", 16384 * 2)
    emit_ln(b, env, XN, tag="l0f_ln")
    h1 = A.alloc("l0f_h1", 32 * 512 * 2)
    h2 = A.alloc("l0f_h2", 32 * 512 * 2)
    ws = [A.alloc(f"l0f_ws{i}", 2048) for i in range(2)]

    def wsrc(dname, M):
        def f(kt, mg0, mgn):
            ap = d[dname][kt * 128:(kt + 1) * 128, mg0 * 128:(mg0 + mgn) * 128]
            return ap, (dname, 0, 1)
        return f

    for tc in range(4):
        def rhs_xn(kt, tc=tc):
            return XN.ap(BF16, 512, coff=kt * T + tc * 512), XN.key(BF16, 512, coff=kt * T + tc * 512)

        def ev_h1(mt, ps, tc=tc):
            b.op("scalar", (lambda mt=mt, ps=ps: lambda e: e.activation(out=h1.ap(BF16, 512, coff=mt * 512), in_=ps.ap(F32, 512), func=AF.Relu, bias=0.0, scale=s0))(),
                 rd=[ps.key(), sc.key(F32, 1, coff=0)], wr=[h1.key(BF16, 512, coff=mt * 512)], tag=f"l0f_h1ev{tc}_{mt}")

        g_gemm(b, env, name=f"l0f_g0_{tc}", KT=8, MT=32, Wsrc=wsrc("l0f_w0eff", 4096), rhs=rhs_xn, evac=ev_h1, wslots=ws)

        def rhs_h1(kt):
            return h1.ap(BF16, 512, coff=kt * 512), h1.key(BF16, 512, coff=kt * 512)

        def ev_h2(mt, ps, tc=tc):
            b.op("scalar", (lambda mt=mt, ps=ps: lambda e: e.activation(out=h2.ap(BF16, 512, coff=mt * 512), in_=ps.ap(F32, 512), func=AF.Relu, bias=0.0, scale=s1))(),
                 rd=[ps.key(), sc.key(F32, 1, coff=1)], wr=[h2.key(BF16, 512, coff=mt * 512)], tag=f"l0f_h2ev{tc}_{mt}")

        g_gemm(b, env, name=f"l0f_g1_{tc}", KT=32, MT=32, Wsrc=wsrc("l0f_w1eff", 4096), rhs=rhs_h1, evac=ev_h2, wslots=ws)

        def rhs_h2(kt):
            return h2.ap(BF16, 512, coff=kt * 512), h2.key(BF16, 512, coff=kt * 512)

        def ev_out(mt, ps, tc=tc):
            xap = env.XR.ap(F32, 512, coff=mt * T + tc * 512)
            xk = env.XR.key(F32, 512, coff=mt * T + tc * 512)
            b.op("vector", (lambda mt=mt, ps=ps, xap=xap: lambda e: e.scalar_tensor_tensor(out=xap, in0=ps.ap(F32, 512), scalar=s2, in1=xap, op0=ALU.mult, op1=ALU.add))(),
                 rd=[ps.key(), xk, sc.key(F32, 1, coff=2)], wr=[xk], tag=f"l0f_out{tc}_{mt}")

        g_gemm(b, env, name=f"l0f_g2_{tc}", KT=32, MT=8, Wsrc=wsrc("l0f_w2eff", 1024), rhs=rhs_h2, evac=ev_out, wslots=ws)
    A.reset(m0)


BLOCKS = {}
BLOCKS["l0f"] = (emit_l0f, dram_l0f, host_l0f)


def base_dram():
    return {
        "xin": ((1024, 2048), F32, "ExternalInput"),
        "xout": ((1024, 2048), F32, "ExternalOutput"),
        "c_ident": ((128, 128), BF16, "ExternalInput"),
        "c_ones_bf": ((128, 1), BF16, "ExternalInput"),
        "c_ones_row": ((1, 128), F32, "ExternalInput"),
        "c_ones_f32c": ((128, 1), F32, "ExternalInput"),
        "c_ones_row_bf": ((1, 128), BF16, "ExternalInput"),
        "c_fvals": ((1, 64), F32, "ExternalInput"),
        "c_ident_f32": ((128, 128), F32, "ExternalInput"),
    }


def host_base():
    return {
        "c_ident": np.eye(128).astype(BF),
        "c_ones_bf": np.ones((128, 1), BF),
        "c_ones_row": np.ones((1, 128), np.float32),
        "c_ones_row_bf": np.ones((1, 128), BF),
        "c_ident_f32": np.eye(128, dtype=np.float32),
        "c_ones_f32c": np.ones((128, 1), np.float32),
    }


def build_program(block_names, race=False, dbg=()):
    nc = bass.Bass("TRN2", target_bir_lowering=False, detect_race_conditions=race)
    spec = base_dram()
    for bn in block_names:
        spec.update(BLOCKS[bn][1]())
    for dn, shape, dt in dbg:
        spec[dn] = (shape, dt, "ExternalOutput")
    env = mk_env(nc, spec)
    b = KB()
    load_consts(b, env)
    b.dma("sync", lambda e: e.dma_start(out=env.ones.ap(F32, 1, coff=1), in_=env.dram["c_ones_f32c"][:, :]),
          rd=[("c_ones_f32c", 0, 1)], wr=[onescf_key(env)], tag="ld_ones_f32c")
    b.dma("sync", lambda e: e.dma_start(out=env.FC_t.ap()[:, :], in_=env.dram["c_fvals"].to_broadcast([128, 64])),
          rd=[("c_fvals", 0, 1)], wr=[("FCT", 0, 256)], tag="ld_fvals")
    emit_load_xr(b, env)
    for bn in block_names:
        BLOCKS[bn][0](b, env)
    emit_store_xr(b, env)
    env.kb = b
    b.emit(nc)
    env.stack.close()
    return nc, env


def host_inputs_for(block_names, inp):
    shared = host_base()
    for bn in block_names:
        shared.update(BLOCKS[bn][2](inp))
    return shared


def run_blocks(block_names, inp, x_cur):
    nc, env = build_program(block_names)
    shared = host_inputs_for(block_names, inp)
    fv = np.zeros((1, 64), np.float32)
    fv[0, :len(env.fvals)] = env.fvals
    shared["c_fvals"] = fv
    in_maps = []
    for c in range(NCORES):
        m = dict(shared)
        m["xin"] = np.ascontiguousarray(x_cur[c].T)
        in_maps.append(m)
    res = run_bass_kernel_spmd(nc, in_maps, core_ids=list(range(NCORES)))
    out = np.stack([np.ascontiguousarray(r["xout"].T) for r in res.results], 0)
    return out


CHAIN = ["l0a", "l0f", "l1a", "l1f", "l2a", "l2f", "l3a", "l3f"]


def kernel(**inputs):
    x = np.ascontiguousarray(inputs["L0_attn_x"], np.float32)
    for bn in CHAIN:
        x = run_blocks([bn], inputs, x)
    return x


def dram_l0a():
    return {
        "l0a_wq": ((1024, 1024), F32, "ExternalInput"),
        "l0a_wk": ((1024, 1024), F32, "ExternalInput"),
        "l0a_wv": ((1024, 1024), F32, "ExternalInput"),
        "l0a_wo": ((1024, 1024), F32, "ExternalInput"),
        "l0a_lam": ((1, 128), F32, "ExternalInput"),
        "l0a_rmsw": ((128, 1), F32, "ExternalInput"),
        "l0a_cos": ((128, 256), F32, "ExternalInput"),
        "l0a_sin": ((128, 256), F32, "ExternalInput"),
        "l0a_spill": ((2, 128, 16384), BF16, "Internal"),
    }


def host_l0a(inp):
    NH, HD = 16, 32
    perm = np.zeros(1024, np.int64)
    for g in range(2):
        for eo in range(2):
            for h in range(16):
                for m in range(16):
                    perm[g * 512 + h * 32 + eo * 16 + m] = (h * 2 + g) * 32 + 2 * m + eo
    ang = (1.0 / 10000.0 ** np.linspace(0.0, 1.0, 16)).astype(np.float32)
    t = np.arange(T, dtype=np.float32)
    f = t[:, None] * ang[None, :]
    cos = np.cos(f).astype(np.float32)
    sin = np.sin(f).astype(np.float32)
    cosT = cos.reshape(16, 128, 16).transpose(1, 0, 2).reshape(128, 256)
    sinT = sin.reshape(16, 128, 16).transpose(1, 0, 2).reshape(128, 256)
    return {
        "l0a_wq": np.ascontiguousarray(inp["L0_attn_Wq"].T[:, perm]),
        "l0a_wk": np.ascontiguousarray(inp["L0_attn_Wk"].T[:, perm]),
        "l0a_wv": np.ascontiguousarray(inp["L0_attn_Wv"].T),
        "l0a_wo": np.ascontiguousarray(inp["L0_attn_Wo"].T),
        "l0a_lam": np.concatenate([inp["L0_attn_lambda_q1"], inp["L0_attn_lambda_k1"],
                                   inp["L0_attn_lambda_q2"], inp["L0_attn_lambda_k2"]]).reshape(1, 128).astype(np.float32),
        "l0a_rmsw": np.tile(np.asarray(inp["L0_attn_rms_weight"], np.float32), 2).reshape(128, 1),
        "l0a_cos": np.ascontiguousarray(cosT),
        "l0a_sin": np.ascontiguousarray(sinT),
    }


def emit_wload(b, env, slot, dname, K, M):
    d = env.dram
    for kt in range(K // 128):
        b.dma("gpsimd", (lambda kt=kt: lambda e: e.dma_start(out=slot.ap(BF16, M, coff=kt * M), in_=d[dname][kt * 128:(kt + 1) * 128, :]))(),
              rd=[(dname, 0, 1)], wr=[slot.key(BF16, M, coff=kt * M)], tag=f"wl_{dname}{kt}")


def emit_tok_gemm(b, env, *, name, XN, slot, dst, dst_coff_fn, KT, ocols, oc0=0, banks=(0, 1, 2, 3), evac=None):
    PS = env.PS
    bi = 0
    for tt in range(16):
        for nc0 in range(0, ocols, 512):
            nn = min(512, ocols - nc0)
            bk = banks[bi % len(banks)]
            bi += 1
            for kt in range(KT):
                lhs = XN.ap(BF16, 128, coff=kt * T + tt * 128)
                lk = XN.key(BF16, 128, coff=kt * T + tt * 128)
                rhs = slot.ap(BF16, nn, coff=kt * 1024 + oc0 + nc0)
                rk = slot.key(BF16, nn, coff=kt * 1024 + oc0 + nc0)
                b.op("tensor", (lambda lhs=lhs, rhs=rhs, bk=bk, kt=kt, KT=KT, nn=nn: lambda e: e.matmul(PS[bk].ap(F32, nn), lhs, rhs, start=(kt == 0), stop=(kt == KT - 1)))(),
                     rd=[lk, rk], wr=[PS[bk].key()], tag=f"{name}_mm{tt}_{nc0}_{kt}")
            if evac is None:
                dap, dk = dst_coff_fn(tt, nc0, nn)
                b.op("scalar", (lambda dap=dap, bk=bk, nn=nn: lambda e: e.activation(out=dap, in_=PS[bk].ap(F32, nn), func=AF.Copy, bias=0.0, scale=1.0))(),
                     rd=[PS[bk].key()], wr=[dk], tag=f"{name}_ev{tt}_{nc0}")
            else:
                evac(tt, nc0, nn, PS[bk])


def emit_rope_transpose(b, env, *, name, tokreg, toff, g, ctab, dstfm, tt):
    A = env.arena
    sc = env.ropesc
    cosap = ctab.ap(F32, 16, coff=tt * 16).unsqueeze(1).to_broadcast([128, 16, 16])
    sinap = ctab.ap(F32, 16, coff=256 + tt * 16).unsqueeze(1).to_broadcast([128, 16, 16])
    ck = ctab.key()
    base4 = tokreg.ap(BF16, 512, coff=toff + g * 512).rearrange("p (h eo m) -> p h eo m", h=16, eo=2)
    re = base4[:, :, 0, :]
    im = base4[:, :, 1, :]
    rek = tokreg.key(BF16, 512, coff=toff + g * 512)
    imk = rek
    t1 = sc.ap(BF16, 256, coff=0).rearrange("p (h m) -> p h m", h=16)
    t2 = sc.ap(BF16, 256, coff=256).rearrange("p (h m) -> p h m", h=16)
    t3 = sc.ap(BF16, 256, coff=512).rearrange("p (h m) -> p h m", h=16)
    k1 = sc.key(BF16, 256, coff=0); k2 = sc.key(BF16, 256, coff=256); k3 = sc.key(BF16, 256, coff=512)
    b.op("vector", lambda e: e.tensor_tensor(out=t1, in0=re, in1=cosap, op=ALU.mult), rd=[rek, ck], wr=[k1], tag=name + "r1")
    b.op("vector", lambda e: e.tensor_tensor(out=t2, in0=re, in1=sinap, op=ALU.mult), rd=[rek, ck], wr=[k2], tag=name + "r2")
    b.op("vector", lambda e: e.tensor_tensor(out=t3, in0=im, in1=sinap, op=ALU.mult), rd=[imk, ck], wr=[k3], tag=name + "r3")
    b.op("vector", lambda e: e.tensor_tensor(out=re, in0=t1, in1=t3, op=ALU.subtract), rd=[k1, k3], wr=[rek], tag=name + "r4")
    b.op("vector", lambda e: e.tensor_tensor(out=t3, in0=im, in1=cosap, op=ALU.mult), rd=[imk, ck], wr=[k3], tag=name + "r5")
    b.op("vector", lambda e: e.tensor_tensor(out=im, in0=t2, in1=t3, op=ALU.add), rd=[k2, k3], wr=[imk], tag=name + "r6")
    for hb in range(4):
        inap = tokreg.ap(BF16, 128, coff=toff + g * 512 + hb * 128)
        pt = env.PS[6 + hb % 2]
        ptap = pt.ap(BF16, 128)
        b.op("tensor", (lambda inap=inap, ptap=ptap: lambda e: e.transpose(ptap, inap, ident_bf(env)))(),
             rd=[rek, imk, env.ident.key()], wr=[pt.key()], tag=f"{name}_tr{hb}")
        dap = dstfm.ap(BF16, 128, coff=hb * 2048 + tt * 128)
        dk = dstfm.key(BF16, 128, coff=hb * 2048 + tt * 128)
        b.op("scalar", (lambda dap=dap, ptap=ptap: lambda e: e.activation(out=dap, in_=ptap, func=AF.Copy, bias=0.0, scale=1.0))(),
             rd=[pt.key()], wr=[dk], tag=f"{name}_trev{hb}")


def emit_l0a(b, env):
    A, d, PS = env.arena, env.dram, env.PS
    m0 = A.mark()
    SCALE = 32.0 ** -0.5
    LAMBDA_INIT = 0.8 - 0.6 * math.exp(-0.3 * 12)
    sm = env.ST.alloc("l0a_sm", 1024)
    b.dma("sync", lambda e: e.dma_start(out=sm.ap(F32, 128, pn=1), in_=d["l0a_lam"][:, :]),
          rd=[("l0a_lam", 0, 1)], wr=[sm.key(F32, 128)], tag="l0a_ldlam")
    b.dma("sync", lambda e: e.dma_start(out=sm.ap(F32, 1, coff=192), in_=d["l0a_rmsw"][:, :]),
          rd=[("l0a_rmsw", 0, 1)], wr=[sm.key(F32, 1, coff=192)], tag="l0a_ldrmsw")
    b.op("vector", lambda e: e.tensor_tensor(out=sm.ap(F32, 32, coff=128, pn=1), in0=sm.ap(F32, 32, coff=0, pn=1), in1=sm.ap(F32, 32, coff=32, pn=1), op=ALU.mult),
         rd=[sm.key(F32, 128)], wr=[sm.key(F32, 32, coff=128)], tag="l0a_l1")
    b.op("vector", lambda e: e.tensor_tensor(out=sm.ap(F32, 32, coff=160, pn=1), in0=sm.ap(F32, 32, coff=64, pn=1), in1=sm.ap(F32, 32, coff=96, pn=1), op=ALU.mult),
         rd=[sm.key(F32, 128)], wr=[sm.key(F32, 32, coff=160)], tag="l0a_l2")
    b.op("vector", lambda e: e.tensor_reduce(out=sm.ap(F32, 1, coff=224, pn=1), in_=sm.ap(F32, 32, coff=128, pn=1), axis=AX.X, op=ALU.add),
         rd=[sm.key(F32, 32, coff=128)], wr=[sm.key(F32, 1, coff=224)], tag="l0a_s1")
    b.op("vector", lambda e: e.tensor_reduce(out=sm.ap(F32, 1, coff=225, pn=1), in_=sm.ap(F32, 32, coff=160, pn=1), axis=AX.X, op=ALU.add),
         rd=[sm.key(F32, 32, coff=160)], wr=[sm.key(F32, 1, coff=225)], tag="l0a_s2")
    b.op("scalar", lambda e: e.activation(out=sm.ap(F32, 2, coff=226, pn=1), in_=sm.ap(F32, 2, coff=224, pn=1), func=AF.Exp, bias=0.0, scale=1.0),
         rd=[sm.key(F32, 2, coff=224)], wr=[sm.key(F32, 2, coff=226)], tag="l0a_e12")
    b.op("vector", lambda e: e.tensor_tensor(out=sm.ap(F32, 1, coff=228, pn=1), in0=sm.ap(F32, 1, coff=226, pn=1), in1=sm.ap(F32, 1, coff=227, pn=1), op=ALU.subtract),
         rd=[sm.key(F32, 2, coff=226)], wr=[sm.key(F32, 1, coff=228)], tag="l0a_lsub")
    b.op("vector", lambda e: e.tensor_scalar(out=sm.ap(F32, 1, coff=228, pn=1), in0=sm.ap(F32, 1, coff=228, pn=1), scalar1=LAMBDA_INIT, scalar2=None, op0=ALU.add),
         rd=[sm.key(F32, 1, coff=228)], wr=[sm.key(F32, 1, coff=228)], tag="l0a_ladd")
    b.op("tensor", lambda e: e.matmul(PS[5].ap(F32, 1), ones_row_f32(env), sm.ap(F32, 1, coff=228, pn=1), start=True, stop=True),
         rd=[onesrow_key(env), sm.key(F32, 1, coff=228)], wr=[PS[5].key()], tag="l0a_lbc")
    b.op("scalar", lambda e: e.activation(out=sm.ap(F32, 1, coff=229), in_=PS[5].ap(F32, 1), func=AF.Copy, bias=0.0, scale=1.0),
         rd=[PS[5].key()], wr=[sm.key(F32, 1, coff=229)], tag="l0a_lbce")
    lamcol64 = sm.ap(F32, 1, coff=229, pn=64)
    lamk = sm.key(F32, 1, coff=229)
    b.op("vector", lambda e: e.tensor_scalar(out=sm.ap(F32, 1, coff=192), in0=sm.ap(F32, 1, coff=192), scalar1=1.0 - LAMBDA_INIT, scalar2=None, op0=ALU.mult),
         rd=[sm.key(F32, 1, coff=192)], wr=[sm.key(F32, 1, coff=192)], tag="l0a_rmw")
    rmswcol = sm.ap(F32, 1, coff=192)
    rmswk = sm.key(F32, 1, coff=192)

    XN = A.alloc("l0a_xn", 16384 * 2)
    emit_ln(b, env, XN, tag="l0a_ln")
    ctab = A.alloc("l0a_ctab", 2048)
    b.dma("sync", lambda e: e.dma_start(out=ctab.ap(F32, 256), in_=d["l0a_cos"][:, :]),
          rd=[("l0a_cos", 0, 1)], wr=[(ctab.name, ctab.off, ctab.off + 1024)], tag="l0a_ldc")
    b.dma("sync", lambda e: e.dma_start(out=ctab.ap(F32, 256, coff=256), in_=d["l0a_sin"][:, :]),
          rd=[("l0a_sin", 0, 1)], wr=[(ctab.name, ctab.off + 1024, ctab.off + 2048)], tag="l0a_lds")
    env.ropesc = A.alloc("l0a_ropesc", 2048)

    slot = A.alloc("l0a_wslot", 16384)
    vtok = A.alloc("l0a_vtok", 16 * 1040 * 2)
    for tt in range(16):
        ap = vtok.ap(BF16, 1040, coff=tt * 1040).rearrange("p (h c) -> p h c", h=16)[:, :, 64:65]
        b.op("vector", (lambda ap=ap, tt=tt: lambda e: e.memset(ap, 1.0))(),
             rd=[], wr=[vtok.key(BF16, 1040, coff=tt * 1040)], tag=f"l0a_vones{tt}")
    emit_wload(b, env, slot, "l0a_wv", 1024, 1024)

    def vdst(tt, nc0, nn):
        h0 = nc0 // 64
        ap = vtok.ap(BF16, 1040, coff=tt * 1040).rearrange("p (h c) -> p h c", h=16)[:, h0:h0 + 8, 0:64]
        return ap, vtok.key(BF16, 1040, coff=tt * 1040)

    emit_tok_gemm(b, env, name="l0a_v", XN=XN, slot=slot, dst=None, dst_coff_fn=vdst, KT=8, ocols=1024)

    qfm = [None, None]
    kfm = [None, None]
    O0 = None

    def attn_group(g, qf, kf):
        nonlocal O0
        ptsl = env.ptsl
        for h in range(16):
            hb, hr = h // 4, (h % 4) * 32
            for qc in range(4):
                ob = 2
                for kt in range(16):
                    sb = kt % 2
                    lhs = kf.ap(BF16, 128, coff=hb * 2048 + kt * 128, p0=hr, pn=32)
                    lk = kf.key(BF16, 128, coff=hb * 2048 + kt * 128)
                    rhs = qf.ap(BF16, 512, coff=hb * 2048 + qc * 512, p0=hr, pn=32)
                    rk = qf.key(BF16, 512, coff=hb * 2048 + qc * 512)
                    b.op("tensor", (lambda lhs=lhs, rhs=rhs, sb=sb, hr=hr: lambda e: e.matmul(PS[sb].ap(F32, 512), lhs, rhs, start=True, stop=True, tile_position=(hr, 0)))(),
                         rd=[lk, rk], wr=[PS[sb].key()], tag=f"l0a_st{g}_{h}_{qc}_{kt}")
                    pt = ptsl.ap(BF16, 512, coff=(kt % 2) * 512)
                    ptk = ptsl.key(BF16, 512, coff=(kt % 2) * 512)
                    b.op("scalar", (lambda pt=pt, sb=sb: lambda e: e.activation(out=pt, in_=PS[sb].ap(F32, 512), func=AF.Exp, bias=0.0, scale=fc(env, SCALE)))(),
                         rd=[PS[sb].key(), fck(env, SCALE)], wr=[ptk], tag=f"l0a_exp{g}_{h}_{qc}_{kt}")
                    vl = vtok.ap(BF16, 65, coff=kt * 1040 + h * 65)
                    vk = vtok.key(BF16, 65, coff=kt * 1040 + h * 65)
                    b.op("tensor", (lambda vl=vl, pt=pt, kt=kt: lambda e: e.matmul(PS[ob].ap(F32, 512, pn=65), vl, pt, start=(kt == 0), stop=(kt == 15)))(),
                         rd=[vk, ptk], wr=[PS[ob].key()], tag=f"l0a_pv{g}_{h}_{qc}_{kt}")
                oft, orow = h // 2, (h % 2) * 64
                zr = env.zrow.ap(F32, 512, pn=1)
                zk = env.zrow.key(F32, 512)
                b.op("vector", (lambda zr=zr: lambda e: e.reciprocal(out=zr, in_=PS[ob].ap(F32, 512, p0=64, pn=1)))(),
                     rd=[PS[ob].key()], wr=[zk], tag=f"l0a_z{g}_{h}_{qc}")
                b.op("tensor", (lambda zr=zr, orow=orow: lambda e: e.matmul(PS[3].ap(F32, 512, p0=orow, pn=64), ones_row_f32(env)[:, 0:64], zr, start=True, stop=True))(),
                     rd=[zk, onesrow_key(env)], wr=[PS[3].key()], tag=f"l0a_zbc{g}_{h}_{qc}")
                bc = env.bcsc.ap(F32, 512, p0=orow, pn=64)
                bck = env.bcsc.key(F32, 512)
                if g == 0:
                    b.op("scalar", (lambda bc=bc, orow=orow: lambda e: e.activation(out=bc, in_=PS[3].ap(F32, 512, p0=orow, pn=64), func=AF.Copy, bias=0.0, scale=1.0))(),
                         rd=[PS[3].key()], wr=[bck], tag=f"l0a_bce{g}_{h}_{qc}")
                else:
                    b.op("scalar", (lambda bc=bc, orow=orow: lambda e: e.activation(out=bc, in_=PS[3].ap(F32, 512, p0=orow, pn=64), func=AF.Identity, bias=0.0, scale=env.ST.allocs["l0a_sm"].ap(F32, 1, coff=229, p0=orow, pn=64)))(),
                         rd=[PS[3].key(), lamk], wr=[bck], tag=f"l0a_bce{g}_{h}_{qc}")
                oap = O0.ap(BF16, 512, coff=oft * 2048 + qc * 512, p0=orow, pn=64)
                ok = O0.key(BF16, 512, coff=oft * 2048 + qc * 512)
                tc = env.tcsc.ap(BF16, 512, p0=orow, pn=64)
                tck = env.tcsc.key(BF16, 512)
                b.op("scalar", (lambda tc=tc: lambda e: e.activation(out=tc, in_=PS[ob].ap(F32, 512, pn=64), func=AF.Copy, bias=0.0, scale=1.0))(),
                     rd=[PS[ob].key()], wr=[tck], tag=f"l0a_oev{g}_{h}_{qc}")
                if g == 0:
                    b.op("vector", (lambda oap=oap, bc=bc, tc=tc: lambda e: e.tensor_tensor(out=oap, in0=tc, in1=bc, op=ALU.mult))(),
                         rd=[tck, bck], wr=[ok], tag=f"l0a_nrm{h}_{qc}")
                else:
                    b.op("vector", (lambda tc=tc, bc=bc: lambda e: e.tensor_tensor(out=tc, in0=tc, in1=bc, op=ALU.mult))(),
                         rd=[tck, bck], wr=[tck], tag=f"l0a_t2{h}_{qc}")
                    b.op("vector", (lambda oap=oap, tc=tc: lambda e: e.tensor_tensor(out=oap, in0=oap, in1=tc, op=ALU.subtract))(),
                         rd=[ok, tck], wr=[ok], tag=f"l0a_cmb{h}_{qc}")

    toks = A.alloc("l0a_tok", 2048 * 2 * 2)
    env.ptsl = A.alloc("l0a_pt", 2048)
    env.zrow = A.alloc("l0a_zrow", 2048)
    env.bcsc = A.alloc("l0a_bc", 2048)
    env.tcsc = A.alloc("l0a_tc", 1024)

    def qk_build(wname, g, dst, slot, toks):
        emit_wload(b, env, slot, wname, 1024, 1024)
        for tt in range(16):
            toff = (tt % 2) * 1024
            bk = 0 if tt % 2 == 0 else 1
            for kt in range(8):
                lhs = XN.ap(BF16, 128, coff=kt * T + tt * 128)
                lk = XN.key(BF16, 128, coff=kt * T + tt * 128)
                rhs = slot.ap(BF16, 512, coff=kt * 1024 + g * 512)
                rk = slot.key(BF16, 512, coff=kt * 1024 + g * 512)
                b.op("tensor", (lambda lhs=lhs, rhs=rhs, bk=bk, kt=kt: lambda e: e.matmul(PS[bk].ap(F32, 512), lhs, rhs, start=(kt == 0), stop=(kt == 7)))(),
                     rd=[lk, rk], wr=[PS[bk].key()], tag=f"{wname}_mm{tt}_{kt}")
            b.op("scalar", (lambda toff=toff, bk=bk, g=g: lambda e: e.activation(out=toks.ap(BF16, 512, coff=toff + g * 512), in_=PS[bk].ap(F32, 512), func=AF.Copy, bias=0.0, scale=1.0))(),
                 rd=[PS[bk].key()], wr=[toks.key(BF16, 512, coff=toff + g * 512)], tag=f"{wname}_ev{tt}")
            emit_rope_transpose(b, env, name=f"{wname}_rt{tt}", tokreg=toks, toff=toff, g=g, ctab=ctab, dstfm=dst, tt=tt)

    def spill(i, reg, out=True):
        if out:
            b.dma("sync", (lambda reg=reg, i=i: lambda e: e.dma_start(out=d["l0a_spill"][i], in_=reg.ap(BF16, 16384)))(),
                  rd=[reg.key()], wr=[("l0a_spill", i, i + 1)], tag=f"l0a_spo{i}")
        else:
            b.dma("sync", (lambda reg=reg, i=i: lambda e: e.dma_start(out=reg.ap(BF16, 16384), in_=d["l0a_spill"][i]))(),
                  rd=[("l0a_spill", i, i + 1)], wr=[reg.key()], tag=f"l0a_spi{i}")

    qfm0 = A.alloc("l0a_q0", 4 * 2048 * 2)
    kfm0 = A.alloc("l0a_k0", 4 * 2048 * 2)
    qk_build("l0a_wq", 0, qfm0, slot, toks)
    qk_build("l0a_wk", 0, kfm0, slot, toks)
    spill(0, XN, True)
    A.free("l0a_xn", "l0a_wslot", "l0a_tok", "l0a_ropesc")
    O0 = A.alloc("l0a_O0", 16384 * 2)
    attn_group(0, qfm0, kfm0)
    spill(1, O0, True)
    A.free("l0a_O0", "l0a_q0", "l0a_k0")
    XN = A.alloc("l0a_xn", 16384 * 2)
    spill(0, XN, False)
    slot = A.alloc("l0a_wslot", 16384)
    toks = A.alloc("l0a_tok", 2048 * 2 * 2)
    env.ropesc = A.alloc("l0a_ropesc", 2048)
    qfm1 = A.alloc("l0a_q1", 4 * 2048 * 2)
    kfm1 = A.alloc("l0a_k1", 4 * 2048 * 2)
    qk_build("l0a_wq", 1, qfm1, slot, toks)
    qk_build("l0a_wk", 1, kfm1, slot, toks)
    A.free("l0a_xn", "l0a_wslot", "l0a_tok", "l0a_ropesc")
    O0 = A.alloc("l0a_O0", 16384 * 2)
    spill(1, O0, False)
    attn_group(1, qfm1, kfm1)
    A.free("l0a_q1", "l0a_k1", "l0a_vtok", "l0a_pt")
    sqr = A.alloc("l0a_sqa", 2048 * 4)
    rrow = env.zrow
    for f in range(8):
        b.op("scalar", (lambda f=f: lambda e: e.activation(out=sqr.ap(F32, T), in_=O0.ap(BF16, T, coff=f * T), func=AF.Square, bias=0.0, scale=1.0))(),
             rd=[O0.key(BF16, T, coff=f * T)], wr=[sqr.key()], tag=f"l0a_sq{f}")
        for half in range(2):
            for ch in range(4):
                b.op("tensor", (lambda f=f, half=half, ch=ch: lambda e: e.matmul(PS[4].ap(F32, 512, pn=1), env.ones.ap(F32, 1, coff=1, p0=half * 64, pn=64), sqr.ap(F32, 512, coff=ch * 512, p0=half * 64, pn=64), start=True, stop=True, tile_position=(half * 64, 0)))(),
                     rd=[sqr.key(), onescf_key(env)], wr=[PS[4].key()], tag=f"l0a_ss{f}_{half}_{ch}")
                b.op("scalar", (lambda: lambda e: e.activation(out=rrow.ap(F32, 512, pn=1), in_=PS[4].ap(F32, 512, pn=1), func=AF.Ln, bias=fc(env, 1e-5, 1), scale=fc(env, 1.0 / 64, 1)))(),
                     rd=[PS[4].key(), fck(env, 1e-5), fck(env, 1.0 / 64)], wr=[rrow.key(F32, 512)], tag=f"l0a_ln{f}_{half}_{ch}")
                b.op("scalar", (lambda: lambda e: e.activation(out=rrow.ap(F32, 512, coff=512, pn=1), in_=rrow.ap(F32, 512, pn=1), func=AF.Exp, bias=0.0, scale=fc(env, -0.5, 1)))(),
                     rd=[rrow.key(F32, 512), fck(env, -0.5)], wr=[rrow.key(F32, 512, coff=512)], tag=f"l0a_rs{f}_{half}_{ch}")
                b.op("tensor", (lambda half=half: lambda e: e.matmul(PS[5].ap(F32, 512, p0=half * 64, pn=64), ones_row_f32(env)[:, 0:64], rrow.ap(F32, 512, coff=512, pn=1), start=True, stop=True))(),
                     rd=[rrow.key(F32, 512, coff=512), onesrow_key(env)], wr=[PS[5].key()], tag=f"l0a_rb{f}_{half}_{ch}")
                b.op("scalar", (lambda half=half: lambda e: e.activation(out=env.bcsc.ap(F32, 512, p0=half * 64, pn=64), in_=PS[5].ap(F32, 512, p0=half * 64, pn=64), func=AF.Copy, bias=0.0, scale=1.0))(),
                     rd=[PS[5].key()], wr=[env.bcsc.key(F32, 512)], tag=f"l0a_rbe{f}_{half}_{ch}")
                oap = O0.ap(BF16, 512, coff=f * T + ch * 512, p0=half * 64, pn=64)
                ok = O0.key(BF16, 512, coff=f * T + ch * 512)
                b.op("vector", (lambda oap=oap, half=half: lambda e: e.tensor_tensor(out=oap, in0=oap, in1=env.bcsc.ap(F32, 512, p0=half * 64, pn=64), op=ALU.mult))(),
                     rd=[ok, env.bcsc.key(F32, 512)], wr=[ok], tag=f"l0a_ra{f}_{half}_{ch}")
        b.op("vector", (lambda f=f: lambda e: e.tensor_scalar(out=O0.ap(BF16, T, coff=f * T), in0=O0.ap(BF16, T, coff=f * T), scalar1=rmswcol, scalar2=None, op0=ALU.mult))(),
             rd=[O0.key(BF16, T, coff=f * T), rmswk], wr=[O0.key(BF16, T, coff=f * T)], tag=f"l0a_rw{f}")

    slot = A.alloc("l0a_wslot", 16384)
    emit_wload(b, env, slot, "l0a_wo", 1024, 1024)
    for mt in range(8):
        for ch in range(4):
            bk = (mt * 4 + ch) % 4
            for kt in range(8):
                lhs = slot.ap(BF16, 128, coff=kt * 1024 + mt * 128)
                lk = slot.key(BF16, 128, coff=kt * 1024 + mt * 128)
                rhs = O0.ap(BF16, 512, coff=kt * T + ch * 512)
                rk = O0.key(BF16, 512, coff=kt * T + ch * 512)
                b.op("tensor", (lambda lhs=lhs, rhs=rhs, bk=bk, kt=kt: lambda e: e.matmul(PS[bk].ap(F32, 512), lhs, rhs, start=(kt == 0), stop=(kt == 7)))(),
                     rd=[lk, rk], wr=[PS[bk].key()], tag=f"l0a_wo{mt}_{ch}_{kt}")
            xap = env.XR.ap(F32, 512, coff=mt * T + ch * 512)
            xk = env.XR.key(F32, 512, coff=mt * T + ch * 512)
            b.op("vector", (lambda xap=xap, bk=bk: lambda e: e.tensor_tensor(out=xap, in0=PS[bk].ap(F32, 512), in1=xap, op=ALU.add))(),
                 rd=[PS[bk].key(), xk], wr=[xk], tag=f"l0a_res{mt}_{ch}")
    A.reset(m0)


BLOCKS["l0a"] = (emit_l0a, dram_l0a, host_l0a)


def dram_l1a():
    return {
        "l1a_wq": ((1024, 1024), F32, "ExternalInput"),
        "l1a_wdn": ((1024, 512), F32, "ExternalInput"),
        "l1a_wku": ((512, 1024), F32, "ExternalInput"),
        "l1a_wvu": ((512, 1024), F32, "ExternalInput"),
        "l1a_wo": ((1024, 1024), F32, "ExternalInput"),
        "l1a_cos": ((128, 512), F32, "ExternalInput"),
        "l1a_sin": ((128, 512), F32, "ExternalInput"),
    }


def host_l1a(inp):
    perm = np.zeros(1024, np.int64)
    for h in range(16):
        for eo in range(2):
            for m in range(32):
                perm[h * 64 + eo * 32 + m] = h * 64 + 2 * m + eo
    inv = 1.0 / 10000.0 ** (np.arange(0, 64, 2, dtype=np.float32) / 64)
    ang = np.outer(np.arange(T, dtype=np.float32), inv)
    cosT = np.cos(ang).astype(np.float32).reshape(16, 128, 32).transpose(1, 0, 2).reshape(128, 512)
    sinT = np.sin(ang).astype(np.float32).reshape(16, 128, 32).transpose(1, 0, 2).reshape(128, 512)
    return {
        "l1a_wq": np.ascontiguousarray(np.asarray(inp["L1_attn_wq"], np.float32)[:, perm]),
        "l1a_wdn": np.ascontiguousarray(np.asarray(inp["L1_attn_w_down"], np.float32)),
        "l1a_wku": np.ascontiguousarray(np.asarray(inp["L1_attn_wk_up"], np.float32)[:, perm]),
        "l1a_wvu": np.ascontiguousarray(np.asarray(inp["L1_attn_wv_up"], np.float32)),
        "l1a_wo": np.ascontiguousarray(np.asarray(inp["L1_attn_wo"], np.float32)),
        "l1a_cos": np.ascontiguousarray(cosT),
        "l1a_sin": np.ascontiguousarray(sinT),
    }


def emit_rope_transpose2(b, env, *, name, tokreg, toff, ctab, dstfm, tt, NH, M, ncols):
    sc = env.ropesc
    cosap = ctab.ap(F32, M, coff=tt * M).unsqueeze(1).to_broadcast([128, NH, M])
    sinap = ctab.ap(F32, M, coff=16 * M + tt * M).unsqueeze(1).to_broadcast([128, NH, M])
    ck = ctab.key()
    base4 = tokreg.ap(BF16, ncols, coff=toff).rearrange("p (h eo m) -> p h eo m", h=NH, eo=2)
    re = base4[:, :, 0, :]
    im = base4[:, :, 1, :]
    rek = tokreg.key(BF16, ncols, coff=toff)
    imk = rek
    HM = NH * M
    t1 = sc.ap(BF16, HM, coff=0).rearrange("p (h m) -> p h m", h=NH)
    t2 = sc.ap(BF16, HM, coff=HM).rearrange("p (h m) -> p h m", h=NH)
    t3 = sc.ap(BF16, HM, coff=2 * HM).rearrange("p (h m) -> p h m", h=NH)
    k1 = sc.key(BF16, HM, coff=0); k2 = sc.key(BF16, HM, coff=HM); k3 = sc.key(BF16, HM, coff=2 * HM)
    b.op("vector", lambda e: e.tensor_tensor(out=t1, in0=re, in1=cosap, op=ALU.mult), rd=[rek, ck], wr=[k1], tag=name + "r1")
    b.op("vector", lambda e: e.tensor_tensor(out=t2, in0=re, in1=sinap, op=ALU.mult), rd=[rek, ck], wr=[k2], tag=name + "r2")
    b.op("vector", lambda e: e.tensor_tensor(out=t3, in0=im, in1=sinap, op=ALU.mult), rd=[imk, ck], wr=[k3], tag=name + "r3")
    b.op("vector", lambda e: e.tensor_tensor(out=re, in0=t1, in1=t3, op=ALU.subtract), rd=[k1, k3], wr=[rek], tag=name + "r4")
    b.op("vector", lambda e: e.tensor_tensor(out=t3, in0=im, in1=cosap, op=ALU.mult), rd=[imk, ck], wr=[k3], tag=name + "r5")
    b.op("vector", lambda e: e.tensor_tensor(out=im, in0=t2, in1=t3, op=ALU.add), rd=[k2, k3], wr=[imk], tag=name + "r6")
    for cb in range(ncols // 128):
        inap = tokreg.ap(BF16, 128, coff=toff + cb * 128)
        pt = env.PS[6 + cb % 2]
        ptap = pt.ap(BF16, 128)
        b.op("tensor", (lambda inap=inap, ptap=ptap: lambda e: e.transpose(ptap, inap, ident_bf(env)))(),
             rd=[rek, env.ident.key()], wr=[pt.key()], tag=f"{name}_tr{cb}")
        dap = dstfm.ap(BF16, 128, coff=cb * 2048 + tt * 128)
        dk = dstfm.key(BF16, 128, coff=cb * 2048 + tt * 128)
        b.op("scalar", (lambda dap=dap, ptap=ptap: lambda e: e.activation(out=dap, in_=ptap, func=AF.Copy, bias=0.0, scale=1.0))(),
             rd=[pt.key()], wr=[dk], tag=f"{name}_trev{cb}")


def emit_l1a(b, env):
    A, d, PS = env.arena, env.dram, env.PS
    m0 = A.mark()
    SCALE = 1.0 / 8.0
    XN = A.alloc("l1a_xn", 16384 * 2)
    slot = A.alloc("l1a_slot", 16384, top=True)
    lat = A.alloc("l1a_lat", 4 * 2048 * 2, top=True)
    emit_ln(b, env, XN, tag="l1a_ln")
    ctab = A.alloc("l1a_ctab", 4096, top=True)
    b.dma("sync", lambda e: e.dma_start(out=ctab.ap(F32, 512), in_=d["l1a_cos"][:, :]),
          rd=[("l1a_cos", 0, 1)], wr=[(ctab.name, ctab.off, ctab.off + 2048)], tag="l1a_ldc")
    b.dma("sync", lambda e: e.dma_start(out=ctab.ap(F32, 512, coff=512), in_=d["l1a_sin"][:, :]),
          rd=[("l1a_sin", 0, 1)], wr=[(ctab.name, ctab.off + 2048, ctab.off + 4096)], tag="l1a_lds")
    env.ropesc = A.alloc("l1a_ropesc", 4096, top=True)
    toks = A.alloc("l1a_tok", 2048 * 2 * 2, top=True)
    emit_wload(b, env, slot, "l1a_wdn", 1024, 512)

    def wl_slot_ap(kt, M, nc0, nn):
        return slot.ap(BF16, nn, coff=kt * M + nc0), slot.key(BF16, nn, coff=kt * M + nc0)

    for mt in range(4):
        for ch in range(4):
            bk = (mt * 4 + ch) % 4
            for kt in range(8):
                lhs = slot.ap(BF16, 128, coff=kt * 512 + mt * 128)
                lk = slot.key(BF16, 128, coff=kt * 512 + mt * 128)
                rhs = XN.ap(BF16, 512, coff=kt * T + ch * 512)
                rk = XN.key(BF16, 512, coff=kt * T + ch * 512)
                b.op("tensor", (lambda lhs=lhs, rhs=rhs, bk=bk, kt=kt: lambda e: e.matmul(PS[bk].ap(F32, 512), lhs, rhs, start=(kt == 0), stop=(kt == 7)))(),
                     rd=[lk, rk], wr=[PS[bk].key()], tag=f"l1a_lat{mt}_{ch}_{kt}")
            b.op("scalar", (lambda mt=mt, ch=ch, bk=bk: lambda e: e.activation(out=lat.ap(BF16, 512, coff=mt * T + ch * 512), in_=PS[bk].ap(F32, 512), func=AF.Copy, bias=0.0, scale=1.0))(),
                 rd=[PS[bk].key()], wr=[lat.key(BF16, 512, coff=mt * T + ch * 512)], tag=f"l1a_latev{mt}_{ch}")
    qfm = A.alloc("l1a_qfm", 8 * 2048 * 2)
    emit_wload(b, env, slot, "l1a_wq", 1024, 1024)
    for tt in range(16):
        toff = (tt % 2) * 1024
        for nc0 in (0, 512):
            bk = (tt * 2 + nc0 // 512) % 4
            for kt in range(8):
                lhs = XN.ap(BF16, 128, coff=kt * T + tt * 128)
                lk = XN.key(BF16, 128, coff=kt * T + tt * 128)
                rhs = slot.ap(BF16, 512, coff=kt * 1024 + nc0)
                rk = slot.key(BF16, 512, coff=kt * 1024 + nc0)
                b.op("tensor", (lambda lhs=lhs, rhs=rhs, bk=bk, kt=kt: lambda e: e.matmul(PS[bk].ap(F32, 512), lhs, rhs, start=(kt == 0), stop=(kt == 7)))(),
                     rd=[lk, rk], wr=[PS[bk].key()], tag=f"l1a_q{tt}_{nc0}_{kt}")
            b.op("scalar", (lambda toff=toff, nc0=nc0, bk=bk: lambda e: e.activation(out=toks.ap(BF16, 512, coff=toff + nc0), in_=PS[bk].ap(F32, 512), func=AF.Copy, bias=0.0, scale=1.0))(),
                 rd=[PS[bk].key()], wr=[toks.key(BF16, 512, coff=toff + nc0)], tag=f"l1a_qev{tt}_{nc0}")
        emit_rope_transpose2(b, env, name=f"l1a_qrt{tt}", tokreg=toks, toff=toff, ctab=ctab, dstfm=qfm, tt=tt, NH=16, M=32, ncols=1024)
    A.free("l1a_xn")
    kfm = A.alloc("l1a_kfm", 8 * 2048 * 2)
    emit_wload(b, env, slot, "l1a_wku", 512, 1024)
    for tt in range(16):
        toff = (tt % 2) * 1024
        for nc0 in (0, 512):
            bk = (tt * 2 + nc0 // 512) % 4
            for kt in range(4):
                lhs = lat.ap(BF16, 128, coff=kt * T + tt * 128)
                lk = lat.key(BF16, 128, coff=kt * T + tt * 128)
                rhs = slot.ap(BF16, 512, coff=kt * 1024 + nc0)
                rk = slot.key(BF16, 512, coff=kt * 1024 + nc0)
                b.op("tensor", (lambda lhs=lhs, rhs=rhs, bk=bk, kt=kt: lambda e: e.matmul(PS[bk].ap(F32, 512), lhs, rhs, start=(kt == 0), stop=(kt == 3)))(),
                     rd=[lk, rk], wr=[PS[bk].key()], tag=f"l1a_k{tt}_{nc0}_{kt}")
            b.op("scalar", (lambda toff=toff, nc0=nc0, bk=bk: lambda e: e.activation(out=toks.ap(BF16, 512, coff=toff + nc0), in_=PS[bk].ap(F32, 512), func=AF.Copy, bias=0.0, scale=1.0))(),
                 rd=[PS[bk].key()], wr=[toks.key(BF16, 512, coff=toff + nc0)], tag=f"l1a_kev{tt}_{nc0}")
        emit_rope_transpose2(b, env, name=f"l1a_krt{tt}", tokreg=toks, toff=toff, ctab=ctab, dstfm=kfm, tt=tt, NH=16, M=32, ncols=1024)
    A.free("l1a_tok", "l1a_ropesc", "l1a_ctab")
    vtok = A.alloc("l1a_vtok", 16 * 1024 * 2)
    emit_wload(b, env, slot, "l1a_wvu", 512, 1024)
    for tt in range(16):
        for nc0 in (0, 512):
            bk = (tt * 2 + nc0 // 512) % 4
            for kt in range(4):
                lhs = lat.ap(BF16, 128, coff=kt * T + tt * 128)
                lk = lat.key(BF16, 128, coff=kt * T + tt * 128)
                rhs = slot.ap(BF16, 512, coff=kt * 1024 + nc0)
                rk = slot.key(BF16, 512, coff=kt * 1024 + nc0)
                b.op("tensor", (lambda lhs=lhs, rhs=rhs, bk=bk, kt=kt: lambda e: e.matmul(PS[bk].ap(F32, 512), lhs, rhs, start=(kt == 0), stop=(kt == 3)))(),
                     rd=[lk, rk], wr=[PS[bk].key()], tag=f"l1a_v{tt}_{nc0}_{kt}")
            b.op("scalar", (lambda tt=tt, nc0=nc0, bk=bk: lambda e: e.activation(out=vtok.ap(BF16, 512, coff=tt * 1024 + nc0), in_=PS[bk].ap(F32, 512), func=AF.Copy, bias=0.0, scale=1.0))(),
                 rd=[PS[bk].key()], wr=[vtok.key(BF16, 512, coff=tt * 1024 + nc0)], tag=f"l1a_vev{tt}_{nc0}")
    A.free("l1a_lat", "l1a_slot")
    O = A.alloc("l1a_O", 16384 * 2)
    ptsl = A.alloc("l1a_pt", 1024)
    zrow = A.alloc("l1a_zrow", 1024)
    bcsc = A.alloc("l1a_bc", 1024)
    for h in range(16):
        hb, hr = h // 2, (h % 2) * 64
        for qc in range(8):
            nkt = 2 * (qc + 1)
            ob = 2
            for kt in range(nkt):
                sb = kt % 2
                lhs = kfm.ap(BF16, 128, coff=hb * 2048 + kt * 128, p0=hr, pn=64)
                lk = kfm.key(BF16, 128, coff=hb * 2048 + kt * 128)
                rhs = qfm.ap(BF16, 256, coff=hb * 2048 + qc * 256, p0=hr, pn=64)
                rk = qfm.key(BF16, 256, coff=hb * 2048 + qc * 256)
                b.op("tensor", (lambda lhs=lhs, rhs=rhs, sb=sb, hr=hr: lambda e: e.matmul(PS[sb].ap(F32, 256), lhs, rhs, start=True, stop=True, tile_position=(hr, 0)))(),
                     rd=[lk, rk], wr=[PS[sb].key()], tag=f"l1a_st{h}_{qc}_{kt}")
                pt = ptsl.ap(BF16, 256, coff=(kt % 2) * 256)
                ptk = ptsl.key(BF16, 256, coff=(kt % 2) * 256)
                b.op("scalar", (lambda pt=pt, sb=sb: lambda e: e.activation(out=pt, in_=PS[sb].ap(F32, 256), func=AF.Exp, bias=0.0, scale=fc(env, SCALE)))(),
                     rd=[PS[sb].key(), fck(env, SCALE)], wr=[ptk], tag=f"l1a_exp{h}_{qc}_{kt}")
                vl = vtok.ap(BF16, 64, coff=kt * 1024 + h * 64)
                vk = vtok.key(BF16, 64, coff=kt * 1024 + h * 64)
                b.op("tensor", (lambda vl=vl, pt=pt, kt=kt, nkt=nkt: lambda e: e.matmul(PS[ob].ap(F32, 256, pn=64), vl, pt, start=(kt == 0), stop=(kt == nkt - 1)))(),
                     rd=[vk, ptk], wr=[PS[ob].key()], tag=f"l1a_pv{h}_{qc}_{kt}")
                b.op("tensor", (lambda pt=pt, kt=kt, nkt=nkt: lambda e: e.matmul(PS[4].ap(F32, 256, pn=1), ones_col_bf(env), pt, start=(kt == 0), stop=(kt == nkt - 1)))(),
                     rd=[(env.ones.name, env.ones.off, env.ones.off + 2), ptk], wr=[PS[4].key()], tag=f"l1a_zm{h}_{qc}_{kt}")
            oft, orow = h // 2, (h % 2) * 64
            zr = zrow.ap(F32, 256, pn=1)
            zk = zrow.key(F32, 256)
            b.op("vector", (lambda zr=zr: lambda e: e.reciprocal(out=zr, in_=PS[4].ap(F32, 256, pn=1)))(),
                 rd=[PS[4].key()], wr=[zk], tag=f"l1a_z{h}_{qc}")
            b.op("tensor", (lambda zr=zr, orow=orow: lambda e: e.matmul(PS[3].ap(F32, 256, p0=orow, pn=64), ones_row_f32(env)[:, 0:64], zr, start=True, stop=True))(),
                 rd=[zk, onesrow_key(env)], wr=[PS[3].key()], tag=f"l1a_zbc{h}_{qc}")
            bc = bcsc.ap(F32, 256, p0=orow, pn=64)
            bck = bcsc.key(F32, 256)
            b.op("scalar", (lambda bc=bc, orow=orow: lambda e: e.activation(out=bc, in_=PS[3].ap(F32, 256, p0=orow, pn=64), func=AF.Copy, bias=0.0, scale=1.0))(),
                 rd=[PS[3].key()], wr=[bck], tag=f"l1a_bce{h}_{qc}")
            tc = env.tcsc.ap(BF16, 256, p0=orow, pn=64) if hasattr(env, 'tcsc') else None
            stg = ptsl.ap(BF16, 256, coff=512, p0=orow, pn=64)
            stgk = ptsl.key(BF16, 256, coff=512)
            b.op("scalar", (lambda stg=stg: lambda e: e.activation(out=stg, in_=PS[ob].ap(F32, 256, pn=64), func=AF.Copy, bias=0.0, scale=1.0))(),
                 rd=[PS[ob].key()], wr=[stgk], tag=f"l1a_oev{h}_{qc}")
            oap = O.ap(BF16, 256, coff=oft * 2048 + qc * 256, p0=orow, pn=64)
            ok = O.key(BF16, 256, coff=oft * 2048 + qc * 256)
            b.op("vector", (lambda oap=oap, stg=stg, bc=bc: lambda e: e.tensor_tensor(out=oap, in0=stg, in1=bc, op=ALU.mult))(),
                 rd=[stgk, bck], wr=[ok], tag=f"l1a_nrm{h}_{qc}")
    A.free("l1a_vtok", "l1a_qfm", "l1a_kfm", "l1a_pt", "l1a_zrow", "l1a_bc")
    slot = A.alloc("l1a_slot", 16384)
    emit_wload(b, env, slot, "l1a_wo", 1024, 1024)
    for mt in range(8):
        for ch in range(4):
            bk = (mt * 4 + ch) % 4
            for kt in range(8):
                lhs = slot.ap(BF16, 128, coff=kt * 1024 + mt * 128)
                lk = slot.key(BF16, 128, coff=kt * 1024 + mt * 128)
                rhs = O.ap(BF16, 512, coff=kt * T + ch * 512)
                rk = O.key(BF16, 512, coff=kt * T + ch * 512)
                b.op("tensor", (lambda lhs=lhs, rhs=rhs, bk=bk, kt=kt: lambda e: e.matmul(PS[bk].ap(F32, 512), lhs, rhs, start=(kt == 0), stop=(kt == 7)))(),
                     rd=[lk, rk], wr=[PS[bk].key()], tag=f"l1a_wo{mt}_{ch}_{kt}")
            xap = env.XR.ap(F32, 512, coff=mt * T + ch * 512)
            xk = env.XR.key(F32, 512, coff=mt * T + ch * 512)
            b.op("vector", (lambda xap=xap, bk=bk: lambda e: e.tensor_tensor(out=xap, in0=PS[bk].ap(F32, 512), in1=xap, op=ALU.add))(),
                 rd=[PS[bk].key(), xk], wr=[xk], tag=f"l1a_res{mt}_{ch}")
    A.reset(m0)


BLOCKS["l1a"] = (emit_l1a, dram_l1a, host_l1a)


def dram_l3f():
    return {
        "l3f_g": ((128, 8), F32, "ExternalInput"),
        "l3f_b": ((128, 8), F32, "ExternalInput"),
        "l3f_rW": ((128, 64), F32, "ExternalInput"),
        "l3f_rb": ((1, 8), F32, "ExternalInput"),
        "l3f_W1": ((8, 1024, 2048), F32, "ExternalInput"),
        "l3f_b1": ((128, 128), F32, "ExternalInput"),
        "l3f_W2": ((8, 2048, 1024), F32, "ExternalInput"),
        "l3f_b2": ((1, 8192), F32, "ExternalInput"),
        "l3f_W1b": ((8, 1024, 2048), BF16, "Internal"),
        "l3f_W2b": ((8, 2048, 1024), BF16, "Internal"),
    }


def host_l3f(inp):
    return {
        "l3f_g": np.ascontiguousarray(np.asarray(inp["L3_ffn_ln_g"], np.float32).reshape(8, 128).T),
        "l3f_b": np.ascontiguousarray(np.asarray(inp["L3_ffn_ln_b"], np.float32).reshape(8, 128).T),
        "l3f_rW": np.ascontiguousarray(np.asarray(inp["L3_ffn_rW"], np.float32).reshape(8, 128, 8).transpose(1, 0, 2).reshape(128, 64)),
        "l3f_rb": np.asarray(inp["L3_ffn_rb"], np.float32).reshape(1, 8),
        "l3f_W1": np.ascontiguousarray(np.asarray(inp["L3_ffn_W1"], np.float32)),
        "l3f_b1": np.ascontiguousarray(np.asarray(inp["L3_ffn_b1"], np.float32).reshape(8, 16, 128).transpose(2, 0, 1).reshape(128, 128)),
        "l3f_W2": np.ascontiguousarray(np.asarray(inp["L3_ffn_W2"], np.float32)),
        "l3f_b2": np.asarray(inp["L3_ffn_b2"], np.float32).reshape(1, 8192),
    }


def emit_cast_dram(b, env, src, dst, nslabs=1):
    d = env.dram
    sh = d[src].shape
    tot = 1
    for s in sh:
        tot *= s
    for i in range(nslabs):
        sl = slice(i * sh[0] // nslabs, (i + 1) * sh[0] // nslabs)
        b.dma("gpsimd", (lambda sl=sl: lambda e: e.dma_start(out=d[dst][sl], in_=d[src][sl]))(),
              rd=[(src, 0, 1)], wr=[(dst, sl.start, sl.stop)], tag=f"cast_{src}_{i}")


def emit_topk2_rows(b, env, probs, pk, N, sm, smoff, name):
    raise NotImplementedError


def emit_l3f(b, env):
    A, d, PS = env.arena, env.dram, env.PS
    m0 = A.mark()
    emit_cast_dram(b, env, "l3f_W1", "l3f_W1b")
    emit_cast_dram(b, env, "l3f_W2", "l3f_W2b")
    XN = A.alloc("l3f_xn", 16384 * 2)
    emit_ln(b, env, XN, tag="l3f_ln")
    for f in range(8):
        b.op("vector", (lambda f=f: lambda e: e.tensor_tensor(out=env.XR.ap(F32, T, coff=f * T), in0=env.XR.ap(F32, T, coff=f * T), in1=XN.ap(BF16, T, coff=f * T), op=ALU.add))(),
             rd=[env.XR.key(F32, T, coff=f * T), XN.key(BF16, T, coff=f * T)], wr=[env.XR.key(F32, T, coff=f * T)], tag=f"l3f_lnres{f}")
    gb = env.ST.alloc("l3f_gb", 2048)
    b.dma("sync", lambda e: e.dma_start(out=gb.ap(F32, 8), in_=d["l3f_g"][:, :]),
          rd=[("l3f_g", 0, 1)], wr=[gb.key(F32, 8)], tag="l3f_ldg")
    b.dma("sync", lambda e: e.dma_start(out=gb.ap(F32, 8, coff=8), in_=d["l3f_b"][:, :]),
          rd=[("l3f_b", 0, 1)], wr=[gb.key(F32, 8, coff=8)], tag="l3f_ldb")
    for f in range(8):
        b.op("vector", (lambda f=f: lambda e: e.tensor_scalar(out=XN.ap(BF16, T, coff=f * T), in0=XN.ap(BF16, T, coff=f * T), scalar1=gb.ap(F32, 1, coff=f), scalar2=gb.ap(F32, 1, coff=8 + f), op0=ALU.mult, op1=ALU.add))(),
             rd=[XN.key(BF16, T, coff=f * T), gb.key()], wr=[XN.key(BF16, T, coff=f * T)], tag=f"l3f_z{f}")
    rws = env.ST.alloc("l3f_rws", 512)
    b.dma("gpsimd", lambda e: e.dma_start(out=rws.ap(BF16, 64), in_=d["l3f_rW"][:, :]),
          rd=[("l3f_rW", 0, 1)], wr=[rws.key()], tag="l3f_ldrw")
    rbr = env.ST.alloc("l3f_rbr", 512)
    b.dma("sync", lambda e: e.dma_start(out=rbr.ap(F32, 8, pn=1), in_=d["l3f_rb"][:, :]),
          rd=[("l3f_rb", 0, 1)], wr=[rbr.key(F32, 8)], tag="l3f_ldrb")
    b2r = A.alloc("l3f_b2r", 16384)
    b.dma("gpsimd", lambda e: e.dma_start(out=b2r.ap(BF16, 8192, pn=1), in_=d["l3f_b2"][:, :]),
          rd=[("l3f_b2", 0, 1)], wr=[b2r.key()], tag="l3f_ldb2")
    wrow = A.alloc("l3f_wrow", 8 * 2048 * 2)
    wsc = A.alloc("l3f_wsc", 4096)
    b1c = env.ST.alloc("l3f_b1c", 1024)
    b.dma("sync", lambda e: e.dma_start(out=b1c.ap(F32, 128), in_=d["l3f_b1"][:, :]),
          rd=[("l3f_b1", 0, 1)], wr=[b1c.key()], tag="l3f_ldb1")
    b.op("tensor", lambda e: e.matmul(PS[1].ap(F32, 8), ones_row_f32(env), rbr.ap(F32, 8, pn=1), start=True, stop=True),
         rd=[onesrow_key(env), rbr.key(F32, 8)], wr=[PS[1].key()], tag="l3f_rbbc")
    rbbc = env.ST.alloc("l3f_rbbc", 512)
    b.op("scalar", lambda e: e.activation(out=rbbc.ap(F32, 8), in_=PS[1].ap(F32, 8), func=AF.Copy, bias=0.0, scale=1.0),
         rd=[PS[1].key()], wr=[rbbc.key()], tag="l3f_rbbce")
    for tt in range(16):
        for kt in range(8):
            lhs = XN.ap(BF16, 128, coff=kt * T + tt * 128)
            lk = XN.key(BF16, 128, coff=kt * T + tt * 128)
            rhs = rws.ap(BF16, 8, coff=kt * 8)
            rk = rws.key()
            b.op("tensor", (lambda lhs=lhs, rhs=rhs, kt=kt: lambda e: e.matmul(PS[0].ap(F32, 8), lhs, rhs, start=(kt == 0), stop=(kt == 7)))(),
                 rd=[lk, rk], wr=[PS[0].key()], tag=f"l3f_rt{tt}_{kt}")
        lg = wsc.ap(F32, 8)
        lgk = wsc.key(F32, 8)
        b.op("vector", (lambda lg=lg: lambda e: e.tensor_tensor(out=lg, in0=PS[0].ap(F32, 8), in1=rbbc.ap(F32, 8), op=ALU.add))(),
             rd=[PS[0].key(), rbbc.key()], wr=[lgk], tag=f"l3f_lg{tt}")
        mcol = wsc.ap(F32, 1, coff=8); mk = wsc.key(F32, 1, coff=8)
        b.op("vector", (lambda lg=lg, mcol=mcol: lambda e: e.tensor_reduce(out=mcol, in_=lg, axis=AX.X, op=ALU.max))(),
             rd=[lgk], wr=[mk], tag=f"l3f_m{tt}")
        nm = wsc.ap(F32, 1, coff=9); nmk = wsc.key(F32, 1, coff=9)
        b.op("vector", (lambda mcol=mcol, nm=nm: lambda e: e.tensor_scalar(out=nm, in0=mcol, scalar1=-1.0, scalar2=None, op0=ALU.mult))(),
             rd=[mk], wr=[nmk], tag=f"l3f_nm{tt}")
        scol = wsc.ap(F32, 1, coff=10); sk = wsc.key(F32, 1, coff=10)
        b.op("scalar", (lambda lg=lg, nm=nm: lambda e: e.activation(out=wsc.ap(F32, 8, coff=16), in_=lg, func=AF.Exp, bias=nm, scale=1.0))(),
             rd=[lgk, nmk], wr=[wsc.key(F32, 8, coff=16)], tag=f"l3f_exp{tt}")
        b.op("vector", (lambda scol=scol: lambda e: e.tensor_scalar(out=wsc.ap(F32, 8, coff=56), in0=wsc.ap(F32, 8, coff=16), scalar1=0.0, scalar2=None, op0=ALU.add, op1=ALU.add, accum_out=scol))(),
             rd=[wsc.key(F32, 8, coff=16)], wr=[wsc.key(F32, 8, coff=56), sk], tag=f"l3f_sum{tt}")
        rcol = wsc.ap(F32, 1, coff=11); rk2 = wsc.key(F32, 1, coff=11)
        b.op("vector", (lambda scol=scol, rcol=rcol: lambda e: e.reciprocal(out=rcol, in_=scol))(),
             rd=[sk], wr=[rk2], tag=f"l3f_rc{tt}")
        pr = wsc.ap(F32, 8, coff=16); prk = wsc.key(F32, 8, coff=16)
        b.op("vector", (lambda pr=pr, rcol=rcol: lambda e: e.tensor_scalar(out=pr, in0=pr, scalar1=rcol, scalar2=None, op0=ALU.mult))(),
             rd=[prk, rk2], wr=[prk], tag=f"l3f_pr{tt}")
        m1 = wsc.ap(F32, 1, coff=12); m1k = wsc.key(F32, 1, coff=12)
        b.op("vector", (lambda pr=pr, m1=m1: lambda e: e.tensor_reduce(out=m1, in_=pr, axis=AX.X, op=ALU.max))(),
             rd=[prk], wr=[m1k], tag=f"l3f_m1{tt}")
        oh1 = wsc.ap(F32, 8, coff=24); oh1k = wsc.key(F32, 8, coff=24)
        b.op("vector", (lambda pr=pr, oh1=oh1, m1=m1: lambda e: e.tensor_scalar(out=oh1, in0=pr, scalar1=m1, scalar2=None, op0=ALU.is_ge))(),
             rd=[prk, m1k], wr=[oh1k], tag=f"l3f_oh1{tt}")
        p2 = wsc.ap(F32, 8, coff=32); p2k = wsc.key(F32, 8, coff=32)
        b.op("vector", (lambda pr=pr, p2=p2, m1=m1: lambda e: e.tensor_scalar(out=p2, in0=pr, scalar1=m1, scalar2=None, op0=ALU.is_lt))(),
             rd=[prk, m1k], wr=[p2k], tag=f"l3f_plt{tt}")
        b.op("vector", (lambda pr=pr, p2=p2: lambda e: e.tensor_tensor(out=p2, in0=p2, in1=pr, op=ALU.mult))(),
             rd=[p2k, prk], wr=[p2k], tag=f"l3f_p2{tt}")
        m2 = wsc.ap(F32, 1, coff=13); m2k = wsc.key(F32, 1, coff=13)
        b.op("vector", (lambda p2=p2, m2=m2: lambda e: e.tensor_reduce(out=m2, in_=p2, axis=AX.X, op=ALU.max))(),
             rd=[p2k], wr=[m2k], tag=f"l3f_m2{tt}")
        oh2 = wsc.ap(F32, 8, coff=40); oh2k = wsc.key(F32, 8, coff=40)
        b.op("vector", (lambda p2=p2, oh2=oh2, m2=m2: lambda e: e.tensor_scalar(out=oh2, in0=p2, scalar1=m2, scalar2=None, op0=ALU.is_ge))(),
             rd=[p2k, m2k], wr=[oh2k], tag=f"l3f_oh2{tt}")
        dn = wsc.ap(F32, 1, coff=14); dnk = wsc.key(F32, 1, coff=14)
        b.op("vector", (lambda m1=m1, m2=m2, dn=dn: lambda e: e.tensor_tensor(out=dn, in0=m1, in1=m2, op=ALU.add))(),
             rd=[m1k, m2k], wr=[dnk], tag=f"l3f_dn{tt}")
        b.op("vector", (lambda dn=dn: lambda e: e.tensor_scalar(out=dn, in0=dn, scalar1=1e-8, scalar2=None, op0=ALU.max))(),
             rd=[dnk], wr=[dnk], tag=f"l3f_dnm{tt}")
        b.op("vector", (lambda dn=dn: lambda e: e.reciprocal(out=dn, in_=dn))(),
             rd=[dnk], wr=[dnk], tag=f"l3f_dni{tt}")
        wv = wsc.ap(BF16, 8, coff=96)
        wvk = wsc.key(BF16, 8, coff=96)
        b.op("vector", (lambda oh1=oh1, m1=m1: lambda e: e.tensor_scalar(out=oh1, in0=oh1, scalar1=m1, scalar2=None, op0=ALU.mult))(),
             rd=[oh1k, m1k], wr=[oh1k], tag=f"l3f_w1{tt}")
        b.op("vector", (lambda oh2=oh2, m2=m2: lambda e: e.tensor_scalar(out=oh2, in0=oh2, scalar1=m2, scalar2=None, op0=ALU.mult))(),
             rd=[oh2k, m2k], wr=[oh2k], tag=f"l3f_w2{tt}")
        b.op("vector", (lambda oh1=oh1, oh2=oh2: lambda e: e.tensor_tensor(out=oh1, in0=oh1, in1=oh2, op=ALU.add))(),
             rd=[oh1k, oh2k], wr=[oh1k], tag=f"l3f_wa{tt}")
        b.op("vector", (lambda oh1=oh1, dn=dn, wv=wv: lambda e: e.tensor_scalar(out=wv, in0=oh1, scalar1=dn, scalar2=None, op0=ALU.mult))(),
             rd=[oh1k, dnk], wr=[wvk], tag=f"l3f_wf{tt}")
        if "l3f_prd" in env.dram and tt < 16:
            pd = env.ST.allocs.get("l3f_prd") or env.ST.alloc("l3f_prd", 1024)
            b.op("vector", (lambda tt=tt, pd=pd: lambda e: e.tensor_copy(out=pd.ap(F32, 8, coff=tt * 8), in_=wsc.ap(F32, 8, coff=16)))(),
                 rd=[wsc.key(F32, 8, coff=16)], wr=[pd.key(F32, 8, coff=tt * 8)], tag=f"l3f_prd{tt}")
            b.op("vector", (lambda tt=tt, pd=pd: lambda e: e.tensor_copy(out=pd.ap(F32, 8, coff=128 + tt * 8), in_=wsc.ap(F32, 8, coff=8)))(),
                 rd=[wsc.key(F32, 8, coff=8)], wr=[pd.key(F32, 8, coff=128 + tt * 8)], tag=f"l3f_prd2{tt}")
        for e_ in range(8):
            pt = PS[6 + e_ % 2]
            b.op("tensor", (lambda e_=e_, pt=pt: lambda e: e.transpose(pt.ap(BF16, 128, pn=1), wsc.ap(BF16, 1, coff=96 + e_), ident_bf(env)))(),
                 rd=[wvk, env.ident.key()], wr=[pt.key()], tag=f"l3f_wt{tt}_{e_}")
            dap = wrow.ap(BF16, 128, coff=e_ * 2048 + tt * 128, pn=1)
            dk = (wrow.name, wrow.off + (e_ * 2048 + tt * 128) * 2, wrow.off + (e_ * 2048 + (tt + 1) * 128) * 2)
            b.op("scalar", (lambda dap=dap, pt=pt: lambda e: e.activation(out=dap, in_=pt.ap(BF16, 128, pn=1), func=AF.Copy, bias=0.0, scale=1.0))(),
                 rd=[pt.key()], wr=[dk], tag=f"l3f_wte{tt}_{e_}")
    if "l3f_zd" in env.dram:
        emit_dbg(b, env, "l3f_zd", XN, dtype=BF16)
        emit_dbg(b, env, "l3f_wrowd", wrow, dtype=BF16)
    if "l3f_prd" in env.dram:
        emit_dbg(b, env, "l3f_prd", env.ST.allocs["l3f_prd"], dtype=F32)
    hid = A.alloc("l3f_hid", 16 * 512 * 2)
    acc = A.alloc("l3f_acc", 8 * 512 * 4)
    wbc = A.alloc("l3f_wbc", 512 * 4)
    ws = [A.alloc(f"l3f_ws{i}", 4096, top=True) for i in range(2)]
    for tc in range(4):
        for e_ in range(8):
            b.op("tensor", (lambda e_=e_, tc=tc: lambda e: e.matmul(PS[5].ap(F32, 512), ones_row_bf(env), wrow.ap(BF16, 512, coff=e_ * 2048 + tc * 512, pn=1), start=True, stop=True))(),
                 rd=[onesrowbf_key(env), (wrow.name, wrow.off, wrow.off + 16384)], wr=[PS[5].key()], tag=f"l3f_wbc{tc}_{e_}")
            b.op("scalar", (lambda: lambda e: e.activation(out=wbc.ap(F32, 512), in_=PS[5].ap(F32, 512), func=AF.Copy, bias=0.0, scale=1.0))(),
                 rd=[PS[5].key()], wr=[wbc.key()], tag=f"l3f_wbce{tc}_{e_}")

            def wsrc(kt, mg0, mgn, e_=e_):
                ap = d["l3f_W1b"][e_, kt * 128:(kt + 1) * 128, mg0 * 128:(mg0 + mgn) * 128]
                return ap, ("l3f_W1b", e_, e_ + 1)

            def rhs_z(kt, tc=tc):
                return XN.ap(BF16, 512, coff=kt * T + tc * 512), XN.key(BF16, 512, coff=kt * T + tc * 512)

            def ev_hid(mt, ps, e_=e_, tc=tc):
                b.op("scalar", (lambda mt=mt, ps=ps: lambda e: e.activation(out=hid.ap(BF16, 512, coff=mt * 512), in_=ps.ap(F32, 512), func=AF.Gelu, bias=b1c.ap(F32, 1, coff=e_ * 16 + mt), scale=1.0))(),
                     rd=[ps.key(), b1c.key()], wr=[hid.key(BF16, 512, coff=mt * 512)], tag=f"l3f_h{tc}_{e_}_{mt}")
                b.op("vector", (lambda mt=mt: lambda e: e.tensor_tensor(out=hid.ap(BF16, 512, coff=mt * 512), in0=hid.ap(BF16, 512, coff=mt * 512), in1=wbc.ap(F32, 512), op=ALU.mult))(),
                     rd=[hid.key(BF16, 512, coff=mt * 512), wbc.key()], wr=[hid.key(BF16, 512, coff=mt * 512)], tag=f"l3f_hw{tc}_{e_}_{mt}")

            g_gemm(b, env, name=f"l3f_up{tc}_{e_}", KT=8, MT=16, Wsrc=wsrc, rhs=rhs_z, evac=ev_hid, wslots=ws, banks=(0, 1, 2, 3))

            def wsrc2(kt, mg0, mgn, e_=e_):
                ap = d["l3f_W2b"][e_, kt * 128:(kt + 1) * 128, mg0 * 128:(mg0 + mgn) * 128]
                return ap, ("l3f_W2b", e_, e_ + 1)

            def rhs_h(kt):
                return hid.ap(BF16, 512, coff=kt * 512), hid.key(BF16, 512, coff=kt * 512)

            def ev_dn(mt, ps, e_=e_, tc=tc):
                b.op("tensor", (lambda mt=mt, ps=ps, e_=e_, tc=tc: lambda e: e.matmul(ps.ap(F32, 512), b2r.ap(BF16, 128, coff=e_ * 1024 + mt * 128, pn=1), wrow.ap(BF16, 512, coff=e_ * 2048 + tc * 512, pn=1), start=False, stop=True))(),
                     rd=[b2r.key(), (wrow.name, wrow.off, wrow.off + 16384)], wr=[ps.key()], tag=f"l3f_b2{tc}_{e_}_{mt}")
                if e_ == 0:
                    b.op("scalar", (lambda mt=mt, ps=ps: lambda e: e.activation(out=acc.ap(F32, 512, coff=mt * 512), in_=ps.ap(F32, 512), func=AF.Copy, bias=0.0, scale=1.0))(),
                         rd=[ps.key()], wr=[acc.key(F32, 512, coff=mt * 512)], tag=f"l3f_ac{tc}_{e_}_{mt}")
                else:
                    b.op("vector", (lambda mt=mt, ps=ps: lambda e: e.tensor_tensor(out=acc.ap(F32, 512, coff=mt * 512), in0=ps.ap(F32, 512), in1=acc.ap(F32, 512, coff=mt * 512), op=ALU.add))(),
                         rd=[ps.key(), acc.key(F32, 512, coff=mt * 512)], wr=[acc.key(F32, 512, coff=mt * 512)], tag=f"l3f_ac{tc}_{e_}_{mt}")

            def gg_dn():
                mg0 = 0
                di = 0
                banks = (0, 1, 2, 3)
                while mg0 < 8:
                    mgn = min(4, 8 - mg0)
                    for kt in range(16):
                        sl = ws[di % 2]
                        di += 1
                        src_ap, src_key = wsrc2(kt, mg0, mgn)
                        nbytes = mgn * 128 * 2
                        b.dma("gpsimd", (lambda src_ap=src_ap, sl=sl, nbytes=nbytes: lambda e: e.dma_start(out=sl.ap(BF16, nbytes // 2), in_=src_ap))(),
                              rd=[src_key], wr=[sl.key(BF16, nbytes // 2)], tag=f"l3f_dw{tc}_{e_}_{kt}_{mg0}")
                        rap, rkey = rhs_h(kt)
                        for j in range(mgn):
                            bk = banks[j]
                            b.op("tensor", (lambda sl=sl, j=j, rap=rap, bk=bk, kt=kt: lambda e: e.matmul(PS[bk].ap(F32, 512), sl.ap(BF16, 128, coff=j * 128), rap, start=(kt == 0), stop=False))(),
                                 rd=[sl.key(BF16, 128, coff=j * 128), rkey], wr=[PS[bk].key()], tag=f"l3f_dmm{tc}_{e_}_{kt}_{mg0 + j}")
                    for j in range(mgn):
                        ev_dn(mg0 + j, PS[banks[j]])
                    mg0 += mgn

            gg_dn()
        if tc == 0 and "l3f_zd" in env.dram:
            emit_dbg(b, env, "l3f_accd", acc, dtype=F32)
        for mt in range(8):
            xap = env.XR.ap(F32, 512, coff=mt * T + tc * 512)
            xk = env.XR.key(F32, 512, coff=mt * T + tc * 512)
            b.op("vector", (lambda xap=xap, mt=mt: lambda e: e.tensor_tensor(out=xap, in0=acc.ap(F32, 512, coff=mt * 512), in1=xap, op=ALU.add))(),
                 rd=[acc.key(F32, 512, coff=mt * 512), xk], wr=[xk], tag=f"l3f_res{tc}_{mt}")
    A.reset(m0)


BLOCKS["l3f"] = (emit_l3f, dram_l3f, host_l3f)


def emit_dbg(b, env, name, reg, nbytes=None, coff=0, dtype=BF16):
    nb = nbytes if nbytes else reg.nbytes
    cols = nb // mybir.dt.size(dtype)
    d = env.dram
    b.dma("sync", (lambda: lambda e: e.dma_start(out=d[name][:, :], in_=reg.ap(dtype, cols, coff=coff)))(),
          rd=[(reg.name, reg.off, reg.off + reg.nbytes)], wr=[(name, 0, 1)], tag=f"dbg_{name}")


def dram_l2f():
    return {
        "l2f_gw": ((128, 64), F32, "ExternalInput"),
        "l2f_gb": ((1, 8), F32, "ExternalInput"),
        "l2f_w1": ((8, 1024, 1408), F32, "ExternalInput"),
        "l2f_w3": ((8, 1024, 1408), F32, "ExternalInput"),
        "l2f_w2": ((8, 1408, 1024), F32, "ExternalInput"),
        "l2f_b1": ((128, 88), F32, "ExternalInput"),
        "l2f_b3": ((128, 88), F32, "ExternalInput"),
        "l2f_b2": ((1, 8192), F32, "ExternalInput"),
        "l2f_sw1": ((1024, 1408), F32, "ExternalInput"),
        "l2f_sw3": ((1024, 1408), F32, "ExternalInput"),
        "l2f_sw2": ((1408, 1024), F32, "ExternalInput"),
        "l2f_sb1": ((128, 11), F32, "ExternalInput"),
        "l2f_sb3": ((128, 11), F32, "ExternalInput"),
        "l2f_sb2": ((128, 8), F32, "ExternalInput"),
        "l2f_w1b": ((8, 1024, 1408), BF16, "Internal"),
        "l2f_w3b": ((8, 1024, 1408), BF16, "Internal"),
        "l2f_w2b": ((8, 1408, 1024), BF16, "Internal"),
        "l2f_sw1b": ((1024, 1408), BF16, "Internal"),
        "l2f_sw3b": ((1024, 1408), BF16, "Internal"),
        "l2f_sw2b": ((1408, 1024), BF16, "Internal"),
    }


def host_l2f(inp):
    def col128(v, n):
        return np.ascontiguousarray(np.asarray(v, np.float32).reshape(n, 128).T)
    b1 = np.asarray(inp["L2_ffn_b1"], np.float32).reshape(8, 11, 128).transpose(2, 0, 1).reshape(128, 88)
    b3 = np.asarray(inp["L2_ffn_b3"], np.float32).reshape(8, 11, 128).transpose(2, 0, 1).reshape(128, 88)
    return {
        "l2f_gw": np.ascontiguousarray(np.asarray(inp["L2_ffn_gate_w"], np.float32).T.reshape(8, 128, 8).transpose(1, 0, 2).reshape(128, 64)),
        "l2f_gb": np.asarray(inp["L2_ffn_gate_b"], np.float32).reshape(1, 8),
        "l2f_w1": np.ascontiguousarray(np.asarray(inp["L2_ffn_w1"], np.float32)),
        "l2f_w3": np.ascontiguousarray(np.asarray(inp["L2_ffn_w3"], np.float32)),
        "l2f_w2": np.ascontiguousarray(np.asarray(inp["L2_ffn_w2"], np.float32)),
        "l2f_b1": np.ascontiguousarray(b1),
        "l2f_b3": np.ascontiguousarray(b3),
        "l2f_b2": np.asarray(inp["L2_ffn_b2"], np.float32).reshape(1, 8192),
        "l2f_sw1": np.ascontiguousarray(np.asarray(inp["L2_ffn_sw1"], np.float32)),
        "l2f_sw3": np.ascontiguousarray(np.asarray(inp["L2_ffn_sw3"], np.float32)),
        "l2f_sw2": np.ascontiguousarray(np.asarray(inp["L2_ffn_sw2"], np.float32)),
        "l2f_sb1": col128(inp["L2_ffn_sb1"], 11),
        "l2f_sb3": col128(inp["L2_ffn_sb3"], 11),
        "l2f_sb2": col128(inp["L2_ffn_sb2"], 8),
    }


def emit_l2f(b, env):
    A, d, PS = env.arena, env.dram, env.PS
    m0 = A.mark()
    for nm in ("w1", "w3", "w2"):
        emit_cast_dram(b, env, f"l2f_{nm}", f"l2f_{nm}b", nslabs=2)
        emit_cast_dram(b, env, f"l2f_s{nm}", f"l2f_s{nm}b")
    XN = A.alloc("l2f_xn", 16384 * 2)
    emit_ln(b, env, XN, tag="l2f_ln")
    gws = env.ST.alloc("l2f_gws", 512)
    b.dma("gpsimd", lambda e: e.dma_start(out=gws.ap(BF16, 64), in_=d["l2f_gw"][:, :]),
          rd=[("l2f_gw", 0, 1)], wr=[gws.key()], tag="l2f_ldgw")
    gbr = env.ST.alloc("l2f_gbr", 512)
    b.dma("sync", lambda e: e.dma_start(out=gbr.ap(F32, 8, pn=1), in_=d["l2f_gb"][:, :]),
          rd=[("l2f_gb", 0, 1)], wr=[gbr.key(F32, 8)], tag="l2f_ldgb")
    b.op("tensor", lambda e: e.matmul(PS[1].ap(F32, 8), ones_row_f32(env), gbr.ap(F32, 8, pn=1), start=True, stop=True),
         rd=[onesrow_key(env), gbr.key(F32, 8)], wr=[PS[1].key()], tag="l2f_gbbc")
    gbbc = env.ST.alloc("l2f_gbbc", 512)
    b.op("scalar", lambda e: e.activation(out=gbbc.ap(F32, 8), in_=PS[1].ap(F32, 8), func=AF.Copy, bias=0.0, scale=1.0),
         rd=[PS[1].key()], wr=[gbbc.key()], tag="l2f_gbbce")
    b2r = A.alloc("l2f_b2r", 16384)
    b.dma("gpsimd", lambda e: e.dma_start(out=b2r.ap(BF16, 8192, pn=1), in_=d["l2f_b2"][:, :]),
          rd=[("l2f_b2", 0, 1)], wr=[b2r.key()], tag="l2f_ldb2")
    bc1 = env.ST.alloc("l2f_bc1", 512)
    b.dma("sync", lambda e: e.dma_start(out=bc1.ap(F32, 88), in_=d["l2f_b1"][:, :]),
          rd=[("l2f_b1", 0, 1)], wr=[bc1.key()], tag="l2f_ldb1")
    bc3 = env.ST.alloc("l2f_bc3", 512)
    b.dma("sync", lambda e: e.dma_start(out=bc3.ap(F32, 88), in_=d["l2f_b3"][:, :]),
          rd=[("l2f_b3", 0, 1)], wr=[bc3.key()], tag="l2f_ldb3")
    sb1 = env.ST.alloc("l2f_sb1", 512)
    b.dma("sync", lambda e: e.dma_start(out=sb1.ap(F32, 11), in_=d["l2f_sb1"][:, :]),
          rd=[("l2f_sb1", 0, 1)], wr=[sb1.key()], tag="l2f_ldsb1")
    sb3 = env.ST.alloc("l2f_sb3", 512)
    b.dma("sync", lambda e: e.dma_start(out=sb3.ap(F32, 11), in_=d["l2f_sb3"][:, :]),
          rd=[("l2f_sb3", 0, 1)], wr=[sb3.key()], tag="l2f_ldsb3")
    sb2c = env.ST.alloc("l2f_sb2", 512)
    b.dma("sync", lambda e: e.dma_start(out=sb2c.ap(F32, 8), in_=d["l2f_sb2"][:, :]),
          rd=[("l2f_sb2", 0, 1)], wr=[sb2c.key()], tag="l2f_ldsb2")
    wrow = A.alloc("l2f_wrow", 8 * 2048 * 2)
    wsc = A.alloc("l2f_wsc", 4096)
    for tt in range(16):
        for kt in range(8):
            lhs = XN.ap(BF16, 128, coff=kt * T + tt * 128)
            lk = XN.key(BF16, 128, coff=kt * T + tt * 128)
            rhs = gws.ap(BF16, 8, coff=kt * 8)
            b.op("tensor", (lambda lhs=lhs, rhs=rhs, kt=kt: lambda e: e.matmul(PS[0].ap(F32, 8), lhs, rhs, start=(kt == 0), stop=(kt == 7)))(),
                 rd=[lk, gws.key()], wr=[PS[0].key()], tag=f"l2f_rt{tt}_{kt}")
        lg = wsc.ap(F32, 8); lgk = wsc.key(F32, 8)
        b.op("vector", (lambda lg=lg: lambda e: e.tensor_copy(out=lg, in_=PS[0].ap(F32, 8)))(),
             rd=[PS[0].key()], wr=[lgk], tag=f"l2f_lg{tt}")
        mcol = wsc.ap(F32, 1, coff=8); mk = wsc.key(F32, 1, coff=8)
        b.op("vector", (lambda lg=lg, mcol=mcol: lambda e: e.tensor_reduce(out=mcol, in_=lg, axis=AX.X, op=ALU.max))(),
             rd=[lgk], wr=[mk], tag=f"l2f_m{tt}")
        nm = wsc.ap(F32, 1, coff=9); nmk = wsc.key(F32, 1, coff=9)
        b.op("vector", (lambda mcol=mcol, nm=nm: lambda e: e.tensor_scalar(out=nm, in0=mcol, scalar1=-1.0, scalar2=None, op0=ALU.mult))(),
             rd=[mk], wr=[nmk], tag=f"l2f_nm{tt}")
        b.op("scalar", (lambda lg=lg, nm=nm: lambda e: e.activation(out=wsc.ap(F32, 8, coff=16), in_=lg, func=AF.Exp, bias=nm, scale=1.0))(),
             rd=[lgk, nmk], wr=[wsc.key(F32, 8, coff=16)], tag=f"l2f_exp{tt}")
        scol = wsc.ap(F32, 1, coff=10); sk = wsc.key(F32, 1, coff=10)
        b.op("vector", (lambda scol=scol: lambda e: e.tensor_scalar(out=wsc.ap(F32, 8, coff=56), in0=wsc.ap(F32, 8, coff=16), scalar1=0.0, scalar2=None, op0=ALU.add, op1=ALU.add, accum_out=scol))(),
             rd=[wsc.key(F32, 8, coff=16)], wr=[wsc.key(F32, 8, coff=56), sk], tag=f"l2f_sum{tt}")
        rcol = wsc.ap(F32, 1, coff=11); rk2 = wsc.key(F32, 1, coff=11)
        b.op("vector", (lambda scol=scol, rcol=rcol: lambda e: e.reciprocal(out=rcol, in_=scol))(),
             rd=[sk], wr=[rk2], tag=f"l2f_rc{tt}")
        pr = wsc.ap(F32, 8, coff=16); prk = wsc.key(F32, 8, coff=16)
        b.op("vector", (lambda pr=pr, rcol=rcol: lambda e: e.tensor_scalar(out=pr, in0=pr, scalar1=rcol, scalar2=None, op0=ALU.mult))(),
             rd=[prk, rk2], wr=[prk], tag=f"l2f_pr{tt}")
        bi = wsc.ap(F32, 8, coff=24); bik = wsc.key(F32, 8, coff=24)
        b.op("vector", (lambda pr=pr, bi=bi: lambda e: e.tensor_tensor(out=bi, in0=pr, in1=gbbc.ap(F32, 8), op=ALU.add))(),
             rd=[prk, gbbc.key()], wr=[bik], tag=f"l2f_bi{tt}")
        m1 = wsc.ap(F32, 1, coff=12); m1k = wsc.key(F32, 1, coff=12)
        b.op("vector", (lambda bi=bi, m1=m1: lambda e: e.tensor_reduce(out=m1, in_=bi, axis=AX.X, op=ALU.max))(),
             rd=[bik], wr=[m1k], tag=f"l2f_m1{tt}")
        oh1 = wsc.ap(F32, 8, coff=32); oh1k = wsc.key(F32, 8, coff=32)
        b.op("vector", (lambda bi=bi, oh1=oh1, m1=m1: lambda e: e.tensor_scalar(out=oh1, in0=bi, scalar1=m1, scalar2=None, op0=ALU.is_ge))(),
             rd=[bik, m1k], wr=[oh1k], tag=f"l2f_oh1{tt}")
        b2_ = wsc.ap(F32, 8, coff=40); b2k = wsc.key(F32, 8, coff=40)
        b.op("vector", (lambda bi=bi, b2_=b2_, m1=m1: lambda e: e.tensor_scalar(out=b2_, in0=bi, scalar1=m1, scalar2=None, op0=ALU.is_lt))(),
             rd=[bik, m1k], wr=[b2k], tag=f"l2f_blt{tt}")
        b.op("vector", (lambda bi=bi, b2_=b2_: lambda e: e.tensor_tensor(out=b2_, in0=b2_, in1=bi, op=ALU.mult))(),
             rd=[b2k, bik], wr=[b2k], tag=f"l2f_b2m{tt}")
        m2 = wsc.ap(F32, 1, coff=13); m2k = wsc.key(F32, 1, coff=13)
        b.op("vector", (lambda b2_=b2_, m2=m2: lambda e: e.tensor_reduce(out=m2, in_=b2_, axis=AX.X, op=ALU.max))(),
             rd=[b2k], wr=[m2k], tag=f"l2f_m2{tt}")
        oh2 = wsc.ap(F32, 8, coff=48); oh2k = wsc.key(F32, 8, coff=48)
        b.op("vector", (lambda b2_=b2_, oh2=oh2, m2=m2: lambda e: e.tensor_scalar(out=oh2, in0=b2_, scalar1=m2, scalar2=None, op0=ALU.is_ge))(),
             rd=[b2k, m2k], wr=[oh2k], tag=f"l2f_oh2{tt}")
        b.op("vector", (lambda oh1=oh1, oh2=oh2: lambda e: e.tensor_tensor(out=oh1, in0=oh1, in1=oh2, op=ALU.add))(),
             rd=[oh1k, oh2k], wr=[oh1k], tag=f"l2f_ohs{tt}")
        wv = wsc.ap(BF16, 8, coff=96); wvk = wsc.key(BF16, 8, coff=96)
        b.op("vector", (lambda oh1=oh1, pr=pr, wv=wv: lambda e: e.tensor_tensor(out=wv, in0=oh1, in1=pr, op=ALU.mult))(),
             rd=[oh1k, prk], wr=[wvk], tag=f"l2f_wf{tt}")
        for e_ in range(8):
            pt = PS[6 + e_ % 2]
            b.op("tensor", (lambda e_=e_, pt=pt: lambda e: e.transpose(pt.ap(BF16, 128, pn=1), wsc.ap(BF16, 1, coff=96 + e_), ident_bf(env)))(),
                 rd=[wvk, env.ident.key()], wr=[pt.key()], tag=f"l2f_wt{tt}_{e_}")
            dap = wrow.ap(BF16, 128, coff=e_ * 2048 + tt * 128, pn=1)
            dk = (wrow.name, wrow.off + (e_ * 2048 + tt * 128) * 2, wrow.off + (e_ * 2048 + (tt + 1) * 128) * 2)
            b.op("scalar", (lambda dap=dap, pt=pt: lambda e: e.activation(out=dap, in_=pt.ap(BF16, 128, pn=1), func=AF.Copy, bias=0.0, scale=1.0))(),
                 rd=[pt.key()], wr=[dk], tag=f"l2f_wte{tt}_{e_}")
    uu = A.alloc("l2f_u", 11 * 512 * 2)
    vv = A.alloc("l2f_v", 11 * 512 * 2)
    acc = A.alloc("l2f_acc", 8 * 512 * 4)
    wbc = A.alloc("l2f_wbc", 512 * 4)
    ws = [A.alloc(f"l2f_ws{i}", 4096, top=True) for i in range(2)]

    def upgemm(name, wd, wdk, rhsreg, bias_reg, bias_off, dst, tc, scale_wbc):
        def wsrc(kt, mg0, mgn):
            ap = d[wd][kt * 128:(kt + 1) * 128, mg0 * 128:(mg0 + mgn) * 128] if wdk is None else d[wd][wdk, kt * 128:(kt + 1) * 128, mg0 * 128:(mg0 + mgn) * 128]
            return ap, (wd, 0 if wdk is None else wdk, 1 if wdk is None else wdk + 1)

        def rhs(kt):
            return rhsreg.ap(BF16, 512, coff=kt * T + tc * 512), rhsreg.key(BF16, 512, coff=kt * T + tc * 512)

        def ev(mt, ps):
            b.op("scalar", (lambda mt=mt, ps=ps: lambda e: e.activation(out=dst.ap(BF16, 512, coff=mt * 512), in_=ps.ap(F32, 512), func=AF.Identity, bias=bias_reg.ap(F32, 1, coff=bias_off + mt), scale=1.0))(),
                 rd=[ps.key(), bias_reg.key()], wr=[dst.key(BF16, 512, coff=mt * 512)], tag=f"{name}_ev{mt}")

        g_gemm(b, env, name=name, KT=8, MT=11, Wsrc=wsrc, rhs=rhs, evac=ev, wslots=ws, banks=(0, 1, 2, 3))

    def dngemm(name, wd, wdk, hreg, tc, use_w, e_):
        mg0 = 0
        banks = (0, 1, 2, 3)
        di = 0
        while mg0 < 8:
            mgn = min(4, 8 - mg0)
            for kt in range(11):
                sl = ws[di % 2]
                di += 1
                src = d[wd][kt * 128:(kt + 1) * 128, mg0 * 128:(mg0 + mgn) * 128] if wdk is None else d[wd][wdk, kt * 128:(kt + 1) * 128, mg0 * 128:(mg0 + mgn) * 128]
                srck = (wd, 0 if wdk is None else wdk, 1 if wdk is None else wdk + 1)
                nb = mgn * 128 * 2
                b.dma("gpsimd", (lambda src=src, sl=sl, nb=nb: lambda e: e.dma_start(out=sl.ap(BF16, nb // 2), in_=src))(),
                      rd=[srck], wr=[sl.key(BF16, nb // 2)], tag=f"{name}_dw{kt}_{mg0}")
                rap = hreg.ap(BF16, 512, coff=kt * 512)
                rk = hreg.key(BF16, 512, coff=kt * 512)
                for j in range(mgn):
                    bk = banks[j]
                    b.op("tensor", (lambda sl=sl, j=j, rap=rap, bk=bk, kt=kt: lambda e: e.matmul(PS[bk].ap(F32, 512), sl.ap(BF16, 128, coff=j * 128), rap, start=(kt == 0), stop=False))(),
                         rd=[sl.key(BF16, 128, coff=j * 128), rk], wr=[PS[bk].key()], tag=f"{name}_mm{kt}_{mg0 + j}")
            for j in range(mgn):
                mt = mg0 + j
                ps = PS[banks[j]]
                if use_w:
                    b.op("tensor", (lambda mt=mt, ps=ps, e_=e_, tc=tc: lambda e: e.matmul(ps.ap(F32, 512), b2r.ap(BF16, 128, coff=e_ * 1024 + mt * 128, pn=1), wrow.ap(BF16, 512, coff=e_ * 2048 + tc * 512, pn=1), start=False, stop=True))(),
                         rd=[b2r.key(), (wrow.name, wrow.off, wrow.off + 16384)], wr=[ps.key()], tag=f"{name}_b2{mt}")
                    b.op("vector", (lambda mt=mt, ps=ps: lambda e: e.tensor_tensor(out=acc.ap(F32, 512, coff=mt * 512), in0=ps.ap(F32, 512), in1=acc.ap(F32, 512, coff=mt * 512), op=ALU.add))(),
                         rd=[ps.key(), acc.key(F32, 512, coff=mt * 512)], wr=[acc.key(F32, 512, coff=mt * 512)], tag=f"{name}_ac{mt}")
                else:
                    b.op("tensor", (lambda mt=mt, ps=ps: lambda e: e.matmul(ps.ap(F32, 512), b2r.ap(BF16, 128, coff=e_ * 1024 + mt * 128, pn=1), wrow.ap(BF16, 512, coff=e_ * 2048 + tc * 512, pn=1), start=False, stop=True))(),
                         rd=[b2r.key(), (wrow.name, wrow.off, wrow.off + 16384)], wr=[ps.key()], tag=f"{name}_b2x{mt}")
            mg0 += mgn

    for tc in range(4):
        upgemm(f"l2f_su1_{tc}", "l2f_sw1b", None, XN, sb1, 0, uu, tc, False)
        upgemm(f"l2f_su3_{tc}", "l2f_sw3b", None, XN, sb3, 0, vv, tc, False)
        for mt in range(11):
            b.op("vector", (lambda mt=mt: lambda e: e.tensor_tensor(out=uu.ap(BF16, 512, coff=mt * 512), in0=uu.ap(BF16, 512, coff=mt * 512), in1=vv.ap(BF16, 512, coff=mt * 512), op=ALU.mult))(),
                 rd=[uu.key(BF16, 512, coff=mt * 512), vv.key(BF16, 512, coff=mt * 512)], wr=[uu.key(BF16, 512, coff=mt * 512)], tag=f"l2f_sm{tc}_{mt}")
            b.op("scalar", (lambda mt=mt: lambda e: e.activation(out=uu.ap(BF16, 512, coff=mt * 512), in_=uu.ap(BF16, 512, coff=mt * 512), func=AF.Silu, bias=0.0, scale=1.0))(),
                 rd=[uu.key(BF16, 512, coff=mt * 512)], wr=[uu.key(BF16, 512, coff=mt * 512)], tag=f"l2f_ss{tc}_{mt}")
        mg0 = 0
        banks = (0, 1, 2, 3)
        di = 0
        while mg0 < 8:
            mgn = min(4, 8 - mg0)
            for kt in range(11):
                sl = ws[di % 2]
                di += 1
                src = d["l2f_sw2b"][kt * 128:(kt + 1) * 128, mg0 * 128:(mg0 + mgn) * 128]
                nb = mgn * 128 * 2
                b.dma("gpsimd", (lambda src=src, sl=sl, nb=nb: lambda e: e.dma_start(out=sl.ap(BF16, nb // 2), in_=src))(),
                      rd=[("l2f_sw2b", 0, 1)], wr=[sl.key(BF16, nb // 2)], tag=f"l2f_sd{tc}_{kt}_{mg0}")
                rap = uu.ap(BF16, 512, coff=kt * 512)
                rk = uu.key(BF16, 512, coff=kt * 512)
                for j in range(mgn):
                    bk = banks[j]
                    b.op("tensor", (lambda sl=sl, j=j, rap=rap, bk=bk, kt=kt: lambda e: e.matmul(PS[bk].ap(F32, 512), sl.ap(BF16, 128, coff=j * 128), rap, start=(kt == 0), stop=(kt == 10)))(),
                         rd=[sl.key(BF16, 128, coff=j * 128), rk], wr=[PS[bk].key()], tag=f"l2f_sdm{tc}_{kt}_{mg0 + j}")
            for j in range(mgn):
                mt = mg0 + j
                ps = PS[banks[j]]
                b.op("scalar", (lambda mt=mt, ps=ps: lambda e: e.activation(out=acc.ap(F32, 512, coff=mt * 512), in_=ps.ap(F32, 512), func=AF.Identity, bias=sb2c.ap(F32, 1, coff=mt), scale=1.0))(),
                     rd=[ps.key(), sb2c.key()], wr=[acc.key(F32, 512, coff=mt * 512)], tag=f"l2f_sac{tc}_{mt}")
            mg0 += mgn
        for e_ in range(8):
            b.op("tensor", (lambda e_=e_, tc=tc: lambda e: e.matmul(PS[5].ap(F32, 512), ones_row_bf(env), wrow.ap(BF16, 512, coff=e_ * 2048 + tc * 512, pn=1), start=True, stop=True))(),
                 rd=[onesrowbf_key(env), (wrow.name, wrow.off, wrow.off + 16384)], wr=[PS[5].key()], tag=f"l2f_wbc{tc}_{e_}")
            b.op("scalar", (lambda: lambda e: e.activation(out=wbc.ap(F32, 512), in_=PS[5].ap(F32, 512), func=AF.Copy, bias=0.0, scale=1.0))(),
                 rd=[PS[5].key()], wr=[wbc.key()], tag=f"l2f_wbce{tc}_{e_}")
            upgemm(f"l2f_u1_{tc}_{e_}", "l2f_w1b", e_, XN, bc1, e_ * 11, uu, tc, True)
            upgemm(f"l2f_u3_{tc}_{e_}", "l2f_w3b", e_, XN, bc3, e_ * 11, vv, tc, True)
            for mt in range(11):
                b.op("vector", (lambda mt=mt: lambda e: e.tensor_tensor(out=uu.ap(BF16, 512, coff=mt * 512), in0=uu.ap(BF16, 512, coff=mt * 512), in1=vv.ap(BF16, 512, coff=mt * 512), op=ALU.mult))(),
                     rd=[uu.key(BF16, 512, coff=mt * 512), vv.key(BF16, 512, coff=mt * 512)], wr=[uu.key(BF16, 512, coff=mt * 512)], tag=f"l2f_hm{tc}_{e_}_{mt}")
                b.op("scalar", (lambda mt=mt: lambda e: e.activation(out=uu.ap(BF16, 512, coff=mt * 512), in_=uu.ap(BF16, 512, coff=mt * 512), func=AF.Silu, bias=0.0, scale=1.0))(),
                     rd=[uu.key(BF16, 512, coff=mt * 512)], wr=[uu.key(BF16, 512, coff=mt * 512)], tag=f"l2f_hs{tc}_{e_}_{mt}")
                b.op("vector", (lambda mt=mt: lambda e: e.tensor_tensor(out=uu.ap(BF16, 512, coff=mt * 512), in0=uu.ap(BF16, 512, coff=mt * 512), in1=wbc.ap(F32, 512), op=ALU.mult))(),
                     rd=[uu.key(BF16, 512, coff=mt * 512), wbc.key()], wr=[uu.key(BF16, 512, coff=mt * 512)], tag=f"l2f_hw{tc}_{e_}_{mt}")
            dngemm(f"l2f_dn{tc}_{e_}", "l2f_w2b", e_, uu, tc, True, e_)
        for mt in range(8):
            xap = env.XR.ap(F32, 512, coff=mt * T + tc * 512)
            xk = env.XR.key(F32, 512, coff=mt * T + tc * 512)
            b.op("vector", (lambda xap=xap, mt=mt: lambda e: e.tensor_tensor(out=xap, in0=acc.ap(F32, 512, coff=mt * 512), in1=xap, op=ALU.add))(),
                 rd=[acc.key(F32, 512, coff=mt * 512), xk], wr=[xk], tag=f"l2f_res{tc}_{mt}")
    A.reset(m0)


BLOCKS["l2f"] = (emit_l2f, dram_l2f, host_l2f)


def dram_l3a():
    return {
        "l3a_wqkv": ((1024, 3072), F32, "ExternalInput"),
        "l3a_bqkv": ((128, 24), F32, "ExternalInput"),
        "l3a_wo": ((1024, 1024), F32, "ExternalInput"),
        "l3a_bo": ((128, 8), F32, "ExternalInput"),
        "l3a_wiq": ((1024, 256), F32, "ExternalInput"),
        "l3a_biq": ((128, 2), F32, "ExternalInput"),
        "l3a_wik": ((1024, 64), F32, "ExternalInput"),
        "l3a_bik": ((128, 1), F32, "ExternalInput"),
        "l3a_wiw": ((1024, 4), F32, "ExternalInput"),
        "l3a_biw": ((1, 4), F32, "ExternalInput"),
        "l3a_bvrow": ((1, 1024), F32, "ExternalInput"),
        "l3a_spill": ((2, 128, 16384), BF16, "Internal"),
        "l3a_cos": ((512, 2048), F32, "ExternalInput"),
        "l3a_sin": ((512, 2048), F32, "ExternalInput"),
    }


def host_l3a(inp):
    half = 512
    inv = 1.0 / 10000.0 ** (np.arange(half, dtype=np.float32) / half)
    ang = np.arange(T, dtype=np.float32)[:, None] * inv[None, :]
    cos = np.ascontiguousarray(np.cos(ang).astype(np.float32).T)
    sin = np.ascontiguousarray(np.sin(ang).astype(np.float32).T)
    def colp(v, n):
        return np.ascontiguousarray(np.asarray(v, np.float32).reshape(n, 128).T)
    bik = np.zeros((128, 1), np.float32)
    bik[:64, 0] = np.asarray(inp["L3_attn_b_ik"], np.float32)
    bik[64:, 0] = np.asarray(inp["L3_attn_b_ik"], np.float32)
    return {
        "l3a_wqkv": np.ascontiguousarray(np.asarray(inp["L3_attn_w_qkv"], np.float32).T),
        "l3a_bqkv": colp(inp["L3_attn_b_qkv"], 24),
        "l3a_wo": np.ascontiguousarray(np.asarray(inp["L3_attn_w_o"], np.float32).T),
        "l3a_bo": colp(inp["L3_attn_b_o"], 8),
        "l3a_wiq": np.ascontiguousarray(np.asarray(inp["L3_attn_w_iq"], np.float32).T),
        "l3a_biq": colp(inp["L3_attn_b_iq"], 2),
        "l3a_wik": np.ascontiguousarray(np.asarray(inp["L3_attn_w_ik"], np.float32).T),
        "l3a_bik": bik,
        "l3a_wiw": np.ascontiguousarray(np.asarray(inp["L3_attn_w_iw"], np.float32).T),
        "l3a_biw": np.asarray(inp["L3_attn_b_iw"], np.float32).reshape(1, 4),
        "l3a_bvrow": np.asarray(inp["L3_attn_b_qkv"], np.float32)[2048:].reshape(1, 1024),
        "l3a_cos": cos,
        "l3a_sin": sin,
    }


def emit_l3a(b, env):
    A, d, PS = env.arena, env.dram, env.PS
    m0 = A.mark()
    NIT = 26
    XN = A.alloc("l3a_xn", 16384 * 2)
    emit_ln(b, env, XN, tag="l3a_ln")
    bcols = env.ST.alloc("l3a_bc", 512)
    b.dma("sync", lambda e: e.dma_start(out=bcols.ap(F32, 24), in_=d["l3a_bqkv"][:, :]),
          rd=[("l3a_bqkv", 0, 1)], wr=[bcols.key(F32, 24)], tag="l3a_ldbqkv")
    b.dma("sync", lambda e: e.dma_start(out=bcols.ap(F32, 8, coff=24), in_=d["l3a_bo"][:, :]),
          rd=[("l3a_bo", 0, 1)], wr=[bcols.key(F32, 8, coff=24)], tag="l3a_ldbo")
    b.dma("sync", lambda e: e.dma_start(out=bcols.ap(F32, 2, coff=32), in_=d["l3a_biq"][:, :]),
          rd=[("l3a_biq", 0, 1)], wr=[bcols.key(F32, 2, coff=32)], tag="l3a_ldbiq")
    b.dma("sync", lambda e: e.dma_start(out=bcols.ap(F32, 1, coff=34), in_=d["l3a_bik"][:, :]),
          rd=[("l3a_bik", 0, 1)], wr=[bcols.key(F32, 1, coff=34)], tag="l3a_ldbik")
    biwr = env.ST.alloc("l3a_biw", 512)
    b.dma("gpsimd", lambda e: e.dma_start(out=biwr.ap(BF16, 4, pn=1), in_=d["l3a_biw"][:, :]),
          rd=[("l3a_biw", 0, 1)], wr=[biwr.key()], tag="l3a_ldbiw")
    slot = A.alloc("l3a_slot", 16384, top=True)
    qfm = A.alloc("l3a_qfm", 16384 * 2)
    kfm = A.alloc("l3a_kfm", 16384 * 2)
    iqf = A.alloc("l3a_iqf", 2 * 2048 * 2)
    ikf = A.alloc("l3a_ikf", 2048 * 2)
    iwt = A.alloc("l3a_iwt", 64 * 4)

    ws = None
    def fm_gemm2(dst, col0, bias_coff, tag, token_major=False, MT=8, addones=False):
        M = MT * 128
        for kt in range(8):
            b.dma("gpsimd", (lambda kt=kt, col0=col0, M=M: lambda e: e.dma_start(out=slot.ap(BF16, M, coff=kt * M), in_=d["l3a_wqkv"][kt * 128:(kt + 1) * 128, col0:col0 + M]))(),
                  rd=[("l3a_wqkv", 0, 1)], wr=[slot.key(BF16, M, coff=kt * M)], tag=f"{tag}_w{kt}")
        if not token_major:
            for mt in range(MT):
                for ch in range(4):
                    bk = (mt * 4 + ch) % 4
                    for kt in range(8):
                        lhs = slot.ap(BF16, 128, coff=kt * M + mt * 128)
                        lk = slot.key(BF16, 128, coff=kt * M + mt * 128)
                        rhs = XN.ap(BF16, 512, coff=kt * T + ch * 512)
                        rk = XN.key(BF16, 512, coff=kt * T + ch * 512)
                        b.op("tensor", (lambda lhs=lhs, rhs=rhs, bk=bk, kt=kt: lambda e: e.matmul(PS[bk].ap(F32, 512), lhs, rhs, start=(kt == 0), stop=(kt == 7)))(),
                             rd=[lk, rk], wr=[PS[bk].key()], tag=f"{tag}_mm{mt}_{ch}_{kt}")
                    b.op("scalar", (lambda mt=mt, ch=ch, bk=bk, bias_coff=bias_coff: lambda e: e.activation(out=dst.ap(BF16, 512, coff=mt * T + ch * 512), in_=PS[bk].ap(F32, 512), func=AF.Identity, bias=bcols.ap(F32, 1, coff=bias_coff + mt), scale=1.0))(),
                         rd=[PS[bk].key(), bcols.key()], wr=[dst.key(BF16, 512, coff=mt * T + ch * 512)], tag=f"{tag}_ev{mt}_{ch}")

    fm_gemm2(qfm, 0, 0, "l3a_q")
    fm_gemm2(kfm, 1024, 8, "l3a_k")
    vbuf = A.alloc("l3a_vbuf", 4096, top=True)
    bvrow = A.alloc("l3a_bvrow", 2048, top=True)
    b.dma("gpsimd", lambda e: e.dma_start(out=bvrow.ap(BF16, 1024, pn=1), in_=d["l3a_bvrow"][:, :]),
          rd=[("l3a_bvrow", 0, 1)], wr=[bvrow.key()], tag="l3a_ldbv")
    for kt in range(8):
        b.dma("gpsimd", (lambda kt=kt: lambda e: e.dma_start(out=slot.ap(BF16, 1024, coff=kt * 1024), in_=d["l3a_wqkv"][kt * 128:(kt + 1) * 128, 2048:3072]))(),
              rd=[("l3a_wqkv", 0, 1)], wr=[slot.key(BF16, 1024, coff=kt * 1024)], tag=f"l3a_vw{kt}")
    for tt in range(16):
        for nc0 in (0, 512):
            bk = (tt * 2 + nc0 // 512) % 4
            for kt in range(8):
                lhs = XN.ap(BF16, 128, coff=kt * T + tt * 128)
                lk = XN.key(BF16, 128, coff=kt * T + tt * 128)
                rhs = slot.ap(BF16, 512, coff=kt * 1024 + nc0)
                rk = slot.key(BF16, 512, coff=kt * 1024 + nc0)
                b.op("tensor", (lambda lhs=lhs, rhs=rhs, bk=bk, kt=kt: lambda e: e.matmul(PS[bk].ap(F32, 512), lhs, rhs, start=(kt == 0), stop=False))(),
                     rd=[lk, rk], wr=[PS[bk].key()], tag=f"l3a_v{tt}_{nc0}_{kt}")
            b.op("tensor", (lambda bk=bk, nc0=nc0: lambda e: e.matmul(PS[bk].ap(F32, 512), ones_row_bf(env), bvrow.ap(BF16, 512, coff=nc0, pn=1), start=False, stop=True))(),
                 rd=[onesrowbf_key(env), bvrow.key()], wr=[PS[bk].key()], tag=f"l3a_vb{tt}_{nc0}")
            vb = (tt % 2) * 1024
            b.op("scalar", (lambda vb=vb, nc0=nc0, bk=bk: lambda e: e.activation(out=vbuf.ap(BF16, 512, coff=vb + nc0), in_=PS[bk].ap(F32, 512), func=AF.Copy, bias=0.0, scale=1.0))(),
                 rd=[PS[bk].key()], wr=[vbuf.key(BF16, 512, coff=vb + nc0)], tag=f"l3a_vev{tt}_{nc0}")
        vb = (tt % 2) * 1024
        b.dma("sync", (lambda tt=tt, vb=vb: lambda e: e.dma_start(out=d["l3a_spill"][1, :, tt * 1024:(tt + 1) * 1024], in_=vbuf.ap(BF16, 1024, coff=vb)))(),
              rd=[vbuf.key(BF16, 1024, coff=vb)], wr=[("l3a_spill", 1, 2)], tag=f"l3a_vsp{tt}")
    for kt in range(8):
        b.dma("gpsimd", (lambda kt=kt: lambda e: e.dma_start(out=slot.ap(BF16, 256, coff=kt * 256), in_=d["l3a_wiq"][kt * 128:(kt + 1) * 128, :]))(),
              rd=[("l3a_wiq", 0, 1)], wr=[slot.key(BF16, 256, coff=kt * 256)], tag=f"l3a_iqw{kt}")
        b.dma("gpsimd", (lambda kt=kt: lambda e: e.dma_start(out=slot.ap(BF16, 64, coff=2048 + kt * 64), in_=d["l3a_wik"][kt * 128:(kt + 1) * 128, :]))(),
              rd=[("l3a_wik", 0, 1)], wr=[slot.key(BF16, 64, coff=2048 + kt * 64)], tag=f"l3a_ikw{kt}")
        b.dma("gpsimd", (lambda kt=kt: lambda e: e.dma_start(out=slot.ap(BF16, 4, coff=2560 + kt * 4), in_=d["l3a_wiw"][kt * 128:(kt + 1) * 128, :]))(),
              rd=[("l3a_wiw", 0, 1)], wr=[slot.key(BF16, 4, coff=2560 + kt * 4)], tag=f"l3a_iww{kt}")
    for mt in range(2):
        for ch in range(4):
            bk = (mt * 4 + ch) % 4
            for kt in range(8):
                lhs = slot.ap(BF16, 128, coff=kt * 256 + mt * 128)
                lk = slot.key(BF16, 256, coff=kt * 256)
                rhs = XN.ap(BF16, 512, coff=kt * T + ch * 512)
                rk = XN.key(BF16, 512, coff=kt * T + ch * 512)
                b.op("tensor", (lambda lhs=lhs, rhs=rhs, bk=bk, kt=kt: lambda e: e.matmul(PS[bk].ap(F32, 512), lhs, rhs, start=(kt == 0), stop=(kt == 7)))(),
                     rd=[lk, rk], wr=[PS[bk].key()], tag=f"l3a_iq{mt}_{ch}_{kt}")
            b.op("scalar", (lambda mt=mt, ch=ch, bk=bk: lambda e: e.activation(out=iqf.ap(BF16, 512, coff=mt * T + ch * 512), in_=PS[bk].ap(F32, 512), func=AF.Identity, bias=bcols.ap(F32, 1, coff=32 + mt), scale=1.0))(),
                 rd=[PS[bk].key(), bcols.key()], wr=[iqf.key(BF16, 512, coff=mt * T + ch * 512)], tag=f"l3a_iqev{mt}_{ch}")
    for ch in range(4):
        bk = ch % 4
        for kt in range(8):
            lhs = slot.ap(BF16, 64, coff=2048 + kt * 64)
            lk = slot.key(BF16, 64, coff=2048 + kt * 64)
            rhs = XN.ap(BF16, 512, coff=kt * T + ch * 512)
            rk = XN.key(BF16, 512, coff=kt * T + ch * 512)
            b.op("tensor", (lambda lhs=lhs, rhs=rhs, bk=bk, kt=kt: lambda e: e.matmul(PS[bk].ap(F32, 512, pn=64), lhs, rhs, start=(kt == 0), stop=(kt == 7)))(),
                 rd=[lk, rk], wr=[PS[bk].key()], tag=f"l3a_ik{ch}_{kt}")
        for half in range(2):
            b.op("scalar", (lambda ch=ch, bk=bk, half=half: lambda e: e.activation(out=ikf.ap(BF16, 512, coff=ch * 512, p0=half * 64, pn=64), in_=PS[bk].ap(F32, 512, pn=64), func=AF.Identity, bias=bcols.ap(F32, 1, coff=34, p0=half * 64, pn=64), scale=1.0))(),
                 rd=[PS[bk].key(), bcols.key()], wr=[ikf.key(BF16, 512, coff=ch * 512)], tag=f"l3a_ikev{ch}_{half}")
    for tt in range(16):
        bk = tt % 4
        for kt in range(8):
            lhs = XN.ap(BF16, 128, coff=kt * T + tt * 128)
            lk = XN.key(BF16, 128, coff=kt * T + tt * 128)
            rhs = slot.ap(BF16, 4, coff=2560 + kt * 4)
            rk = slot.key(BF16, 4, coff=2560 + kt * 4)
            b.op("tensor", (lambda lhs=lhs, rhs=rhs, bk=bk, kt=kt: lambda e: e.matmul(PS[bk].ap(F32, 4), lhs, rhs, start=(kt == 0), stop=False))(),
                 rd=[lk, rk], wr=[PS[bk].key()], tag=f"l3a_iw{tt}_{kt}")
        b.op("tensor", (lambda bk=bk: lambda e: e.matmul(PS[bk].ap(F32, 4), ones_row_bf(env), biwr.ap(BF16, 4, pn=1), start=False, stop=True))(),
             rd=[onesrowbf_key(env), biwr.key()], wr=[PS[bk].key()], tag=f"l3a_iwb{tt}")
        b.op("vector", (lambda tt=tt, bk=bk: lambda e: e.tensor_copy(out=iwt.ap(F32, 4, coff=tt * 4), in_=PS[bk].ap(F32, 4)))(),
             rd=[PS[bk].key()], wr=[iwt.key(F32, 4, coff=tt * 4)], tag=f"l3a_iwev{tt}")
    A.free("l3a_slot", "l3a_vbuf", "l3a_bvrow")
    ctile = A.alloc("l3a_ct", 16384, top=True)
    rsc = A.alloc("l3a_rsc", 8192, top=True)
    for ft in range(4):
        b.dma("sync", (lambda ft=ft: lambda e: e.dma_start(out=ctile.ap(F32, T), in_=d["l3a_cos"][ft * 128:(ft + 1) * 128, :]))(),
              rd=[("l3a_cos", 0, 1)], wr=[(ctile.name, ctile.off, ctile.off + 8192)], tag=f"l3a_ldc{ft}")
        b.dma("sync", (lambda ft=ft: lambda e: e.dma_start(out=ctile.ap(F32, T, coff=T), in_=d["l3a_sin"][ft * 128:(ft + 1) * 128, :]))(),
              rd=[("l3a_sin", 0, 1)], wr=[(ctile.name, ctile.off + 8192, ctile.off + 16384)], tag=f"l3a_lds{ft}")
        for reg in (qfm, kfm):
            x1 = reg.ap(BF16, T, coff=ft * T)
            x2 = reg.ap(BF16, T, coff=(ft + 4) * T)
            k1 = reg.key(BF16, T, coff=ft * T)
            k2 = reg.key(BF16, T, coff=(ft + 4) * T)
            ca = ctile.ap(F32, T)
            sa = ctile.ap(F32, T, coff=T)
            ck = (ctile.name, ctile.off, ctile.off + 8192)
            sk2 = (ctile.name, ctile.off + 8192, ctile.off + 16384)
            t1 = rsc.ap(BF16, T)
            t2 = rsc.ap(BF16, T, coff=T)
            tk1 = rsc.key(BF16, T)
            tk2 = rsc.key(BF16, T, coff=T)
            nm = f"l3a_r{ft}_{reg.off}"
            b.op("vector", (lambda t1=t1, x1=x1, ca=ca: lambda e: e.tensor_tensor(out=t1, in0=x1, in1=ca, op=ALU.mult))(), rd=[k1, ck], wr=[tk1], tag=nm + "a")
            b.op("vector", (lambda t2=t2, x2=x2, sa=sa: lambda e: e.tensor_tensor(out=t2, in0=x2, in1=sa, op=ALU.mult))(), rd=[k2, sk2], wr=[tk2], tag=nm + "b")
            b.op("vector", (lambda t1=t1, x1=x1, sa=sa: lambda e: e.tensor_tensor(out=x1, in0=x1, in1=sa, op=ALU.mult))(), rd=[k1, sk2], wr=[k1], tag=nm + "c")
            b.op("vector", (lambda t1=t1, x2=x2, ca=ca: lambda e: e.tensor_tensor(out=x2, in0=x2, in1=ca, op=ALU.mult))(), rd=[k2, ck], wr=[k2], tag=nm + "d")
            b.op("vector", (lambda x2=x2, x1=x1: lambda e: e.tensor_tensor(out=x2, in0=x1, in1=x2, op=ALU.add))(), rd=[k1, k2], wr=[k2], tag=nm + "e")
            b.op("vector", (lambda t1=t1, t2=t2, x1=x1: lambda e: e.tensor_tensor(out=x1, in0=t1, in1=t2, op=ALU.subtract))(), rd=[tk1, tk2], wr=[k1], tag=nm + "f")
    b.dma("sync", lambda e: e.dma_start(out=d["l3a_spill"][0], in_=qfm.ap(BF16, 16384)),
          rd=[qfm.key()], wr=[("l3a_spill", 0, 1)], tag="l3a_qsp")
    A.free("l3a_ct", "l3a_rsc", "l3a_qfm")
    qtl = A.alloc("l3a_qtl", 2 * 2048)
    vtl = A.alloc("l3a_vtl", 2 * 2048)
    isc = A.alloc("l3a_isc", 8192)
    ssb = A.alloc("l3a_ssb", 8192)
    psb = A.alloc("l3a_psb", 4096)
    ngb = A.alloc("l3a_ngb", 1024)
    ptg = A.alloc("l3a_ptg", 16 * 512 * 2)
    bis = env.ST.alloc("l3a_bis", 512)
    b.op("vector", lambda e: e.memset(ngb.ap(BF16, 512), -1e38), rd=[], wr=[ngb.key()], tag="l3a_ngset")
    ofm = XN
    for qg in range(4):
        for qi in range(4):
            qt = qg * 4 + qi
            for h in range(4):
                hr = (h % 2) * 64
                ht = h // 2
                for ch in range(4):
                    bk = ch % 2
                    lhs = iqf.ap(BF16, 128, coff=ht * T + qt * 128, p0=hr, pn=64)
                    lk = iqf.key(BF16, 128, coff=ht * T + qt * 128)
                    rhs = ikf.ap(BF16, 512, coff=ch * 512, p0=hr, pn=64)
                    rk = ikf.key(BF16, 512, coff=ch * 512)
                    b.op("tensor", (lambda lhs=lhs, rhs=rhs, bk=bk, hr=hr: lambda e: e.matmul(PS[bk].ap(F32, 512), lhs, rhs, start=True, stop=True, tile_position=(hr, 0)))(),
                         rd=[lk, rk], wr=[PS[bk].key()], tag=f"l3a_hs{qt}_{h}_{ch}")
                    b.op("scalar", (lambda bk=bk, ch=ch: lambda e: e.activation(out=psb.ap(BF16, 512, coff=ch * 512), in_=PS[bk].ap(F32, 512), func=AF.Relu, bias=0.0, scale=1.0))(),
                         rd=[PS[bk].key()], wr=[psb.key(BF16, 512, coff=ch * 512)], tag=f"l3a_hsr{qt}_{h}_{ch}")
                    iwcol = iwt.ap(F32, 1, coff=qt * 4 + h)
                    iwk = iwt.key(F32, 1, coff=qt * 4 + h)
                    if h == 0:
                        b.op("vector", (lambda ch=ch, iwcol=iwcol: lambda e: e.tensor_scalar(out=isc.ap(F32, 512, coff=ch * 512), in0=psb.ap(BF16, 512, coff=ch * 512), scalar1=iwcol, scalar2=None, op0=ALU.mult))(),
                             rd=[psb.key(BF16, 512, coff=ch * 512), iwk], wr=[isc.key(F32, 512, coff=ch * 512)], tag=f"l3a_is{qt}_{h}_{ch}")
                    else:
                        b.op("vector", (lambda ch=ch, iwcol=iwcol: lambda e: e.scalar_tensor_tensor(out=isc.ap(F32, 512, coff=ch * 512), in0=psb.ap(BF16, 512, coff=ch * 512), scalar=iwcol, in1=isc.ap(F32, 512, coff=ch * 512), op0=ALU.mult, op1=ALU.add))(),
                             rd=[psb.key(BF16, 512, coff=ch * 512), iwk, isc.key(F32, 512, coff=ch * 512)], wr=[isc.key(F32, 512, coff=ch * 512)], tag=f"l3a_is{qt}_{h}_{ch}")
            lo = bis.ap(F32, 1, coff=0); lok = bis.key(F32, 1, coff=0)
            hi = bis.ap(F32, 1, coff=1); hik = bis.key(F32, 1, coff=1)
            mid = bis.ap(F32, 1, coff=2); midk = bis.key(F32, 1, coff=2)
            cnt = bis.ap(F32, 1, coff=3); cntk = bis.key(F32, 1, coff=3)
            ge = bis.ap(U8, 1, coff=16); gek = bis.key(F32, 1, coff=4)
            b.op("vector", lambda e: e.tensor_reduce(out=hi, in_=isc.ap(F32, T), axis=AX.X, op=ALU.max),
                 rd=[isc.key()], wr=[hik], tag=f"l3a_bh{qt}")
            b.op("vector", lambda e: e.tensor_reduce(out=lo, in_=isc.ap(F32, T), axis=AX.X, op=ALU.min),
                 rd=[isc.key()], wr=[lok], tag=f"l3a_bl{qt}")
            b.op("vector", lambda e: e.tensor_scalar(out=lo, in0=lo, scalar1=1.0, scalar2=None, op0=ALU.subtract),
                 rd=[lok], wr=[lok], tag=f"l3a_bl2{qt}")
            for it in range(NIT):
                b.op("vector", (lambda: lambda e: e.tensor_tensor(out=mid, in0=lo, in1=hi, op=ALU.add))(),
                     rd=[lok, hik], wr=[midk], tag=f"l3a_bm{qt}_{it}")
                b.op("vector", (lambda: lambda e: e.tensor_scalar(out=mid, in0=mid, scalar1=0.5, scalar2=None, op0=ALU.mult))(),
                     rd=[midk], wr=[midk], tag=f"l3a_bm2{qt}_{it}")
                b.op("vector", (lambda: lambda e: e.tensor_scalar(out=psb.ap(BF16, T), in0=isc.ap(F32, T), scalar1=mid, scalar2=None, op0=ALU.is_gt, op1=ALU.add, accum_out=cnt))(),
                     rd=[isc.key(), midk], wr=[psb.key(), cntk], tag=f"l3a_bc{qt}_{it}")
                b.op("vector", (lambda: lambda e: e.tensor_scalar(out=ge, in0=cnt, scalar1=256.0, scalar2=None, op0=ALU.is_ge))(),
                     rd=[cntk], wr=[gek], tag=f"l3a_bg{qt}_{it}")
                b.op("vector", (lambda: lambda e: e.select(out=lo, mask=ge, on_true=mid, on_false=lo))(),
                     rd=[gek, midk, lok], wr=[lok], tag=f"l3a_bs1{qt}_{it}")
                b.op("vector", (lambda: lambda e: e.select(out=hi, mask=ge, on_true=hi, on_false=mid))(),
                     rd=[gek, midk, hik], wr=[hik], tag=f"l3a_bs2{qt}_{it}")
            qsl = (qt % 2) * 1024
            for kt in range(8):
                b.dma("sync", (lambda kt=kt, qt=qt, qsl=qsl: lambda e: e.dma_start(out=qtl.ap(BF16, 128, coff=qsl + kt * 128), in_=d["l3a_spill"][0, :, kt * T + qt * 128: kt * T + (qt + 1) * 128]))(),
                      rd=[("l3a_spill", 0, 1)], wr=[qtl.key(BF16, 128, coff=qsl + kt * 128)], tag=f"l3a_qld{qt}_{kt}")
            for ch in range(4):
                bk = ch % 2
                for kt in range(8):
                    lhs = qtl.ap(BF16, 128, coff=qsl + kt * 128)
                    lk = qtl.key(BF16, 128, coff=qsl + kt * 128)
                    rhs = kfm.ap(BF16, 512, coff=kt * T + ch * 512)
                    rk = kfm.key(BF16, 512, coff=kt * T + ch * 512)
                    b.op("tensor", (lambda lhs=lhs, rhs=rhs, bk=bk, kt=kt: lambda e: e.matmul(PS[4 + bk].ap(F32, 512), lhs, rhs, start=(kt == 0), stop=(kt == 7)))(),
                         rd=[lk, rk], wr=[PS[4 + bk].key()], tag=f"l3a_qk{qt}_{ch}_{kt}")
                b.op("scalar", (lambda ch=ch, bk=bk: lambda e: e.activation(out=ssb.ap(F32, 512, coff=ch * 512), in_=PS[4 + bk].ap(F32, 512), func=AF.Copy, bias=0.0, scale=1.0 / 32.0))(),
                     rd=[PS[4 + bk].key()], wr=[ssb.key(F32, 512, coff=ch * 512)], tag=f"l3a_qke{qt}_{ch}")
            b.op("vector", (lambda: lambda e: e.tensor_scalar(out=psb.ap(U8, T), in0=isc.ap(F32, T), scalar1=lo, scalar2=None, op0=ALU.is_gt))(),
                 rd=[isc.key(), lok], wr=[psb.key()], tag=f"l3a_msk{qt}")
            for ch in range(4):
                b.op("vector", (lambda ch=ch: lambda e: e.select(out=ssb.ap(F32, 512, coff=ch * 512), mask=psb.ap(U8, 512, coff=ch * 512), on_true=ssb.ap(F32, 512, coff=ch * 512), on_false=ngb.ap(BF16, 512)))(),
                     rd=[psb.key(F32, 512, coff=ch * 512), ssb.key(F32, 512, coff=ch * 512), ngb.key()], wr=[ssb.key(F32, 512, coff=ch * 512)], tag=f"l3a_sel{qt}_{ch}")
            rm = bis.ap(F32, 1, coff=8); rmk = bis.key(F32, 1, coff=8)
            b.op("vector", lambda e: e.tensor_reduce(out=rm, in_=ssb.ap(F32, T), axis=AX.X, op=ALU.max),
                 rd=[ssb.key()], wr=[rmk], tag=f"l3a_rm{qt}")
            b.op("vector", (lambda: lambda e: e.tensor_scalar(out=rm, in0=rm, scalar1=-1.0, scalar2=None, op0=ALU.mult))(),
                 rd=[rmk], wr=[rmk], tag=f"l3a_rmn{qt}")
            b.op("scalar", (lambda: lambda e: e.activation(out=psb.ap(BF16, T), in_=ssb.ap(F32, T), func=AF.Exp, bias=rm, scale=1.0))(),
                 rd=[ssb.key(), rmk], wr=[psb.key()], tag=f"l3a_exp{qt}")
            sm = bis.ap(F32, 1, coff=9); smk = bis.key(F32, 1, coff=9)
            b.op("vector", (lambda: lambda e: e.tensor_scalar(out=ssb.ap(F32, T), in0=psb.ap(BF16, T), scalar1=0.0, scalar2=None, op0=ALU.add, op1=ALU.add, accum_out=sm))(),
                 rd=[psb.key()], wr=[ssb.key(), smk], tag=f"l3a_sm{qt}")
            b.op("vector", (lambda: lambda e: e.reciprocal(out=sm, in_=sm))(),
                 rd=[smk], wr=[smk], tag=f"l3a_si{qt}")
            b.op("vector", (lambda: lambda e: e.tensor_scalar(out=psb.ap(BF16, T), in0=psb.ap(BF16, T), scalar1=sm, scalar2=None, op0=ALU.mult))(),
                 rd=[psb.key(), smk], wr=[psb.key()], tag=f"l3a_pn{qt}")
            for kt in range(16):
                pt = PS[6 + kt % 2]
                b.op("tensor", (lambda kt=kt, pt=pt: lambda e: e.transpose(pt.ap(BF16, 128), psb.ap(BF16, 128, coff=kt * 128), ident_bf(env)))(),
                     rd=[psb.key(), env.ident.key()], wr=[pt.key()], tag=f"l3a_pt{qt}_{kt}")
                b.op("scalar", (lambda kt=kt, pt=pt, qi=qi: lambda e: e.activation(out=ptg.ap(BF16, 128, coff=kt * 512 + qi * 128), in_=pt.ap(BF16, 128), func=AF.Copy, bias=0.0, scale=1.0))(),
                     rd=[pt.key()], wr=[ptg.key(BF16, 128, coff=kt * 512 + qi * 128)], tag=f"l3a_pte{qt}_{kt}")
        for half in range(2):
            for kt16 in range(16):
                vsl = (kt16 % 2) * 1024
                b.dma("sync", (lambda kt16=kt16, vsl=vsl: lambda e: e.dma_start(out=vtl.ap(BF16, 1024, coff=vsl), in_=d["l3a_spill"][1, :, kt16 * 1024:(kt16 + 1) * 1024]))(),
                      rd=[("l3a_spill", 1, 2)], wr=[vtl.key(BF16, 1024, coff=vsl)], tag=f"l3a_vld{qg}_{half}_{kt16}")
                for mb in range(half * 4, half * 4 + 4):
                    bk = mb % 4
                    lhs = vtl.ap(BF16, 128, coff=vsl + mb * 128)
                    lk = vtl.key(BF16, 128, coff=vsl + mb * 128)
                    kt = kt16
                    rhs = ptg.ap(BF16, 512, coff=kt * 512)
                    rk = ptg.key(BF16, 512, coff=kt * 512)
                    b.op("tensor", (lambda lhs=lhs, rhs=rhs, bk=bk, kt=kt: lambda e: e.matmul(PS[bk].ap(F32, 512), lhs, rhs, start=(kt == 0), stop=(kt == 15)))(),
                         rd=[lk, rk], wr=[PS[bk].key()], tag=f"l3a_pv{qg}_{half}_{mb}_{kt}")
            for mb in range(half * 4, half * 4 + 4):
                bk = mb % 4
                b.op("scalar", (lambda mb=mb, bk=bk, qg=qg: lambda e: e.activation(out=ofm.ap(BF16, 512, coff=mb * T + qg * 512), in_=PS[bk].ap(F32, 512), func=AF.Copy, bias=0.0, scale=1.0))(),
                     rd=[PS[bk].key()], wr=[ofm.key(BF16, 512, coff=mb * T + qg * 512)], tag=f"l3a_oev{qg}_{mb}")
    A.free("l3a_isc", "l3a_ssb", "l3a_psb", "l3a_ngb", "l3a_ptg")
    slot2 = A.alloc("l3a_slot2", 16384)
    emit_wload(b, env, slot2, "l3a_wo", 1024, 1024)
    for mt in range(8):
        for ch in range(4):
            bk = (mt * 4 + ch) % 4
            for kt in range(8):
                lhs = slot2.ap(BF16, 128, coff=kt * 1024 + mt * 128)
                lk = slot2.key(BF16, 128, coff=kt * 1024 + mt * 128)
                rhs = ofm.ap(BF16, 512, coff=kt * T + ch * 512)
                rk = ofm.key(BF16, 512, coff=kt * T + ch * 512)
                b.op("tensor", (lambda lhs=lhs, rhs=rhs, bk=bk, kt=kt: lambda e: e.matmul(PS[bk].ap(F32, 512), lhs, rhs, start=(kt == 0), stop=(kt == 7)))(),
                     rd=[lk, rk], wr=[PS[bk].key()], tag=f"l3a_wo{mt}_{ch}_{kt}")
            xap = env.XR.ap(F32, 512, coff=mt * T + ch * 512)
            xk = env.XR.key(F32, 512, coff=mt * T + ch * 512)
            b.op("vector", (lambda xap=xap, bk=bk, mt=mt: lambda e: e.scalar_tensor_tensor(out=xap, in0=PS[bk].ap(F32, 512), scalar=bcols.ap(F32, 1, coff=24 + mt), in1=xap, op0=ALU.add, op1=ALU.add))(),
                 rd=[PS[bk].key(), bcols.key(), xk], wr=[xk], tag=f"l3a_res{mt}_{ch}")
    A.reset(m0)


BLOCKS["l3a"] = (emit_l3a, dram_l3a, host_l3a)


def dram_l1f():
    return {
        "l1f_wqkv": ((2, 1024, 1536), F32, "ExternalInput"),
        "l1f_wo": ((2, 512, 1024), F32, "ExternalInput"),
        "l1f_w1": ((2, 1024, 5460), F32, "ExternalInput"),
        "l1f_w2": ((2, 2730, 1024), F32, "ExternalInput"),
        "l1f_wqkvb": ((2, 1024, 1536), BF16, "Internal"),
        "l1f_wob": ((2, 512, 1024), BF16, "Internal"),
        "l1f_w1b": ((2, 1024, 5460), BF16, "Internal"),
        "l1f_w2b": ((2, 2730, 1024), BF16, "Internal"),
        "l1f_qs": ((2, 1, 64), F32, "ExternalInput"),
        "l1f_ks": ((2, 1, 64), F32, "ExternalInput"),
        "l1f_lg": ((128, 16), F32, "ExternalInput"),
        "l1f_lb": ((128, 16), F32, "ExternalInput"),
        "l1f_b1": ((128, 88), F32, "ExternalInput"),
        "l1f_b2": ((128, 16), F32, "ExternalInput"),
        "l1f_csq": ((128, 1024), F32, "ExternalInput"),
        "l1f_ssq": ((128, 1024), F32, "ExternalInput"),
        "l1f_csk": ((128, 1024), F32, "ExternalInput"),
        "l1f_ssk": ((128, 1024), F32, "ExternalInput"),
        "l1f_mask": ((128, 640), F32, "ExternalInput"),
    }


def host_l1f(inp):
    DH, WINDOW = 64, 512
    half_idx = np.arange(0, DH, 2, dtype=np.float32)
    inv_freq = 1.0 / 10000.0 ** (half_idx / DH)
    tt_ = np.arange(T, dtype=np.float32)
    freqs = tt_[:, None] * inv_freq[None, :]
    freqs = np.concatenate([freqs, freqs], axis=-1)
    base = (half_idx + 0.4 * DH) / (1.4 * DH)
    power = (tt_ - T // 2) / float(WINDOW // 2)
    scale = base[None, :] ** power[:, None]
    scale = np.concatenate([scale, scale], axis=-1)
    cos = np.cos(freqs); sin = np.sin(freqs)
    def tokmaj(a):
        return np.ascontiguousarray(a.astype(np.float32).reshape(16, 128, 64).transpose(1, 0, 2).reshape(128, 1024))
    ii = np.arange(128)[:, None]
    jj = np.arange(640)[None, :]
    mask = np.where((jj >= ii) & (jj <= ii + 512), 0.0, -1e37).astype(np.float32)
    b1raw = np.asarray(inp["L1_ffn_b1"], np.float32)
    b1 = np.zeros((2, 2, 22, 128), np.float32)
    for l in range(2):
        for part in range(2):
            h = b1raw[l, part * 2730:(part + 1) * 2730]
            b1[l, part].reshape(-1)[:2730] = h
    b1 = b1.reshape(2, 44, 128)
    return {
        "l1f_wqkv": np.ascontiguousarray(np.asarray(inp["L1_ffn_wqkv"], np.float32)),
        "l1f_wo": np.ascontiguousarray(np.asarray(inp["L1_ffn_wo"], np.float32)),
        "l1f_w1": np.ascontiguousarray(np.asarray(inp["L1_ffn_w1"], np.float32)),
        "l1f_w2": np.ascontiguousarray(np.asarray(inp["L1_ffn_w2"], np.float32)),
        "l1f_qs": np.asarray(inp["L1_ffn_q_scale"], np.float32).reshape(2, 1, 64),
        "l1f_ks": np.asarray(inp["L1_ffn_k_scale"], np.float32).reshape(2, 1, 64),
        "l1f_lg": np.ascontiguousarray(np.asarray(inp["L1_ffn_ln_g"], np.float32).reshape(2, 8, 128).transpose(2, 0, 1).reshape(128, 16)),
        "l1f_lb": np.ascontiguousarray(np.asarray(inp["L1_ffn_ln_b"], np.float32).reshape(2, 8, 128).transpose(2, 0, 1).reshape(128, 16)),
        "l1f_b1": np.ascontiguousarray(b1.transpose(2, 0, 1).reshape(128, 88)),
        "l1f_b2": np.ascontiguousarray(np.asarray(inp["L1_ffn_b2"], np.float32).reshape(2, 8, 128).transpose(2, 0, 1).reshape(128, 16)),
        "l1f_csq": tokmaj(cos * scale),
        "l1f_ssq": tokmaj(sin * scale),
        "l1f_csk": tokmaj(cos / scale),
        "l1f_ssk": tokmaj(sin / scale),
        "l1f_mask": np.ascontiguousarray(mask),
    }


def emit_l1f(b, env):
    A, d, PS = env.arena, env.dram, env.PS
    m0 = A.mark()
    for nm in ("wqkv", "wo", "w1", "w2"):
        emit_cast_dram(b, env, f"l1f_{nm}", f"l1f_{nm}b", nslabs=2)
    XI = A.alloc("l1f_xi", 16384 * 2)
    emit_ln(b, env, XI, tag="l1f_ln0")
    cols = env.ST.alloc("l1f_cols", 1024)
    b.dma("sync", lambda e: e.dma_start(out=cols.ap(F32, 16), in_=d["l1f_lg"][:, :]),
          rd=[("l1f_lg", 0, 1)], wr=[cols.key(F32, 16)], tag="l1f_ldlg")
    b.dma("sync", lambda e: e.dma_start(out=cols.ap(F32, 16, coff=16), in_=d["l1f_lb"][:, :]),
          rd=[("l1f_lb", 0, 1)], wr=[cols.key(F32, 16, coff=16)], tag="l1f_ldlb")
    b.dma("sync", lambda e: e.dma_start(out=cols.ap(F32, 16, coff=32), in_=d["l1f_b2"][:, :]),
          rd=[("l1f_b2", 0, 1)], wr=[cols.key(F32, 16, coff=32)], tag="l1f_ldb2")
    b1c = env.ST.alloc("l1f_b1c", 512)
    b.dma("sync", lambda e: e.dma_start(out=b1c.ap(F32, 88), in_=d["l1f_b1"][:, :]),
          rd=[("l1f_b1", 0, 1)], wr=[b1c.key()], tag="l1f_ldb1")
    qsc = env.ST.alloc("l1f_qsc", 1024)


    def emit_layer(l):
        xptab = A.alloc(f"l1f_xp{l}", 16384, top=True)
        for i, nm in enumerate(("csq", "ssq", "csk", "ssk")):
            b.dma("sync", (lambda i=i, nm=nm, xptab=xptab: lambda e: e.dma_start(out=xptab.ap(F32, 1024, coff=i * 1024), in_=d[f"l1f_{nm}"][:, :]))(),
                  rd=[(f"l1f_{nm}", 0, 1)], wr=[(xptab.name, xptab.off + i * 4096, xptab.off + (i + 1) * 4096)], tag=f"l1f_ldxp{l}_{i}")
        mskt = A.alloc(f"l1f_mask{l}", 2560, top=True)
        b.dma("sync", (lambda mskt=mskt: lambda e: e.dma_start(out=mskt.ap(F32, 640), in_=d["l1f_mask"][:, :]))(),
              rd=[("l1f_mask", 0, 1)], wr=[mskt.key()], tag=f"l1f_ldmask{l}")
        b.dma("sync", (lambda l=l: lambda e: e.dma_start(out=qsc.ap(F32, 64), in_=d["l1f_qs"][l].to_broadcast([128, 64])))(),
              rd=[("l1f_qs", 0, 1)], wr=[qsc.key(F32, 64)], tag=f"l1f_qs{l}")
        b.dma("sync", (lambda l=l: lambda e: e.dma_start(out=qsc.ap(F32, 64, coff=64), in_=d["l1f_ks"][l].to_broadcast([128, 64])))(),
              rd=[("l1f_ks", 0, 1)], wr=[qsc.key(F32, 64, coff=64)], tag=f"l1f_ks{l}")
        HSH0 = A.alloc(f"l1f_h0_{l}", 2048)
        for f in range(8):
            if f < 4:
                b.op("vector", (lambda f=f: lambda e: e.tensor_copy(out=HSH0.ap(BF16, 128, coff=f * 128), in_=XI.ap(BF16, 128, coff=f * T)))(),
                     rd=[XI.key(BF16, 128, coff=f * T)], wr=[HSH0.key(BF16, 128, coff=f * 128)], tag=f"l1f_h0c{l}_{f}")
            else:
                b.op("vector", (lambda f=f: lambda e: e.tensor_copy(out=HSH0.ap(BF16, 127, coff=f * 128 + 1), in_=XI.ap(BF16, 127, coff=f * T)))(),
                     rd=[XI.key(BF16, 128, coff=f * T)], wr=[HSH0.key(BF16, 128, coff=f * 128)], tag=f"l1f_h0c{l}_{f}")
                b.op("vector", (lambda f=f: lambda e: e.memset(HSH0.ap(BF16, 1, coff=f * 128), 0.0))(),
                     rd=[], wr=[HSH0.key(BF16, 1, coff=f * 128)], tag=f"l1f_h0z{l}_{f}")

        def shlhs(kt, tt):
            if tt == 0:
                return HSH0.ap(BF16, 128, coff=kt * 128), HSH0.key(BF16, 128, coff=kt * 128)
            off = kt * T + tt * 128 - (1 if kt >= 4 else 0)
            return XI.ap(BF16, 128, coff=off), XI.key(BF16, 129, coff=kt * T + tt * 128 - 1)

        slot = A.alloc(f"l1f_slot{l}", 24576, top=True)
        for kt in range(8):
            b.dma("gpsimd", (lambda kt=kt, l=l: lambda e: e.dma_start(out=slot.ap(BF16, 1536, coff=kt * 1536), in_=d["l1f_wqkvb"][l, kt * 128:(kt + 1) * 128, :]))(),
                  rd=[("l1f_wqkvb", l, l + 1)], wr=[slot.key(BF16, 1536, coff=kt * 1536)], tag=f"l1f_qkvw{l}_{kt}")
        qt_ = A.alloc(f"l1f_qt{l}", 2 * 1024)
        kt_ = A.alloc(f"l1f_kt{l}", 2 * 1024)
        vt_ = A.alloc(f"l1f_vt{l}", 8 * 2048)
        qfm = A.alloc(f"l1f_qfm{l}", 4 * 2048 * 2)
        kfm = A.alloc(f"l1f_kfm{l}", 4 * 2048 * 2)
        nsc = A.alloc(f"l1f_nsc{l}", 512, top=True)
        rsc = A.alloc(f"l1f_rsc{l}", 2048, top=True)

        def headnorm_xpos(src_, soff, dst_, ci, si, sci, tt, l=l):
            for h in range(8):
                xap = src_.ap(BF16, 64, coff=soff + h * 64)
                xk = src_.key(BF16, 64, coff=soff + h * 64)
                b.op("vector", (lambda xap=xap, h=h: lambda e: e.scalar_tensor_tensor(out=nsc.ap(F32, 64, coff=16), in0=xap, scalar=1.0, in1=xap, op0=ALU.mult, op1=ALU.mult, accum_out=nsc.ap(F32, 1, coff=h)))(),
                     rd=[xk], wr=[nsc.key(F32, 64, coff=16), nsc.key(F32, 8)], tag=f"l1f_n{l}_{sci}_{tt}_{h}")
            b.op("scalar", (lambda: lambda e: e.activation(out=nsc.ap(F32, 8, coff=8), in_=nsc.ap(F32, 8), func=AF.Sqrt, bias=0.0, scale=1.0))(),
                 rd=[nsc.key(F32, 8)], wr=[nsc.key(F32, 8, coff=8)], tag=f"l1f_ns{l}_{sci}_{tt}")
            b.op("vector", (lambda: lambda e: e.tensor_scalar(out=nsc.ap(F32, 8, coff=8), in0=nsc.ap(F32, 8, coff=8), scalar1=1e-12, scalar2=None, op0=ALU.max))(),
                 rd=[nsc.key(F32, 8, coff=8)], wr=[nsc.key(F32, 8, coff=8)], tag=f"l1f_nm{l}_{sci}_{tt}")
            b.op("vector", (lambda: lambda e: e.reciprocal(out=nsc.ap(F32, 8, coff=8), in_=nsc.ap(F32, 8, coff=8)))(),
                 rd=[nsc.key(F32, 8, coff=8)], wr=[nsc.key(F32, 8, coff=8)], tag=f"l1f_nr{l}_{sci}_{tt}")
            base3 = src_.ap(BF16, 512, coff=soff).rearrange("p (h dd) -> p h dd", h=8)
            xnv = rsc.ap(BF16, 512).rearrange("p (h dd) -> p h dd", h=8)
            b.op("vector", (lambda base3=base3, xnv=xnv, sci=sci: lambda e: e.tensor_tensor(out=xnv, in0=base3, in1=qsc.ap(F32, 64, coff=sci).unsqueeze(1).to_broadcast([128, 8, 64]), op=ALU.mult))(),
                 rd=[src_.key(BF16, 512, coff=soff), qsc.key()], wr=[rsc.key(BF16, 512)], tag=f"l1f_xs{l}_{sci}_{tt}")
            for h in range(8):
                b.op("vector", (lambda h=h: lambda e: e.tensor_scalar(out=rsc.ap(BF16, 64, coff=h * 64), in0=rsc.ap(BF16, 64, coff=h * 64), scalar1=nsc.ap(F32, 1, coff=8 + h), scalar2=None, op0=ALU.mult))(),
                     rd=[rsc.key(BF16, 512), nsc.key(F32, 8, coff=8)], wr=[rsc.key(BF16, 512)], tag=f"l1f_xr{l}_{sci}_{tt}_{h}")
            cs = xptab.ap(F32, 64, coff=ci * 1024 + tt * 64).unsqueeze(1).to_broadcast([128, 8, 64])
            ss = xptab.ap(F32, 64, coff=si * 1024 + tt * 64).unsqueeze(1).to_broadcast([128, 8, 64])
            ck = (xptab.name, xptab.off + ci * 4096, xptab.off + (ci + 1) * 4096)
            sk = (xptab.name, xptab.off + si * 4096, xptab.off + (si + 1) * 4096)
            t2v = rsc.ap(BF16, 512, coff=512).rearrange("p (h dd) -> p h dd", h=8)
            b.op("vector", (lambda xnv=xnv, t2v=t2v, ss=ss: lambda e: e.tensor_tensor(out=t2v, in0=xnv, in1=ss, op=ALU.mult))(),
                 rd=[rsc.key(BF16, 512), sk], wr=[rsc.key(BF16, 512, coff=512)], tag=f"l1f_x1{l}_{sci}_{tt}")
            b.op("vector", (lambda xnv=xnv, cs=cs: lambda e: e.tensor_tensor(out=xnv, in0=xnv, in1=cs, op=ALU.mult))(),
                 rd=[rsc.key(BF16, 512), ck], wr=[rsc.key(BF16, 512)], tag=f"l1f_x2{l}_{sci}_{tt}")
            lo1 = rsc.ap(BF16, 512).rearrange("p (h two dd) -> p h two dd", h=8, two=2)
            hi2 = rsc.ap(BF16, 512, coff=512).rearrange("p (h two dd) -> p h two dd", h=8, two=2)
            b.op("vector", (lambda lo1=lo1, hi2=hi2: lambda e: e.tensor_tensor(out=lo1[:, :, 0, :], in0=lo1[:, :, 0, :], in1=hi2[:, :, 1, :], op=ALU.subtract))(),
                 rd=[rsc.key(BF16, 512), rsc.key(BF16, 512, coff=512)], wr=[rsc.key(BF16, 512)], tag=f"l1f_x3{l}_{sci}_{tt}")
            b.op("vector", (lambda lo1=lo1, hi2=hi2: lambda e: e.tensor_tensor(out=lo1[:, :, 1, :], in0=lo1[:, :, 1, :], in1=hi2[:, :, 0, :], op=ALU.add))(),
                 rd=[rsc.key(BF16, 512), rsc.key(BF16, 512, coff=512)], wr=[rsc.key(BF16, 512)], tag=f"l1f_x4{l}_{sci}_{tt}")
            for cb in range(4):
                pt = PS[6 + cb % 2]
                b.op("tensor", (lambda cb=cb, pt=pt: lambda e: e.transpose(pt.ap(BF16, 128), rsc.ap(BF16, 128, coff=cb * 128), ident_bf(env)))(),
                     rd=[rsc.key(BF16, 512), env.ident.key()], wr=[pt.key()], tag=f"l1f_t{l}_{sci}_{tt}_{cb}")
                b.op("scalar", (lambda cb=cb, pt=pt, tt=tt, dst_=dst_: lambda e: e.activation(out=dst_.ap(BF16, 128, coff=cb * 2048 + tt * 128), in_=pt.ap(BF16, 128), func=AF.Copy, bias=0.0, scale=1.0))(),
                     rd=[pt.key()], wr=[dst_.key(BF16, 128, coff=cb * 2048 + tt * 128)], tag=f"l1f_te{l}_{sci}_{tt}_{cb}")

        for tt in range(16):
            for nc0 in (0, 512, 1024):
                bk = (tt * 3 + nc0 // 512) % 4
                for kt in range(8):
                    lhs, lk = shlhs(kt, tt)
                    rhs = slot.ap(BF16, 512, coff=kt * 1536 + nc0)
                    rk = slot.key(BF16, 512, coff=kt * 1536 + nc0)
                    b.op("tensor", (lambda lhs=lhs, rhs=rhs, bk=bk, kt=kt: lambda e: e.matmul(PS[bk].ap(F32, 512), lhs, rhs, start=(kt == 0), stop=(kt == 7)))(),
                         rd=[lk, rk], wr=[PS[bk].key()], tag=f"l1f_qkv{l}_{tt}_{nc0}_{kt}")
                if nc0 == 1024:
                    b.op("scalar", (lambda tt=tt, bk=bk: lambda e: e.activation(out=vt_.ap(BF16, 512, coff=tt * 512), in_=PS[bk].ap(F32, 512), func=AF.Copy, bias=0.0, scale=1.0))(),
                         rd=[PS[bk].key()], wr=[vt_.key(BF16, 512, coff=tt * 512)], tag=f"l1f_qkve{l}_{tt}_{nc0}")
                else:
                    dst = qt_ if nc0 == 0 else kt_
                    soff = (tt % 2) * 512
                    b.op("scalar", (lambda dst=dst, soff=soff, bk=bk: lambda e: e.activation(out=dst.ap(BF16, 512, coff=soff), in_=PS[bk].ap(F32, 512), func=AF.Copy, bias=0.0, scale=1.0))(),
                         rd=[PS[bk].key()], wr=[dst.key(BF16, 512, coff=soff)], tag=f"l1f_qkve{l}_{tt}_{nc0}")
            headnorm_xpos(qt_, (tt % 2) * 512, qfm, 0, 1, 0, tt)
            headnorm_xpos(kt_, (tt % 2) * 512, kfm, 2, 3, 64, tt)
        A.free(f"l1f_slot{l}", f"l1f_h0_{l}", f"l1f_qt{l}", f"l1f_kt{l}")
        ofm = A.alloc(f"l1f_ofm{l}", 4 * 2048 * 2)
        ssc = A.alloc(f"l1f_ssc{l}", 4096, top=True)
        ptb = A.alloc(f"l1f_ptb{l}", 2048, top=True)
        for h in range(8):
            hf, hr = h // 2, (h % 2) * 64
            for qt in range(16):
                k0t = max(0, qt - 4)
                nk = (qt - k0t + 1) * 128
                for (n0, nn, bk) in ((0, min(512, nk), 0), (512, nk - 512, 1)):
                    if nn <= 0:
                        continue
                    lhs = qfm.ap(BF16, 128, coff=hf * 2048 + qt * 128, p0=hr, pn=64)
                    lk = qfm.key(BF16, 128, coff=hf * 2048 + qt * 128)
                    rhs = kfm.ap(BF16, nn, coff=hf * 2048 + k0t * 128 + n0, p0=hr, pn=64)
                    rk = kfm.key(BF16, nn, coff=hf * 2048 + k0t * 128 + n0)
                    b.op("tensor", (lambda lhs=lhs, rhs=rhs, bk=bk, hr=hr, nn=nn: lambda e: e.matmul(PS[bk].ap(F32, nn), lhs, rhs, start=True, stop=True, tile_position=(hr, 0)))(),
                         rd=[lk, rk], wr=[PS[bk].key()], tag=f"l1f_s{l}_{h}_{qt}_{n0}")
                    moff = (4 - (qt - k0t)) * 128 + n0
                    b.op("vector", (lambda bk=bk, n0=n0, nn=nn, moff=moff: lambda e: e.tensor_tensor(out=ssc.ap(F32, nn, coff=n0), in0=PS[bk].ap(F32, nn), in1=mskt.ap(F32, nn, coff=moff), op=ALU.add))(),
                         rd=[PS[bk].key(), mskt.key()], wr=[ssc.key(F32, nn, coff=n0)], tag=f"l1f_sm{l}_{h}_{qt}_{n0}")
                rm = nsc.ap(F32, 1, coff=24); rmk = nsc.key(F32, 1, coff=24)
                b.op("vector", (lambda nk=nk: lambda e: e.tensor_reduce(out=rm, in_=ssc.ap(F32, nk), axis=AX.X, op=ALU.max))(),
                     rd=[ssc.key()], wr=[rmk], tag=f"l1f_rm{l}_{h}_{qt}")
                b.op("vector", (lambda: lambda e: e.tensor_scalar(out=rm, in0=rm, scalar1=-8.0, scalar2=None, op0=ALU.mult))(),
                     rd=[rmk], wr=[rmk], tag=f"l1f_rmn{l}_{h}_{qt}")
                b.op("scalar", (lambda nk=nk: lambda e: e.activation(out=ssc.ap(BF16, nk, coff=2048), in_=ssc.ap(F32, nk), func=AF.Exp, bias=rm, scale=fc(env, 8.0)))(),
                     rd=[ssc.key(), rmk, fck(env, 8.0)], wr=[ssc.key(BF16, nk, coff=2048)], tag=f"l1f_ex{l}_{h}_{qt}")
                sm = nsc.ap(F32, 1, coff=25); smk = nsc.key(F32, 1, coff=25)
                b.op("vector", (lambda nk=nk: lambda e: e.tensor_scalar(out=ssc.ap(F32, nk), in0=ssc.ap(BF16, nk, coff=2048), scalar1=0.0, scalar2=None, op0=ALU.add, op1=ALU.add, accum_out=sm))(),
                     rd=[ssc.key(BF16, nk, coff=2048)], wr=[ssc.key(), smk], tag=f"l1f_sum{l}_{h}_{qt}")
                b.op("vector", (lambda: lambda e: e.reciprocal(out=sm, in_=sm))(),
                     rd=[smk], wr=[smk], tag=f"l1f_rc{l}_{h}_{qt}")
                b.op("vector", (lambda nk=nk: lambda e: e.tensor_scalar(out=ssc.ap(BF16, nk, coff=2048), in0=ssc.ap(BF16, nk, coff=2048), scalar1=sm, scalar2=None, op0=ALU.mult))(),
                     rd=[ssc.key(BF16, nk, coff=2048), smk], wr=[ssc.key(BF16, nk, coff=2048)], tag=f"l1f_pn{l}_{h}_{qt}")
                nkt = nk // 128
                for ktl in range(nkt):
                    pt = PS[6 + ktl % 2]
                    b.op("tensor", (lambda ktl=ktl, pt=pt: lambda e: e.transpose(pt.ap(BF16, 128), ssc.ap(BF16, 128, coff=2048 + ktl * 128), ident_bf(env)))(),
                         rd=[ssc.key(BF16, nk, coff=2048), env.ident.key()], wr=[pt.key()], tag=f"l1f_pt{l}_{h}_{qt}_{ktl}")
                    b.op("scalar", (lambda ktl=ktl, pt=pt: lambda e: e.activation(out=ptb.ap(BF16, 128, coff=ktl * 128), in_=pt.ap(BF16, 128), func=AF.Copy, bias=0.0, scale=1.0))(),
                         rd=[pt.key()], wr=[ptb.key(BF16, 128, coff=ktl * 128)], tag=f"l1f_pte{l}_{h}_{qt}_{ktl}")
                for ktl in range(nkt):
                    lhs = vt_.ap(BF16, 64, coff=(k0t + ktl) * 512 + h * 64)
                    lk = vt_.key(BF16, 64, coff=(k0t + ktl) * 512 + h * 64)
                    rhs = ptb.ap(BF16, 128, coff=ktl * 128)
                    rk = ptb.key(BF16, 128, coff=ktl * 128)
                    b.op("tensor", (lambda lhs=lhs, rhs=rhs, ktl=ktl, nkt=nkt: lambda e: e.matmul(PS[2].ap(F32, 128, pn=64), lhs, rhs, start=(ktl == 0), stop=(ktl == nkt - 1)))(),
                         rd=[lk, rk], wr=[PS[2].key()], tag=f"l1f_pv{l}_{h}_{qt}_{ktl}")
                b.op("scalar", (lambda h=h, qt=qt, hr=hr, hf=hf: lambda e: e.activation(out=ofm.ap(BF16, 128, coff=hf * 2048 + qt * 128, p0=hr, pn=64), in_=PS[2].ap(F32, 128, pn=64), func=AF.Copy, bias=0.0, scale=1.0))(),
                     rd=[PS[2].key()], wr=[ofm.key(BF16, 128, coff=hf * 2048 + qt * 128)], tag=f"l1f_oe{l}_{h}_{qt}")
        A.free(f"l1f_qfm{l}", f"l1f_kfm{l}", f"l1f_vt{l}", f"l1f_ssc{l}", f"l1f_ptb{l}", f"l1f_nsc{l}", f"l1f_rsc{l}", f"l1f_xp{l}", f"l1f_mask{l}")
        slot2 = A.alloc(f"l1f_slot2{l}", 8192, top=True)
        for kt in range(4):
            b.dma("gpsimd", (lambda kt=kt, l=l: lambda e: e.dma_start(out=slot2.ap(BF16, 1024, coff=kt * 1024), in_=d["l1f_wob"][l, kt * 128:(kt + 1) * 128, :]))(),
                  rd=[("l1f_wob", l, l + 1)], wr=[slot2.key(BF16, 1024, coff=kt * 1024)], tag=f"l1f_wow{l}_{kt}")
        for mt in range(8):
            for ch in range(4):
                bk = (mt * 4 + ch) % 4
                for kt in range(4):
                    lhs = slot2.ap(BF16, 128, coff=kt * 1024 + mt * 128)
                    lk = slot2.key(BF16, 128, coff=kt * 1024 + mt * 128)
                    rhs = ofm.ap(BF16, 512, coff=kt * T + ch * 512)
                    rk = ofm.key(BF16, 512, coff=kt * T + ch * 512)
                    b.op("tensor", (lambda lhs=lhs, rhs=rhs, bk=bk, kt=kt: lambda e: e.matmul(PS[bk].ap(F32, 512), lhs, rhs, start=(kt == 0), stop=(kt == 3)))(),
                         rd=[lk, rk], wr=[PS[bk].key()], tag=f"l1f_wom{l}_{mt}_{ch}_{kt}")
                xap = XI.ap(BF16, 512, coff=mt * T + ch * 512)
                xk = XI.key(BF16, 512, coff=mt * T + ch * 512)
                b.op("vector", (lambda xap=xap, bk=bk: lambda e: e.tensor_tensor(out=xap, in0=PS[bk].ap(F32, 512), in1=xap, op=ALU.add))(),
                     rd=[PS[bk].key(), xk], wr=[xk], tag=f"l1f_wores{l}_{mt}_{ch}")
        A.free(f"l1f_ofm{l}", f"l1f_slot2{l}")
        ZI = A.alloc(f"l1f_zi{l}", 16384 * 2)
        for f in range(8):
            if f < 4:
                b.op("vector", (lambda f=f: lambda e: e.tensor_copy(out=ZI.ap(BF16, T, coff=f * T), in_=XI.ap(BF16, T, coff=f * T)))(),
                     rd=[XI.key(BF16, T, coff=f * T)], wr=[ZI.key(BF16, T, coff=f * T)], tag=f"l1f_s2{l}_{f}")
            else:
                b.op("vector", (lambda f=f: lambda e: e.tensor_copy(out=ZI.ap(BF16, T - 1, coff=f * T + 1), in_=XI.ap(BF16, T - 1, coff=f * T)))(),
                     rd=[XI.key(BF16, T, coff=f * T)], wr=[ZI.key(BF16, T, coff=f * T)], tag=f"l1f_s2{l}_{f}")
                b.op("vector", (lambda f=f: lambda e: e.memset(ZI.ap(BF16, 1, coff=f * T), 0.0))(),
                     rd=[], wr=[ZI.key(BF16, 1, coff=f * T)], tag=f"l1f_s2z{l}_{f}")
        emit_ln(b, env, ZI, tag=f"l1f_iln{l}", src=ZI, srcdt=BF16)
        for f in range(8):
            b.op("vector", (lambda f=f, l=l: lambda e: e.tensor_scalar(out=ZI.ap(BF16, T, coff=f * T), in0=ZI.ap(BF16, T, coff=f * T), scalar1=cols.ap(F32, 1, coff=l * 8 + f), scalar2=cols.ap(F32, 1, coff=16 + l * 8 + f), op0=ALU.mult, op1=ALU.add))(),
                 rd=[ZI.key(BF16, T, coff=f * T), cols.key()], wr=[ZI.key(BF16, T, coff=f * T)], tag=f"l1f_zg{l}_{f}")
        hg = A.alloc(f"l1f_hg{l}", 22 * 512 * 2)
        ha = A.alloc(f"l1f_ha{l}", 22 * 512 * 2)
        ws = [A.alloc(f"l1f_ws{l}_{i}", 4096, top=True) for i in range(2)]
        MT2 = [128] * 21 + [42]
        for tc in range(4):
            for part, dst in ((1, hg), (0, ha)):
                base = 2730 * part
                mg0 = 0
                di = 0
                while mg0 < 22:
                    mgn = min(4, 22 - mg0)
                    for kt in range(8):
                        sl = ws[di % 2]
                        di += 1
                        c0 = base + mg0 * 128
                        c1 = min(base + (mg0 + mgn) * 128, base + 2730)
                        b.dma("gpsimd", (lambda kt=kt, l=l, c0=c0, c1=c1, sl=sl: lambda e: e.dma_start(out=sl.ap(BF16, c1 - c0), in_=d["l1f_w1b"][l, kt * 128:(kt + 1) * 128, c0:c1]))(),
                              rd=[("l1f_w1b", l, l + 1)], wr=[sl.key(BF16, c1 - c0)], tag=f"l1f_w1{l}_{tc}_{part}_{kt}_{mg0}")
                        for j in range(mgn):
                            mt = mg0 + j
                            mw = MT2[mt]
                            bk = j % 4
                            lhs = sl.ap(BF16, mw, coff=j * 128)
                            lk = sl.key(BF16, mw, coff=j * 128)
                            rhs = ZI.ap(BF16, 512, coff=kt * T + tc * 512)
                            rk = ZI.key(BF16, 512, coff=kt * T + tc * 512)
                            b.op("tensor", (lambda lhs=lhs, rhs=rhs, bk=bk, kt=kt, mw=mw: lambda e: e.matmul(PS[bk].ap(F32, 512, pn=mw), lhs, rhs, start=(kt == 0), stop=(kt == 7)))(),
                                 rd=[lk, rk], wr=[PS[bk].key()], tag=f"l1f_u{l}_{tc}_{part}_{kt}_{mt}")
                    for j in range(mgn):
                        mt = mg0 + j
                        mw = MT2[mt]
                        bk = j % 4
                        if part == 0:
                            b.op("scalar", (lambda mt=mt, bk=bk, mw=mw, l=l, dst=dst: lambda e: e.activation(out=dst.ap(BF16, 512, coff=mt * 512, pn=mw), in_=PS[bk].ap(F32, 512, pn=mw), func=AF.Identity, bias=b1c.ap(F32, 1, coff=l * 44 + 0 * 22 + mt, pn=mw), scale=1.0))(),
                                 rd=[PS[bk].key(), b1c.key()], wr=[dst.key(BF16, 512, coff=mt * 512)], tag=f"l1f_ue{l}_{tc}_{part}_{mt}")
                        else:
                            b.op("scalar", (lambda mt=mt, bk=bk, mw=mw, l=l, dst=dst: lambda e: e.activation(out=dst.ap(BF16, 512, coff=mt * 512, pn=mw), in_=PS[bk].ap(F32, 512, pn=mw), func=AF.Gelu, bias=b1c.ap(F32, 1, coff=l * 44 + 22 + mt, pn=mw), scale=1.0))(),
                                 rd=[PS[bk].key(), b1c.key()], wr=[dst.key(BF16, 512, coff=mt * 512)], tag=f"l1f_ue{l}_{tc}_{part}_{mt}")
                    mg0 += mgn
            for mt in range(22):
                mw = MT2[mt]
                b.op("vector", (lambda mt=mt, mw=mw: lambda e: e.tensor_tensor(out=ha.ap(BF16, 512, coff=mt * 512, pn=mw), in0=ha.ap(BF16, 512, coff=mt * 512, pn=mw), in1=hg.ap(BF16, 512, coff=mt * 512, pn=mw), op=ALU.mult))(),
                     rd=[ha.key(BF16, 512, coff=mt * 512), hg.key(BF16, 512, coff=mt * 512)], wr=[ha.key(BF16, 512, coff=mt * 512)], tag=f"l1f_ag{l}_{tc}_{mt}")
            mg0 = 0
            di = 0
            while mg0 < 8:
                mgn = min(4, 8 - mg0)
                for kt in range(22):
                    kw = MT2[kt]
                    sl = ws[di % 2]
                    di += 1
                    b.dma("gpsimd", (lambda kt=kt, l=l, mg0=mg0, mgn=mgn, sl=sl, kw=kw: lambda e: e.dma_start(out=sl.ap(BF16, mgn * 128, pn=kw), in_=d["l1f_w2b"][l, kt * 128:kt * 128 + kw, mg0 * 128:(mg0 + mgn) * 128]))(),
                          rd=[("l1f_w2b", l, l + 1)], wr=[sl.key(BF16, mgn * 128)], tag=f"l1f_w2{l}_{tc}_{kt}_{mg0}")
                    for j in range(mgn):
                        bk = j % 4
                        lhs = sl.ap(BF16, 128, coff=j * 128, pn=kw)
                        lk = sl.key(BF16, mgn * 128)
                        rhs = ha.ap(BF16, 512, coff=kt * 512, pn=kw)
                        rk = ha.key(BF16, 512, coff=kt * 512)
                        b.op("tensor", (lambda lhs=lhs, rhs=rhs, bk=bk, kt=kt, kw=kw: lambda e: e.matmul(PS[bk].ap(F32, 512), lhs, rhs, start=(kt == 0), stop=(kt == 21)))(),
                             rd=[lk, rk], wr=[PS[bk].key()], tag=f"l1f_d{l}_{tc}_{kt}_{mg0 + j}")
                for j in range(mgn):
                    mt = mg0 + j
                    bk = j % 4
                    xap = XI.ap(BF16, 512, coff=mt * T + tc * 512)
                    xk = XI.key(BF16, 512, coff=mt * T + tc * 512)
                    b.op("vector", (lambda xap=xap, bk=bk, mt=mt, l=l: lambda e: e.scalar_tensor_tensor(out=xap, in0=PS[bk].ap(F32, 512), scalar=cols.ap(F32, 1, coff=32 + l * 8 + mt), in1=xap, op0=ALU.add, op1=ALU.add))(),
                         rd=[PS[bk].key(), cols.key(), xk], wr=[xk], tag=f"l1f_dres{l}_{tc}_{mt}")
                mg0 += mgn
        A.free(f"l1f_zi{l}", f"l1f_hg{l}", f"l1f_ha{l}", f"l1f_ws{l}_0", f"l1f_ws{l}_1")

    for _l in range(2):
        emit_layer(_l)
    for f in range(8):
        b.op("vector", (lambda f=f: lambda e: e.tensor_tensor(out=env.XR.ap(F32, T, coff=f * T), in0=env.XR.ap(F32, T, coff=f * T), in1=XI.ap(BF16, T, coff=f * T), op=ALU.add))(),
             rd=[env.XR.key(F32, T, coff=f * T), XI.key(BF16, T, coff=f * T)], wr=[env.XR.key(F32, T, coff=f * T)], tag=f"l1f_res{f}")
    A.reset(m0)


BLOCKS["l1f"] = (emit_l1f, dram_l1f, host_l1f)


def dram_l2a():
    return {
        "l2a_win": ((1024, 3080), F32, "ExternalInput"),
        "l2a_bin": ((1, 3080), F32, "ExternalInput"),
        "l2a_xuprt": ((64, 64), F32, "ExternalInput"),
        "l2a_ouprt": ((1024, 64), F32, "ExternalInput"),
        "l2a_xupA": ((64, 512), F32, "ExternalInput"),
        "l2a_xupB": ((64, 1024), F32, "ExternalInput"),
        "l2a_oupA": ((1024, 32), F32, "ExternalInput"),
        "l2a_oupB": ((1024, 32), F32, "ExternalInput"),
        "l2a_oupAe": ((64, 1024, 32), F32, "ExternalInput"),
        "l2a_oupBe": ((64, 1024, 32), F32, "ExternalInput"),
        "l2a_thetal": ((128, 1), F32, "ExternalInput"),
        "l2a_gpat": ((8, 128), F32, "ExternalInput"),
        "l2a_tri": ((128, 128), BF16, "ExternalInput"),
        "l2a_normB": ((1, 512), F32, "ExternalInput"),
        "l2a_normC": ((1, 512), F32, "ExternalInput"),
        "l2a_biasB": ((1, 512), F32, "ExternalInput"),
        "l2a_biasC": ((1, 512), F32, "ExternalInput"),
        "l2a_Dex": ((128, 8), F32, "ExternalInput"),
        "l2a_preg": ((1, 1024), F32, "ExternalInput"),
        "l2a_xupsc": ((1, 1), F32, "ExternalInput"),
        "l2a_xupbi": ((1, 128), F32, "ExternalInput"),
        "l2a_oupsc": ((1, 1), F32, "ExternalInput"),
        "l2a_oupbi": ((1, 1024), F32, "ExternalInput"),
        "l2a_spill": ((4, 128, 16384), BF16, "Internal"),
        "l2a_spill2": ((1, 128, 2048), F32, "Internal"),
        "l2a_wsp": ((64, 2048), F32, "Internal"),
        "l2a_xvsp": ((128, 32768), BF16, "Internal"),
        "l2a_ydw": ((128, 64), F32, "ExternalInput"),
        "l2a_dupm": ((128, 128), BF16, "ExternalInput"),
        "l2a_Drow": ((1, 1024), F32, "ExternalInput"),
        "l2a_oupbif": ((128, 8), F32, "ExternalInput"),
        "l2a_wdtA": ((128, 64), F32, "ExternalInput"),
        "l2a_bdtA": ((1, 8), F32, "ExternalInput"),
        "l2a_stsp": ((1, 4096), F32, "Internal"),
    }


def host_l2a(inp):
    H, G, P, N, R = 16, 4, 64, 64, 2
    W = np.asarray(inp["L2_attn_in_proj_w"], np.float32)
    bb = np.asarray(inp["L2_attn_in_proj_b"], np.float32)
    idx = []
    idx += list(range(0, 1024))
    idx += list(range(1024, 2048))
    for sec in (2048, 2560):
        for g in range(4):
            for r in range(2):
                for eo in range(2):
                    for m in range(32):
                        n = 2 * m + eo
                        idx.append(sec + g * 128 + n * 2 + r)
    idx += list(range(3072, 3076))
    idx += list(range(3076, 3080))
    idx = np.array(idx)
    xup_perm = np.zeros(128, np.int64)
    for o in range(128):
        p, r = o // 2, o % 2
        xup_perm[r * 64 + p] = o
    A_ = np.asarray(inp["L2_attn_xup_A"], np.float32)
    B_ = np.asarray(inp["L2_attn_xup_B"], np.float32)
    Ah = np.zeros((64, 8, 64), np.float32)
    Bh = np.zeros((64, 16, 64), np.float32)
    for ij in range(64):
        i, j = ij // 8, ij % 8
        Ah[ij] = A_[:, :, i].T
        Bh[ij] = B_[:, :, j].T
    Ah2 = np.zeros((64, 512), np.float32)
    Bh2 = np.zeros((64, 1024), np.float32)
    for ij in range(64):
        i, j = ij // 8, ij % 8
        Ah2[ij] = A_[:, :, i].reshape(64 * 8)
        Bh2[ij] = B_[:, :, j].reshape(64 * 16)
    oA = np.asarray(inp["L2_attn_oup_A"], np.float32)
    oB = np.asarray(inp["L2_attn_oup_B"], np.float32)
    oAe = np.zeros((64, 1024, 32), np.float32)
    oBe = np.zeros((64, 1024, 32), np.float32)
    for ij in range(1024):
        i, j = ij // 32, ij % 32
        oAe[:, ij, :] = oA[:, :, i]
        oBe[:, ij, :] = oB[:, :, j]
    tl = np.asarray(inp["L2_attn_theta_log"], np.float32).reshape(4, 32)
    tlcol = np.zeros((128, 1), np.float32)
    for g in range(4):
        tlcol[g * 32:(g + 1) * 32, 0] = tl[g]
    gpat = np.zeros((8, 128), np.float32)
    for g in range(4):
        gpat[g, g * 32:(g + 1) * 32] = 1.0
    tri = np.tril(np.ones((128, 128), np.float32))
    tri = np.triu(np.ones((128, 128), np.float32))
    nperm = np.zeros(128, np.int64)
    for r in range(2):
        for eo in range(2):
            for m in range(32):
                nperm[r * 64 + eo * 32 + m] = (2 * m + eo) * 2 + r
    nB = np.asarray(inp["L2_attn_normB_w"], np.float32)[nperm]
    nC = np.asarray(inp["L2_attn_normC_w"], np.float32)[nperm]
    bB = np.asarray(inp["L2_attn_bias_B"], np.float32).reshape(4, 128)[:, nperm]
    bC = np.asarray(inp["L2_attn_bias_C"], np.float32).reshape(4, 128)[:, nperm]
    Dex = np.repeat(np.asarray(inp["L2_attn_D"], np.float32), 64).reshape(8, 128).T
    xbi = np.asarray(inp["L2_attn_xup_bias"], np.float32)[xup_perm]
    return {
        "l2a_win": np.ascontiguousarray(W.T[:, idx]),
        "l2a_bin": np.ascontiguousarray(bb[idx].reshape(1, 3080)),
        "l2a_xuprt": np.ascontiguousarray(np.asarray(inp["L2_attn_xup_router"], np.float32).T),
        "l2a_ouprt": np.ascontiguousarray(np.asarray(inp["L2_attn_oup_router"], np.float32).T),
        "l2a_xupA": Ah2,
        "l2a_xupB": Bh2,
        "l2a_oupA": oAe[0], "l2a_oupB": oBe[0],
        "l2a_oupAe": oAe, "l2a_oupBe": oBe,
        "l2a_thetal": tlcol,
        "l2a_gpat": gpat,
        "l2a_tri": tri.astype(BF),
        "l2a_normB": np.concatenate([nB] * 4).reshape(1, 512),
        "l2a_normC": np.concatenate([nC] * 4).reshape(1, 512),
        "l2a_biasB": bB.reshape(1, 512),
        "l2a_biasC": bC.reshape(1, 512),
        "l2a_Dex": np.ascontiguousarray(Dex),
        "l2a_preg": np.asarray(inp["L2_attn_pregate_w"], np.float32).reshape(1, 1024),
        "l2a_xupsc": np.asarray(inp["L2_attn_xup_scale"], np.float32).reshape(1, 1),
        "l2a_xupbi": xbi.reshape(1, 128),
        "l2a_oupsc": np.asarray(inp["L2_attn_oup_scale"], np.float32).reshape(1, 1),
        "l2a_oupbi": np.asarray(inp["L2_attn_oup_bias"], np.float32).reshape(1, 1024),
        "l2a_ydw": np.ascontiguousarray(np.asarray(inp["L2_attn_ydown_w"], np.float32).T[[2 * p + rp for rp in range(2) for p in range(64)], :]),
        "l2a_dupm": np.tile(np.eye(64, dtype=np.float32), (2, 2)).astype(BF),
        "l2a_Drow": np.repeat(np.asarray(inp["L2_attn_D"], np.float32), 64).reshape(1, 1024),
        "l2a_oupbif": np.ascontiguousarray(np.asarray(inp["L2_attn_oup_bias"], np.float32).reshape(8, 128).T),
        "l2a_wdtA": np.ascontiguousarray(W.T[:, 3072:3080].reshape(8, 128, 8).transpose(1, 0, 2).reshape(128, 64)),
        "l2a_bdtA": np.ascontiguousarray(bb[3072:3080].reshape(1, 8)),
    }


def emit_l2a(b, env):
    A, d, PS = env.arena, env.dram, env.PS
    m0 = A.mark()
    XN = A.alloc("l2a_xn", 16384 * 2)
    stats = A.alloc("l2a_stats", 2 * T * 4)
    emit_ln(b, env, XN, tag="l2a_ln", keep=stats)
    b.dma("sync", (lambda stats=stats: lambda e: e.dma_start(out=d["l2a_stsp"][:, :], in_=stats.ap(F32, 2 * T, pn=1)))(),
          rd=[stats.key()], wr=[("l2a_stsp", 0, 1)], tag="l2a_stsp")
    A.free("l2a_stats")
    if getattr(env, "l2a_cut", 99) <= 1:
        A.reset(m0)
        return
    binr = A.alloc("l2a_binr", 6400, top=True)
    b.dma("gpsimd", lambda e: e.dma_start(out=binr.ap(BF16, 3080, pn=1), in_=d["l2a_bin"][:, :]),
          rd=[("l2a_bin", 0, 1)], wr=[binr.key()], tag="l2a_ldbin")
    wsl = [A.alloc(f"l2a_wsl{i}", 6400, top=True) for i in range(2)]
    xpt = A.alloc("l2a_xpt", 16 * 1024 * 2)
    Bt = A.alloc("l2a_Bt", 16 * 512 * 2)
    Ct = A.alloc("l2a_Ct", 16 * 512 * 2)
    dtt = A.alloc("l2a_dtt", 16 * 8 * 4)
    ztile = A.alloc("l2a_zt", 2 * 1024 * 2)
    CH = [(0, 512), (512, 512), (1024, 512), (1536, 512), (2048, 512), (2560, 512)]
    for tt in range(16):
        for kt in range(8):
            sl = wsl[kt % 2]
            b.dma("gpsimd", (lambda kt=kt, sl=sl: lambda e: e.dma_start(out=sl.ap(BF16, 3080), in_=d["l2a_win"][kt * 128:(kt + 1) * 128, :]))(),
                  rd=[("l2a_win", 0, 1)], wr=[sl.key()], tag=f"l2a_wl{tt}_{kt}")
            lhs = XN.ap(BF16, 128, coff=kt * T + tt * 128)
            lk = XN.key(BF16, 128, coff=kt * T + tt * 128)
            for ci, (c0, cn) in enumerate(CH):
                rhs = sl.ap(BF16, cn, coff=c0)
                rk = sl.key(BF16, cn, coff=c0)
                b.op("tensor", (lambda lhs=lhs, lk=lk, rhs=rhs, rk=rk, ci=ci, kt=kt, cn=cn: lambda e: e.matmul(PS[ci].ap(F32, cn), lhs, rhs, start=(kt == 0), stop=False))(),
                     rd=[lk, rk], wr=[PS[ci].key()], tag=f"l2a_p{tt}_{ci}_{kt}")
        for ci, (c0, cn) in enumerate(CH):
            bk = ci
            b.op("tensor", (lambda bk=bk, c0=c0, cn=cn: lambda e: e.matmul(PS[bk].ap(F32, cn), ones_row_bf(env), binr.ap(BF16, cn, coff=c0, pn=1), start=False, stop=True))(),
                 rd=[onesrowbf_key(env), binr.key()], wr=[PS[bk].key()], tag=f"l2a_pb{tt}_{ci}")
            if ci < 2:
                dap = ztile.ap(BF16, cn, coff=(tt % 2) * 1024 + c0)
                dk = ztile.key(BF16, cn, coff=(tt % 2) * 1024 + c0)
            elif ci < 4:
                dap = xpt.ap(BF16, cn, coff=tt * 1024 + (c0 - 1024))
                dk = xpt.key(BF16, cn, coff=tt * 1024 + (c0 - 1024))
            elif ci == 4:
                dap = Bt.ap(BF16, cn, coff=tt * 512)
                dk = Bt.key(BF16, cn, coff=tt * 512)
            else:
                dap = Ct.ap(BF16, cn, coff=tt * 512)
                dk = Ct.key(BF16, cn, coff=tt * 512)
            b.op("scalar", (lambda dap=dap, bk=bk, cn=cn: lambda e: e.activation(out=dap, in_=PS[bk].ap(F32, cn), func=AF.Copy, bias=0.0, scale=1.0))(),
                 rd=[PS[bk].key()], wr=[dk], tag=f"l2a_pe{tt}_{ci}")
        b.dma("sync", (lambda tt=tt: lambda e: e.dma_start(out=d["l2a_spill"][0, :, tt * 1024:(tt + 1) * 1024], in_=ztile.ap(BF16, 1024, coff=(tt % 2) * 1024)))(),
              rd=[ztile.key(BF16, 1024, coff=(tt % 2) * 1024)], wr=[("l2a_spill", 0, 1)], tag=f"l2a_zsp{tt}")
    if getattr(env, "l2a_cut", 99) <= 2:
        A.reset(m0)
        return
    A.free("l2a_xn", "l2a_zt", "l2a_wsl0", "l2a_wsl1", "l2a_binr")
    stats = A.alloc("l2a_stats", 2 * T * 4, top=True)
    b.dma("sync", lambda e: e.dma_start(out=stats.ap(F32, 2 * T, pn=1), in_=d["l2a_stsp"][:, :]),
          rd=[("l2a_stsp", 0, 1)], wr=[stats.key()], tag="l2a_stld")
    idf1 = A.alloc("l2a_idf1", 512, top=True)
    b.dma("sync", lambda e: e.dma_start(out=idf1.ap(F32, 128), in_=d["c_ident_f32"][:, :]),
          rd=[("c_ident_f32", 0, 1)], wr=[idf1.key()], tag="l2a_ldidf1")
    wdta = A.alloc("l2a_wdta", 256, top=True)
    b.dma("sync", lambda e: e.dma_start(out=wdta.ap(F32, 64), in_=d["l2a_wdtA"][:, :]),
          rd=[("l2a_wdtA", 0, 1)], wr=[wdta.key()], tag="l2a_ldwdta")
    da_sc = env.ST.alloc("l2a_dasc", 512)
    b.dma("sync", lambda e: e.dma_start(out=da_sc.ap(F32, 8, coff=16, pn=1), in_=d["l2a_bdtA"][:, :]),
          rd=[("l2a_bdtA", 0, 1)], wr=[da_sc.key(F32, 8, coff=16)], tag="l2a_ldbdta")
    for kt in range(8):
        b.op("tensor", (lambda kt=kt: lambda e: e.matmul(PS[6].ap(F32, 1, pn=8), wdta.ap(F32, 8, coff=kt * 8), env.ones.ap(F32, 1, coff=1), start=(kt == 0), stop=(kt == 7)))(),
             rd=[wdta.key(), onescf_key(env)], wr=[PS[6].key()], tag=f"l2a_s2m{kt}")
    b.op("scalar", lambda e: e.activation(out=da_sc.ap(F32, 1, coff=30, pn=8), in_=PS[6].ap(F32, 1, pn=8), func=AF.Copy, bias=0.0, scale=1.0),
         rd=[PS[6].key()], wr=[da_sc.key(F32, 1, coff=30)], tag="l2a_s2e")
    b.op("tensor", lambda e: e.transpose(PS[6].ap(F32, 8, pn=1), da_sc.ap(F32, 1, coff=30, pn=8), idf1.ap(F32, 8, pn=8)),
         rd=[da_sc.key(F32, 1, coff=30), idf1.key()], wr=[PS[6].key()], tag="l2a_s2T")
    b.op("scalar", lambda e: e.activation(out=da_sc.ap(F32, 8, pn=1), in_=PS[6].ap(F32, 8, pn=1), func=AF.Copy, bias=0.0, scale=fc(env, -1.0, 1)),
         rd=[PS[6].key(), fck(env, -1.0)], wr=[da_sc.key(F32, 8)], tag="l2a_ns2")
    bbc8 = A.alloc("l2a_bbc8", 512, top=True)
    b.op("tensor", lambda e: e.matmul(PS[6].ap(F32, 8), ones_row_f32(env), da_sc.ap(F32, 8, coff=16, pn=1), start=True, stop=True),
         rd=[onesrow_key(env), da_sc.key(F32, 8, coff=16)], wr=[PS[6].key()], tag="l2a_bb8")
    b.op("vector", lambda e: e.tensor_copy(out=bbc8.ap(F32, 8), in_=PS[6].ap(F32, 8)),
         rd=[PS[6].key()], wr=[bbc8.key()], tag="l2a_bb8e")
    for tt in range(16):
        murow = stats.ap(F32, 128, coff=tt * 128, pn=1)
        muk = (stats.name, stats.off + tt * 512, stats.off + (tt + 1) * 512)
        rsrow = stats.ap(F32, 128, coff=T + tt * 128, pn=1)
        rsk = (stats.name, stats.off + T * 4 + tt * 512, stats.off + T * 4 + (tt + 1) * 512)
        b.op("tensor", (lambda rsrow=rsrow, rsk=rsk: lambda e: e.transpose(PS[7].ap(F32, 1), rsrow, idf1.ap(F32, 1, pn=1)))(),
             rd=[rsk, idf1.key()], wr=[PS[7].key()], tag=f"l2a_rsT{tt}")
        b.op("scalar", (lambda: lambda e: e.activation(out=da_sc.ap(F32, 1, coff=10), in_=PS[7].ap(F32, 1, pn=128), func=AF.Copy, bias=0.0, scale=1.0))(),
             rd=[PS[7].key()], wr=[da_sc.key(F32, 1, coff=10)], tag=f"l2a_rsTe{tt}")
        for kt in range(8):
            b.op("tensor", (lambda kt=kt, tt=tt: lambda e: e.matmul(PS[5].ap(F32, 8), env.XR.ap(F32, 128, coff=kt * T + tt * 128), wdta.ap(F32, 8, coff=kt * 8), start=(kt == 0), stop=False))(),
                 rd=[env.XR.key(F32, 128, coff=kt * T + tt * 128), wdta.key()], wr=[PS[5].key()], tag=f"l2a_dam{tt}_{kt}")
        b.op("tensor", (lambda murow=murow, muk=muk: lambda e: e.matmul(PS[5].ap(F32, 8), murow, da_sc.ap(F32, 8, pn=1), start=False, stop=True))(),
             rd=[muk, da_sc.key(F32, 8)], wr=[PS[5].key()], tag=f"l2a_dao{tt}")
        b.op("scalar", (lambda tt=tt: lambda e: e.activation(out=dtt.ap(F32, 8, coff=tt * 8), in_=PS[5].ap(F32, 8), func=AF.Identity, bias=0.0, scale=da_sc.ap(F32, 1, coff=10)))(),
             rd=[PS[5].key(), da_sc.key(F32, 1, coff=10)], wr=[dtt.key(F32, 8, coff=tt * 8)], tag=f"l2a_dae{tt}")
        b.op("vector", (lambda tt=tt: lambda e: e.tensor_tensor(out=dtt.ap(F32, 8, coff=tt * 8), in0=dtt.ap(F32, 8, coff=tt * 8), in1=bbc8.ap(F32, 8), op=ALU.add))(),
             rd=[dtt.key(F32, 8, coff=tt * 8), bbc8.key()], wr=[dtt.key(F32, 8, coff=tt * 8)], tag=f"l2a_dab{tt}")
    if "l2d_dtt" in env.dram:
        emit_dbg(b, env, "l2d_dtt", dtt, dtype=F32)
    A.free("l2a_stats", "l2a_wdta", "l2a_bbc8")
    if getattr(env, "l2a_cut", 99) <= 3:
        A.reset(m0)
        return
    dtv = dtt.ap(F32, 128).rearrange("p (t c) -> p t c", c=8)[:, :, 0:4]
    dtk = dtt.key()
    av = dtt.ap(F32, 128).rearrange("p (t c) -> p t c", c=8)[:, :, 4:8]
    sps = env.ST.alloc("l2a_sps", 2048)
    s1v = sps.ap(F32, 64).rearrange("p (t c) -> p t c", c=4)
    s2v = sps.ap(F32, 64, coff=64).rearrange("p (t c) -> p t c", c=4)
    b.op("scalar", lambda e: e.activation(out=s1v, in_=dtv, func=AF.Abs, bias=0.0, scale=1.0), rd=[dtk], wr=[sps.key(F32, 64)], tag="l2a_sp1")
    b.op("scalar", lambda e: e.activation(out=s1v, in_=s1v, func=AF.Exp, bias=0.0, scale=fc(env, -1.0)), rd=[sps.key(F32, 64), fck(env, -1.0)], wr=[sps.key(F32, 64)], tag="l2a_sp2")
    b.op("scalar", lambda e: e.activation(out=s1v, in_=s1v, func=AF.Ln, bias=fc(env, 1.0), scale=1.0), rd=[sps.key(F32, 64), fck(env, 1.0)], wr=[sps.key(F32, 64)], tag="l2a_sp3")
    b.op("vector", lambda e: e.tensor_scalar(out=s2v, in0=dtv, scalar1=0.0, scalar2=None, op0=ALU.max), rd=[dtk], wr=[sps.key(F32, 64, coff=64)], tag="l2a_sp4")
    b.op("vector", lambda e: e.tensor_tensor(out=dtv, in0=s1v, in1=s2v, op=ALU.add), rd=[sps.key()], wr=[dtk], tag="l2a_sp5")
    b.op("scalar", lambda e: e.activation(out=av, in_=av, func=AF.Exp, bias=0.0, scale=1.0), rd=[dtk], wr=[dtk], tag="l2a_ae")
    b.op("vector", lambda e: e.tensor_scalar(out=av, in0=av, scalar1=-1.0, scalar2=None, op0=ALU.mult), rd=[dtk], wr=[dtk], tag="l2a_an")
    dtpfm = A.alloc("l2a_dtpfm", 8192)
    afm = A.alloc("l2a_afm", 8192)
    b.op("vector", lambda e: e.memset(dtpfm.ap(F32, T, pn=8), 0.0), rd=[], wr=[dtpfm.key()], tag="l2a_dtpz")
    for tt in range(16):
        for half, dst in ((0, dtpfm), (1, afm)):
            pt = PS[6 + half]
            b.op("tensor", (lambda tt=tt, half=half, pt=pt: lambda e: e.transpose(pt.ap(F32, 128, pn=4), dtt.ap(F32, 4, coff=tt * 8 + half * 4), idf1.ap(F32, 128)))(),
                 rd=[dtt.key(), idf1.key()], wr=[pt.key()], tag=f"l2a_dT{tt}_{half}")
            b.op("scalar", (lambda tt=tt, pt=pt, dst=dst: lambda e: e.activation(out=dst.ap(F32, 128, coff=tt * 128, pn=4), in_=pt.ap(F32, 128, pn=4), func=AF.Copy, bias=0.0, scale=1.0))(),
                 rd=[pt.key()], wr=[(dst.name, dst.off + tt * 512, dst.off + (tt + 1) * 512)], tag=f"l2a_dTe{tt}_{half}")
    prod = A.alloc("l2a_prod", 8192)
    zer = A.alloc("l2a_zer", 8192)
    b.op("vector", lambda e: e.memset(zer.ap(F32, T), 0.0), rd=[], wr=[zer.key()], tag="l2a_zer")
    b.op("vector", lambda e: e.tensor_tensor(out=prod.ap(F32, T, pn=4), in0=dtpfm.ap(F32, T, pn=4), in1=afm.ap(F32, T, pn=4), op=ALU.mult),
         rd=[dtpfm.key(), afm.key()], wr=[prod.key()], tag="l2a_pr")
    cumA = A.alloc("l2a_cumA", 8192)
    b.op("vector", lambda e: e.tensor_tensor_scan(out=cumA.ap(F32, T, pn=4), data0=prod.ap(F32, T, pn=4), data1=zer.ap(F32, T, pn=4), initial=0.0, op0=ALU.add, op1=ALU.add),
         rd=[prod.key(), zer.key()], wr=[cumA.key()], tag="l2a_cumA")
    A.free("l2a_prod", "l2a_afm")
    if getattr(env, "l2a_cut", 99) <= 5:
        A.reset(m0)
        return
    gpt = env.ST.alloc("l2a_gpt", 512)
    b.dma("sync", lambda e: e.dma_start(out=gpt.ap(F32, 128, pn=8), in_=d["l2a_gpat"][:, :]), rd=[("l2a_gpat", 0, 1)], wr=[gpt.key()], tag="l2a_ldgp")
    thc = env.ST.alloc("l2a_thc", 512)
    b.dma("sync", lambda e: e.dma_start(out=thc.ap(F32, 1), in_=d["l2a_thetal"][:, :]), rd=[("l2a_thetal", 0, 1)], wr=[thc.key(F32, 1)], tag="l2a_ldth")
    b.op("scalar", lambda e: e.activation(out=thc.ap(F32, 1, coff=1), in_=thc.ap(F32, 1), func=AF.Exp, bias=0.0, scale=1.0),
         rd=[thc.key(F32, 1)], wr=[thc.key(F32, 1, coff=1)], tag="l2a_the")

    rate = A.alloc("l2a_rate", 8192)
    for ch in range(4):
        b.op("tensor", (lambda ch=ch: lambda e: e.matmul(PS[0].ap(F32, 512), gpt.ap(F32, 128, pn=8), dtpfm.ap(F32, 512, coff=ch * 512, pn=8), start=True, stop=True))(),
             rd=[gpt.key(), dtpfm.key()], wr=[PS[0].key()], tag=f"l2a_rm{ch}")
        b.op("scalar", (lambda ch=ch: lambda e: e.activation(out=rate.ap(F32, 512, coff=ch * 512), in_=PS[0].ap(F32, 512), func=AF.Copy, bias=0.0, scale=1.0))(),
             rd=[PS[0].key()], wr=[rate.key(F32, 512, coff=ch * 512)], tag=f"l2a_rme{ch}")
    b.op("vector", lambda e: e.tensor_scalar(out=rate.ap(F32, T), in0=rate.ap(F32, T), scalar1=thc.ap(F32, 1, coff=1), scalar2=None, op0=ALU.mult),
         rd=[rate.key(), thc.key(F32, 1, coff=1)], wr=[rate.key()], tag="l2a_rth")
    ang = A.alloc("l2a_ang", 8192)
    b.op("vector", lambda e: e.tensor_tensor_scan(out=ang.ap(F32, T), data0=rate.ap(F32, T), data1=zer.ap(F32, T), initial=0.0, op0=ALU.add, op1=ALU.add),
         rd=[rate.key(), zer.key()], wr=[ang.key()], tag="l2a_angs")
    A.free("l2a_rate")
    zer2 = zer
    mint = A.alloc("l2a_mint", 8192)
    cosf = A.alloc("l2a_cosf", 8192)
    sinf = A.alloc("l2a_sinf", 8192)
    TWO_PI = 2.0 * math.pi
    for (dst, shift, nm) in ((sinf, 0.0, "s"), (cosf, 0.25, "c")):
        b.op("vector", (lambda dst=dst, shift=shift: lambda e: e.tensor_scalar(out=dst.ap(F32, T), in0=ang.ap(F32, T), scalar1=1.0 / TWO_PI, scalar2=shift, op0=ALU.mult, op1=ALU.add))(),
             rd=[ang.key()], wr=[dst.key()], tag=f"l2a_m{nm}")
        b.op("vector", (lambda dst=dst: lambda e: e.tensor_copy(out=mint.ap(I32, T), in_=dst.ap(F32, T)))(),
             rd=[dst.key()], wr=[mint.key()], tag=f"l2a_mi{nm}")
        b.op("vector", (lambda: lambda e: e.tensor_copy(out=zer2.ap(F32, T), in_=mint.ap(I32, T)))(),
             rd=[mint.key()], wr=[zer2.key()], tag=f"l2a_mf{nm}")
        b.op("vector", (lambda dst=dst: lambda e: e.tensor_tensor(out=dst.ap(F32, T), in0=dst.ap(F32, T), in1=zer2.ap(F32, T), op=ALU.subtract))(),
             rd=[dst.key(), zer2.key()], wr=[dst.key()], tag=f"l2a_fr{nm}")
        b.op("vector", (lambda dst=dst: lambda e: e.tensor_scalar(out=zer2.ap(F32, T), in0=dst.ap(F32, T), scalar1=0.0, scalar2=None, op0=ALU.is_lt))(),
             rd=[dst.key()], wr=[zer2.key()], tag=f"l2a_fneg{nm}")
        b.op("vector", (lambda dst=dst: lambda e: e.tensor_tensor(out=dst.ap(F32, T), in0=dst.ap(F32, T), in1=zer2.ap(F32, T), op=ALU.add))(),
             rd=[dst.key(), zer2.key()], wr=[dst.key()], tag=f"l2a_ffix{nm}")
        b.op("scalar", (lambda dst=dst: lambda e: e.activation(out=dst.ap(F32, T), in_=dst.ap(F32, T), func=AF.Sin, bias=fc(env, -math.pi), scale=fc(env, TWO_PI)))(),
             rd=[dst.key(), fck(env, -math.pi), fck(env, TWO_PI)], wr=[dst.key()], tag=f"l2a_sin{nm}")
        b.op("vector", (lambda dst=dst: lambda e: e.tensor_scalar(out=dst.ap(F32, T), in0=dst.ap(F32, T), scalar1=-1.0, scalar2=None, op0=ALU.mult))(),
             rd=[dst.key()], wr=[dst.key()], tag=f"l2a_neg{nm}")
    A.free("l2a_mint", "l2a_ang", "l2a_zer", "l2a_idf1")
    cost = A.alloc("l2a_cost", 8192)
    sint = A.alloc("l2a_sint", 8192)
    csb = A.alloc("l2a_csb", 8192)
    for (src, dstt, nm) in ((cosf, cost, "c"), (sinf, sint, "s")):
        b.op("vector", (lambda src=src: lambda e: e.tensor_copy(out=csb.ap(BF16, T), in_=src.ap(F32, T)))(),
             rd=[src.key()], wr=[csb.key()], tag=f"l2a_cst{nm}")
        for tt in range(16):
            pt = PS[6 + tt % 2]
            b.op("tensor", (lambda tt=tt, pt=pt: lambda e: e.transpose(pt.ap(BF16, 128), csb.ap(BF16, 128, coff=tt * 128), ident_bf(env)))(),
                 rd=[csb.key(), env.ident.key()], wr=[pt.key()], tag=f"l2a_ct{nm}{tt}")
            b.op("scalar", (lambda tt=tt, pt=pt, dstt=dstt: lambda e: e.activation(out=dstt.ap(F32, 128, coff=tt * 128), in_=pt.ap(BF16, 128), func=AF.Copy, bias=0.0, scale=1.0))(),
                 rd=[pt.key()], wr=[(dstt.name, dstt.off + tt * 512, dstt.off + (tt + 1) * 512)], tag=f"l2a_cte{nm}{tt}")
    if "l2d_cost" in env.dram:
        emit_dbg(b, env, "l2d_cost", cost, dtype=F32)
        emit_dbg(b, env, "l2d_sint", sint, dtype=F32)
    A.free("l2a_cosf", "l2a_sinf", "l2a_csb")
    if getattr(env, "l2a_cut", 99) <= 6:
        A.reset(m0)
        return
    nrm = env.ST.alloc("l2a_nrm", 1024)
    nbc = A.alloc("l2a_nbc", 8192)
    nrow = A.alloc("l2a_nrow", 8192, top=True)
    for i, nm in enumerate(("normB", "normC", "biasB", "biasC")):
        b.dma("gpsimd", (lambda i=i, nm=nm: lambda e: e.dma_start(out=nrow.ap(BF16, 512, coff=i * 1024, pn=1), in_=d[f"l2a_{nm}"][:, :]))(),
              rd=[(f"l2a_{nm}", 0, 1)], wr=[(nrow.name, nrow.off + i * 2048, nrow.off + (i + 1) * 2048)], tag=f"l2a_ldn{i}")
    nbcB = A.alloc("l2a_nbcB", 2048)
    nbcC = A.alloc("l2a_nbcC", 2048)
    bbcB = A.alloc("l2a_bbcB", 2048)
    bbcC = A.alloc("l2a_bbcC", 2048)
    for i, dst in enumerate((nbcB, nbcC, bbcB, bbcC)):
        b.op("tensor", (lambda i=i: lambda e: e.matmul(PS[5].ap(F32, 512), ones_row_bf(env), nrow.ap(BF16, 512, coff=i * 1024, pn=1), start=True, stop=True))(),
             rd=[onesrowbf_key(env), (nrow.name, nrow.off + i * 2048, nrow.off + (i + 1) * 2048)], wr=[PS[5].key()], tag=f"l2a_nb{i}")
        b.op("vector", (lambda dst=dst: lambda e: e.tensor_copy(out=dst.ap(BF16, 512), in_=PS[5].ap(F32, 512)))(),
             rd=[PS[5].key()], wr=[dst.key()], tag=f"l2a_nbe{i}")
    rsc2 = A.alloc("l2a_rsc", 1024)
    for (src, nw, bw, nm) in ((Bt, nbcB, bbcB, "B"), (Ct, nbcC, bbcC, "C")):
        for tt in range(16):
            for g in range(4):
                base = tt * 512 + g * 128
                xap = src.ap(BF16, 128, coff=base)
                xk = src.key(BF16, 128, coff=base)
                b.op("vector", (lambda xap=xap, g=g: lambda e: e.scalar_tensor_tensor(out=rsc2.ap(F32, 128), in0=xap, scalar=1.0, in1=xap, op0=ALU.mult, op1=ALU.mult, accum_out=nrm.ap(F32, 1, coff=g)))(),
                     rd=[xk], wr=[rsc2.key(), nrm.key(F32, 4)], tag=f"l2a_{nm}n{tt}_{g}")
            b.op("scalar", (lambda: lambda e: e.activation(out=nrm.ap(F32, 4, coff=4), in_=nrm.ap(F32, 4), func=AF.Ln, bias=fc(env, 1.28e-3), scale=fc(env, 1.0 / 128)))(),
                 rd=[nrm.key(F32, 4), fck(env, 1.28e-3), fck(env, 1.0 / 128)], wr=[nrm.key(F32, 4, coff=4)], tag=f"l2a_{nm}l{tt}")
            b.op("scalar", (lambda: lambda e: e.activation(out=nrm.ap(F32, 4, coff=4), in_=nrm.ap(F32, 4, coff=4), func=AF.Exp, bias=0.0, scale=fc(env, -0.5)))(),
                 rd=[nrm.key(F32, 4, coff=4), fck(env, -0.5)], wr=[nrm.key(F32, 4, coff=4)], tag=f"l2a_{nm}e{tt}")
            for g in range(4):
                base = tt * 512 + g * 128
                xap = src.ap(BF16, 128, coff=base)
                xk = src.key(BF16, 128, coff=base)
                b.op("vector", (lambda xap=xap, g=g: lambda e: e.tensor_scalar(out=xap, in0=xap, scalar1=nrm.ap(F32, 1, coff=4 + g), scalar2=None, op0=ALU.mult))(),
                     rd=[xk, nrm.key(F32, 4, coff=4)], wr=[xk], tag=f"l2a_{nm}s{tt}_{g}")
                b.op("vector", (lambda xap=xap, nw=nw, g=g: lambda e: e.tensor_tensor(out=xap, in0=xap, in1=nw.ap(BF16, 128, coff=g * 128), op=ALU.mult))(),
                     rd=[xk, nw.key()], wr=[xk], tag=f"l2a_{nm}w{tt}_{g}")
                b.op("vector", (lambda xap=xap, bw=bw, g=g: lambda e: e.tensor_tensor(out=xap, in0=xap, in1=bw.ap(BF16, 128, coff=g * 128), op=ALU.add))(),
                     rd=[xk, bw.key()], wr=[xk], tag=f"l2a_{nm}b{tt}_{g}")
                v4 = src.ap(BF16, 128, coff=base).rearrange("p (r eo m) -> p r eo m", r=2, eo=2)
                cosv = cost.ap(F32, 32, coff=tt * 128 + g * 32).unsqueeze(1).to_broadcast([128, 2, 32])
                sinv = sint.ap(F32, 32, coff=tt * 128 + g * 32).unsqueeze(1).to_broadcast([128, 2, 32])
                ck = (cost.name, cost.off + tt * 512, cost.off + (tt + 1) * 512)
                sk = (sint.name, sint.off + tt * 512, sint.off + (tt + 1) * 512)
                t1 = rsc2.ap(BF16, 64).rearrange("p (r m) -> p r m", r=2)
                t2 = rsc2.ap(BF16, 64, coff=64).rearrange("p (r m) -> p r m", r=2)
                t3 = rsc2.ap(BF16, 64, coff=128).rearrange("p (r m) -> p r m", r=2)
                k1 = rsc2.key(BF16, 64); k2 = rsc2.key(BF16, 64, coff=64); k3 = rsc2.key(BF16, 64, coff=128)
                re = v4[:, :, 0, :]; im = v4[:, :, 1, :]
                if "l2d_norot" in env.dram:
                    continue
                b.op("vector", (lambda t1=t1, re=re, cosv=cosv: lambda e: e.tensor_tensor(out=t1, in0=re, in1=cosv, op=ALU.mult))(), rd=[xk, ck], wr=[k1], tag=f"l2a_{nm}r1{tt}_{g}")
                b.op("vector", (lambda t2=t2, re=re, sinv=sinv: lambda e: e.tensor_tensor(out=t2, in0=re, in1=sinv, op=ALU.mult))(), rd=[xk, sk], wr=[k2], tag=f"l2a_{nm}r2{tt}_{g}")
                b.op("vector", (lambda t3=t3, im=im, sinv=sinv: lambda e: e.tensor_tensor(out=t3, in0=im, in1=sinv, op=ALU.mult))(), rd=[xk, sk], wr=[k3], tag=f"l2a_{nm}r3{tt}_{g}")
                b.op("vector", (lambda re=re, t1=t1, t3=t3: lambda e: e.tensor_tensor(out=re, in0=t1, in1=t3, op=ALU.subtract))(), rd=[k1, k3], wr=[xk], tag=f"l2a_{nm}r4{tt}_{g}")
                b.op("vector", (lambda t3=t3, im=im, cosv=cosv: lambda e: e.tensor_tensor(out=t3, in0=im, in1=cosv, op=ALU.mult))(), rd=[xk, ck], wr=[k3], tag=f"l2a_{nm}r5{tt}_{g}")
                b.op("vector", (lambda im=im, t2=t2, t3=t3: lambda e: e.tensor_tensor(out=im, in0=t2, in1=t3, op=ALU.add))(), rd=[k2, k3], wr=[xk], tag=f"l2a_{nm}r6{tt}_{g}")
    A.free("l2a_nbcB", "l2a_nbcC", "l2a_bbcB", "l2a_bbcC", "l2a_nrow")
    if getattr(env, "l2a_cut", 99) <= 7:
        A.reset(m0)
        return
    b.dma("sync", lambda e: e.dma_start(out=d["l2a_spill"][1, :, 0:8192], in_=Bt.ap(BF16, 8192)), rd=[Bt.key()], wr=[("l2a_spill", 1, 2)], tag="l2a_Bsp")
    b.dma("sync", lambda e: e.dma_start(out=d["l2a_spill"][1, :, 8192:16384], in_=Ct.ap(BF16, 8192)), rd=[Ct.key()], wr=[("l2a_spill", 1, 2)], tag="l2a_Csp")
    A.free("l2a_Bt", "l2a_Ct", "l2a_cost", "l2a_sint", "l2a_rsc")
    b.dma("sync", lambda e: e.dma_start(out=d["l2a_spill2"][0, 0:4, :], in_=cumA.ap(F32, T, pn=4)),
          rd=[cumA.key()], wr=[("l2a_spill2", 0, 1)], tag="l2a_casp")
    b.dma("sync", lambda e: e.dma_start(out=d["l2a_spill2"][0, 4:8, :], in_=dtpfm.ap(F32, T, pn=4)),
          rd=[dtpfm.key()], wr=[("l2a_spill2", 0, 1)], tag="l2a_dtsp")
    A.free("l2a_cumA", "l2a_dtpfm", "l2a_dtt")
    env.l2a_state = dict(xpt=xpt)
    emit_l2a_xup(b, env)
    A.reset(m0)
    emit_l2a_rest(b, env)


BLOCKS["l2a"] = (emit_l2a, dram_l2a, host_l2a)


def emit_l2a_xup(b, env):
    A, d, PS = env.arena, env.dram, env.PS
    st = env.l2a_state
    xpt = st["xpt"]
    xpfm = A.alloc("l2a_xpfm", 16384 * 2)
    for tt in range(16):
        for ft in range(8):
            pt = PS[6 + ft % 2]
            b.op("tensor", (lambda tt=tt, ft=ft, pt=pt: lambda e: e.transpose(pt.ap(BF16, 128), xpt.ap(BF16, 128, coff=tt * 1024 + ft * 128), ident_bf(env)))(),
                 rd=[xpt.key(BF16, 128, coff=tt * 1024 + ft * 128), env.ident.key()], wr=[pt.key()], tag=f"l2a_xT{tt}_{ft}")
            b.op("scalar", (lambda tt=tt, ft=ft, pt=pt: lambda e: e.activation(out=xpfm.ap(BF16, 128, coff=ft * 2048 + tt * 128), in_=pt.ap(BF16, 128), func=AF.Copy, bias=0.0, scale=1.0))(),
                 rd=[pt.key()], wr=[xpfm.key(BF16, 128, coff=ft * 2048 + tt * 128)], tag=f"l2a_xTe{tt}_{ft}")
    b.dma("sync", lambda e: e.dma_start(out=d["l2a_spill"][2], in_=xpt.ap(BF16, 16384)),
          rd=[xpt.key()], wr=[("l2a_spill", 2, 3)], tag="l2a_xpsp")
    A.free("l2a_xpt")
    Xvt = A.alloc("l2a_Xvt", 2 * 2048 * 2)
    env.wyd = A.alloc("l2a_wyd", 16384)
    Ahs = A.alloc("l2a_Ahs", 1024)
    Bhs = A.alloc("l2a_Bhs", 2048)
    for half in range(2):
        b.dma("gpsimd", (lambda half=half: lambda e: e.dma_start(out=Ahs.ap(BF16, 512, p0=half * 64, pn=64), in_=d["l2a_xupA"][:, :]))(),
              rd=[("l2a_xupA", 0, 1)], wr=[Ahs.key()], tag=f"l2a_ldAh{half}")
        b.dma("gpsimd", (lambda half=half: lambda e: e.dma_start(out=Bhs.ap(BF16, 1024, p0=half * 64, pn=64), in_=d["l2a_xupB"][:, :]))(),
              rd=[("l2a_xupB", 0, 1)], wr=[Bhs.key()], tag=f"l2a_ldBh{half}")
    Mr = A.alloc("l2a_Mr", 16384)
    for ex in range(64):
        for r in range(2):
            outv = Mr.ap(BF16, 64, coff=(ex * 2 + r) * 64).rearrange("p (a c) -> p a c", a=8)
            in0 = Ahs.ap(BF16, 8, coff=ex * 8).unsqueeze(2).to_broadcast([128, 8, 8])
            in1 = Bhs.ap(BF16, 16, coff=ex * 16).rearrange("p (a r) -> p r a", r=2)[:, r, :].unsqueeze(1).to_broadcast([128, 8, 8])
            b.op("vector", (lambda outv=outv, in0=in0, in1=in1, ex=ex, r=r: lambda e: e.tensor_tensor(out=outv, in0=in0, in1=in1, op=ALU.mult))(),
                 rd=[Ahs.key(), Bhs.key()], wr=[Mr.key(BF16, 64, coff=(ex * 2 + r) * 64)], tag=f"l2a_Mb{ex}_{r}")
    rts = env.ST.alloc("l2a_rts", 512)
    for half in range(2):
        b.dma("gpsimd", (lambda half=half: lambda e: e.dma_start(out=rts.ap(BF16, 64, p0=half * 64, pn=64), in_=d["l2a_xuprt"][:, :]))(),
              rd=[("l2a_xuprt", 0, 1)], wr=[rts.key()], tag=f"l2a_ldrt{half}")
    wsc = A.alloc("l2a_wsc", 2048)
    xsc = env.ST.alloc("l2a_xsc", 512)
    bcast_scalar(b, env, "l2a_xupsc", xsc, 0)
    xbr = env.ST.alloc("l2a_xbr", 512)
    b.dma("gpsimd", lambda e: e.dma_start(out=xbr.ap(BF16, 128, pn=1), in_=d["l2a_xupbi"][:, :]),
          rd=[("l2a_xupbi", 0, 1)], wr=[xbr.key()], tag="l2a_ldxbi")
    b.op("tensor", lambda e: e.matmul(PS[5].ap(F32, 128), ones_row_bf(env), xbr.ap(BF16, 128, pn=1), start=True, stop=True),
         rd=[onesrowbf_key(env), xbr.key()], wr=[PS[5].key()], tag="l2a_xbb")
    xbb = A.alloc("l2a_xbb", 512)
    b.op("vector", lambda e: e.tensor_copy(out=xbb.ap(BF16, 128), in_=PS[5].ap(F32, 128)),
         rd=[PS[5].key()], wr=[xbb.key()], tag="l2a_xbbe")
    for tt in range(16):
        for h in range(16):
            hb = (h % 2) * 64
            ft = h // 2
            lhs = xpfm.ap(BF16, 128, coff=ft * 2048 + tt * 128, p0=hb, pn=64)
            lk = xpfm.key(BF16, 128, coff=ft * 2048 + tt * 128)
            b.op("tensor", (lambda lhs=lhs, hb=hb: lambda e: e.matmul(PS[5].ap(F32, 64), lhs, rts.ap(BF16, 64, p0=hb, pn=64), start=True, stop=True, tile_position=(hb, 0)))(),
                 rd=[lk, rts.key()], wr=[PS[5].key()], tag=f"l2a_rt{tt}_{h}")
            lg = wsc.ap(F32, 64); lgk = wsc.key(F32, 64)
            b.op("vector", (lambda lg=lg: lambda e: e.tensor_copy(out=lg, in_=PS[5].ap(F32, 64)))(),
                 rd=[PS[5].key()], wr=[lgk], tag=f"l2a_lg{tt}_{h}")
            m1 = wsc.ap(F32, 1, coff=64); m1k = wsc.key(F32, 1, coff=64)
            b.op("vector", (lambda lg=lg, m1=m1: lambda e: e.tensor_reduce(out=m1, in_=lg, axis=AX.X, op=ALU.max))(),
                 rd=[lgk], wr=[m1k], tag=f"l2a_m1{tt}_{h}")
            oh1 = wsc.ap(F32, 64, coff=72); oh1k = wsc.key(F32, 64, coff=72)
            b.op("vector", (lambda lg=lg, oh1=oh1, m1=m1: lambda e: e.tensor_scalar(out=oh1, in0=lg, scalar1=m1, scalar2=None, op0=ALU.is_ge))(),
                 rd=[lgk, m1k], wr=[oh1k], tag=f"l2a_oh1{tt}_{h}")
            p2_ = wsc.ap(F32, 64, coff=136); p2k = wsc.key(F32, 64, coff=136)
            b.op("vector", (lambda lg=lg, p2_=p2_, oh1=oh1: lambda e: e.scalar_tensor_tensor(out=p2_, in0=oh1, scalar=-1e30, in1=lg, op0=ALU.mult, op1=ALU.add))(),
                 rd=[oh1k, lgk], wr=[p2k], tag=f"l2a_p2{tt}_{h}")
            m2 = wsc.ap(F32, 1, coff=65); m2k = wsc.key(F32, 1, coff=65)
            b.op("vector", (lambda p2_=p2_, m2=m2: lambda e: e.tensor_reduce(out=m2, in_=p2_, axis=AX.X, op=ALU.max))(),
                 rd=[p2k], wr=[m2k], tag=f"l2a_m2{tt}_{h}")
            oh2 = wsc.ap(F32, 64, coff=200); oh2k = wsc.key(F32, 64, coff=200)
            b.op("vector", (lambda p2_=p2_, oh2=oh2, m2=m2: lambda e: e.tensor_scalar(out=oh2, in0=p2_, scalar1=m2, scalar2=None, op0=ALU.is_ge))(),
                 rd=[p2k, m2k], wr=[oh2k], tag=f"l2a_oh2{tt}_{h}")
            dd = wsc.ap(F32, 1, coff=66); ddk = wsc.key(F32, 1, coff=66)
            b.op("vector", (lambda m1=m1, m2=m2, dd=dd: lambda e: e.tensor_tensor(out=dd, in0=m2, in1=m1, op=ALU.subtract))(),
                 rd=[m1k, m2k], wr=[ddk], tag=f"l2a_dd{tt}_{h}")
            b.op("scalar", (lambda dd=dd: lambda e: e.activation(out=dd, in_=dd, func=AF.Sigmoid, bias=0.0, scale=fc(env, -1.0)))(),
                 rd=[ddk, fck(env, -1.0)], wr=[ddk], tag=f"l2a_p1{tt}_{h}")
            d2 = wsc.ap(F32, 1, coff=67); d2k = wsc.key(F32, 1, coff=67)
            b.op("vector", (lambda dd=dd, d2=d2: lambda e: e.tensor_scalar(out=d2, in0=dd, scalar1=-1.0, scalar2=1.0, op0=ALU.mult, op1=ALU.add))(),
                 rd=[ddk], wr=[d2k], tag=f"l2a_p2v{tt}_{h}")
            wv = wsc.ap(F32, 64, coff=264)
            wvk = wsc.key(F32, 64, coff=264)
            b.op("vector", (lambda oh1=oh1, dd=dd: lambda e: e.tensor_scalar(out=oh1, in0=oh1, scalar1=dd, scalar2=None, op0=ALU.mult))(),
                 rd=[oh1k, ddk], wr=[oh1k], tag=f"l2a_w1{tt}_{h}")
            b.op("vector", (lambda oh2=oh2, d2=d2: lambda e: e.tensor_scalar(out=oh2, in0=oh2, scalar1=d2, scalar2=None, op0=ALU.mult))(),
                 rd=[oh2k, d2k], wr=[oh2k], tag=f"l2a_w2{tt}_{h}")
            b.op("vector", (lambda oh1=oh1, oh2=oh2, wv=wv: lambda e: e.tensor_tensor(out=wv, in0=oh1, in1=oh2, op=ALU.add))(),
                 rd=[oh1k, oh2k], wr=[wvk], tag=f"l2a_wf{tt}_{h}")
            for eg in range(16):
                bk = eg % 4
                b.op("tensor", (lambda lhs=lhs, hb=hb, eg=eg, bk=bk: lambda e: e.matmul(PS[bk].ap(F32, 512), lhs, Mr.ap(BF16, 512, coff=eg * 512, p0=hb, pn=64), start=True, stop=True, tile_position=(hb, 0)))(),
                     rd=[lk, Mr.key()], wr=[PS[bk].key()], tag=f"l2a_xm{tt}_{h}_{eg}")
                w4 = wsc.ap(F32, 4, coff=264 + eg * 4).unsqueeze(2).to_broadcast([128, 4, 128])
                ps3 = PS[bk].ap(F32, 512).rearrange("p (e c) -> p e c", e=4)
                wy3 = env.wyd.ap(BF16, 512, coff=eg * 512).rearrange("p (e c) -> p e c", e=4)
                b.op("vector", (lambda ps3=ps3, wy3=wy3, w4=w4, eg=eg: lambda e: e.tensor_tensor(out=wy3, in0=ps3, in1=w4, op=ALU.mult))(),
                     rd=[PS[bk].key(), wvk], wr=[env.wyd.key(BF16, 512, coff=eg * 512)], tag=f"l2a_xw{tt}_{h}_{eg}")
            width = 4096
            while width >= 128:
                if width == 128:
                    o = Xvt.ap(BF16, 128, coff=(tt % 2) * 2048 + h * 128)
                    ok2 = Xvt.key(BF16, 128, coff=(tt % 2) * 2048 + h * 128)
                else:
                    o = env.wyd.ap(BF16, width)
                    ok2 = env.wyd.key(BF16, width)
                b.op("vector", (lambda o=o, width=width: lambda e: e.tensor_tensor(out=o, in0=env.wyd.ap(BF16, width), in1=env.wyd.ap(BF16, width, coff=width), op=ALU.add))(),
                     rd=[env.wyd.key(BF16, 2 * width)], wr=[ok2], tag=f"l2a_xt{tt}_{h}_{width}")
                width //= 2
        xvap = Xvt.ap(BF16, 2048, coff=(tt % 2) * 2048)
        xvk = Xvt.key(BF16, 2048, coff=(tt % 2) * 2048)
        b.op("vector", (lambda xvap=xvap, xvk=xvk: lambda e: e.tensor_scalar(out=xvap, in0=xvap, scalar1=xsc.ap(F32, 1), scalar2=None, op0=ALU.mult))(),
             rd=[xvk, xsc.key(F32, 1)], wr=[xvk], tag=f"l2a_xs{tt}")
        xv3 = Xvt.ap(BF16, 2048, coff=(tt % 2) * 2048).rearrange("p (hh c) -> p hh c", hh=16)
        b.op("vector", (lambda xv3=xv3, xvk=xvk: lambda e: e.tensor_tensor(out=xv3, in0=xv3, in1=xbb.ap(BF16, 128).unsqueeze(1).to_broadcast([128, 16, 128]), op=ALU.add))(),
             rd=[xvk, xbb.key()], wr=[xvk], tag=f"l2a_xb{tt}")
        b.dma("sync", (lambda tt=tt, xvk=xvk: lambda e: e.dma_start(out=d["l2a_xvsp"][:, tt * 2048:(tt + 1) * 2048], in_=Xvt.ap(BF16, 2048, coff=(tt % 2) * 2048)))(),
              rd=[xvk], wr=[("l2a_xvsp", 0, 1)], tag=f"l2a_xvsp{tt}")



def emit_l2a_rest(b, env):
    A, d, PS = env.arena, env.dram, env.PS
    m1 = A.mark()
    Xvg = A.alloc("l2b_Xvg", 16384)
    ydfm = A.alloc("l2b_ydfm", 16384 * 2)
    identf = A.alloc("l2b_idf", 512, top=True)
    b.dma("sync", lambda e: e.dma_start(out=identf.ap(F32, 128), in_=d["c_ident_f32"][:, :]),
          rd=[("c_ident_f32", 0, 1)], wr=[identf.key()], tag="l2b_ldidf")
    Hs = A.alloc("l2b_H", 16 * 256, top=True)
    Hb = A.alloc("l2b_Hb", 16 * 128, top=True)
    tric = A.alloc("l2b_tri", 512, top=True)
    b.dma("gpsimd", lambda e: e.dma_start(out=tric.ap(BF16, 128), in_=d["l2a_tri"][:, :]),
          rd=[("l2a_tri", 0, 1)], wr=[tric.key()], tag="l2b_ldtri")
    dupm = A.alloc("l2b_dupm", 512, top=True)
    b.dma("gpsimd", lambda e: e.dma_start(out=dupm.ap(BF16, 128), in_=d["l2a_dupm"][:, :]),
          rd=[("l2a_dupm", 0, 1)], wr=[dupm.key()], tag="l2b_lddup")
    ydw = A.alloc("l2b_ydw", 512, top=True)
    b.dma("gpsimd", lambda e: e.dma_start(out=ydw.ap(BF16, 64), in_=d["l2a_ydw"][:, :]),
          rd=[("l2a_ydw", 0, 1)], wr=[ydw.key()], tag="l2b_ldydw")
    crow = A.alloc("l2b_crow", 8192, top=True)
    dtrow = A.alloc("l2b_dtrow", 8192, top=True)
    Bfm = A.alloc("l2b_Bfm", 4096, top=True)
    Cd0 = A.alloc("l2b_Cd0", 4096, top=True)
    Cd1 = A.alloc("l2b_Cd1", 4096, top=True)
    Btokg = A.alloc("l2b_Btokg", 4096, top=True)
    Ctokg = A.alloc("l2b_Ctokg", 4096, top=True)
    yfmh = A.alloc("l2b_yfmh", 4096, top=True)
    Dall = A.alloc("l2b_Dall", 4096, top=True)
    Dtmp = A.alloc("l2b_Dtmp", 512, top=True)
    Cdec = A.alloc("l2b_Cdec", 8192, top=True)
    wdall = A.alloc("l2b_wdall", 512, top=True)
    env.ST.free("l2a_sps", "l2a_gpt", "l2a_thc", "l2a_nrm", "l2a_dasc", "l2a_rts", "l2a_xsc", "l2a_xbr")
    ssc = env.ST.alloc("l2b_ssc", 2048)
    b.op("vector", lambda e: e.memset(Hs.ap(F32, 1024, pn=64), 0.0), rd=[], wr=[Hs.key()], tag="l2b_H0")
    b.op("vector", lambda e: e.memset(Hb.ap(BF16, 1024, pn=64), 0.0), rd=[], wr=[Hb.key()], tag="l2b_Hb0")
    for g in range(4):
        b.dma("sync", (lambda g=g: lambda e: e.dma_start(out=crow.ap(F32, T, pn=1), in_=d["l2a_spill2"][0, g:g + 1, :]))(),
              rd=[("l2a_spill2", 0, 1)], wr=[crow.key()], tag=f"l2b_cr{g}")
        b.dma("sync", (lambda g=g: lambda e: e.dma_start(out=dtrow.ap(F32, T, pn=1), in_=d["l2a_spill2"][0, 4 + g:5 + g, :]))(),
              rd=[("l2a_spill2", 0, 1)], wr=[dtrow.key()], tag=f"l2b_dr{g}")
        b.dma("sync", (lambda g=g: lambda e: e.dma_start(out=Btokg.ap(BF16, 2048).rearrange("p (t c) -> p t c", c=128), in_=d["l2a_spill"][1].rearrange("p (t c) -> p t c", c=512)[:, 0:16, g * 128:(g + 1) * 128]))(),
              rd=[("l2a_spill", 1, 2)], wr=[Btokg.key()], tag=f"l2b_Bt{g}")
        b.dma("sync", (lambda g=g: lambda e: e.dma_start(out=Ctokg.ap(BF16, 2048).rearrange("p (t c) -> p t c", c=128), in_=d["l2a_spill"][1].rearrange("p (t c) -> p t c", c=512)[:, 16:32, g * 128:(g + 1) * 128]))(),
              rd=[("l2a_spill", 1, 2)], wr=[Ctokg.key()], tag=f"l2b_Ct{g}")
        b.dma("sync", (lambda g=g: lambda e: e.dma_start(out=Xvg.ap(BF16, 8192).rearrange("p (t c) -> p t c", c=512), in_=d["l2a_xvsp"].rearrange("p (t c) -> p t c", c=2048)[:, :, g * 512:(g + 1) * 512]))(),
              rd=[("l2a_xvsp", 0, 1)], wr=[Xvg.key()], tag=f"l2b_xvg{g}")
        for tt in range(16):
            ptB = PS[6]
            b.op("tensor", (lambda tt=tt, ptB=ptB: lambda e: e.transpose(ptB.ap(BF16, 128), Btokg.ap(BF16, 128, coff=tt * 128), ident_bf(env)))(),
                 rd=[Btokg.key(BF16, 128, coff=tt * 128), env.ident.key()], wr=[ptB.key()], tag=f"l2b_BT{g}_{tt}")
            b.op("scalar", (lambda tt=tt, ptB=ptB: lambda e: e.activation(out=Bfm.ap(BF16, 128, coff=tt * 128), in_=ptB.ap(BF16, 128), func=AF.Copy, bias=0.0, scale=1.0))(),
                 rd=[ptB.key()], wr=[Bfm.key(BF16, 128, coff=tt * 128)], tag=f"l2b_BTe{g}_{tt}")
            ptC = PS[7]
            b.op("tensor", (lambda tt=tt, ptC=ptC: lambda e: e.transpose(ptC.ap(BF16, 128), Ctokg.ap(BF16, 128, coff=tt * 128), ident_bf(env)))(),
                 rd=[Ctokg.key(BF16, 128, coff=tt * 128), env.ident.key()], wr=[ptC.key()], tag=f"l2b_CT{g}_{tt}")
            for r in range(2):
                dst = Cd0 if r == 0 else Cd1
                b.op("scalar", (lambda tt=tt, ptC=ptC, r=r, dst=dst: lambda e: e.activation(out=dst.ap(BF16, 128, coff=tt * 128, p0=r * 64, pn=64), in_=ptC.ap(BF16, 128, p0=r * 64, pn=64), func=AF.Copy, bias=0.0, scale=1.0))(),
                     rd=[ptC.key()], wr=[dst.key(BF16, 128, coff=tt * 128)], tag=f"l2b_CTe{g}_{tt}_{r}")
            for r in range(2):
                dst = Cd0 if r == 0 else Cd1
                pb = PS[4 + r]
                b.op("tensor", (lambda tt=tt, r=r, dst=dst, pb=pb: lambda e: e.matmul(pb.ap(F32, 128), dupm.ap(BF16, 128, p0=r * 64, pn=64), dst.ap(BF16, 128, coff=tt * 128, p0=r * 64, pn=64), start=True, stop=True, tile_position=(r * 64, 0)))(),
                     rd=[dupm.key(), dst.key(BF16, 128, coff=tt * 128)], wr=[pb.key()], tag=f"l2b_Cdup{g}_{tt}_{r}")
                b.op("scalar", (lambda tt=tt, dst=dst, pb=pb: lambda e: e.activation(out=dst.ap(BF16, 128, coff=tt * 128), in_=pb.ap(F32, 128), func=AF.Copy, bias=0.0, scale=1.0))(),
                     rd=[pb.key()], wr=[dst.key(BF16, 128, coff=tt * 128)], tag=f"l2b_Cdue{g}_{tt}_{r}")
        for ch in range(16):
            crs = crow.ap(F32, 128, coff=ch * 128, pn=1)
            crk = crow.key(F32, 128, coff=ch * 128)
            b.op("tensor", (lambda crs=crs: lambda e: e.matmul(PS[4].ap(F32, 128), ones_row_f32(env), crs, start=True, stop=True))(),
                 rd=[onesrow_key(env), crk], wr=[PS[4].key()], tag=f"l2b_Db{g}_{ch}")
            b.op("tensor", (lambda crs=crs: lambda e: e.transpose(PS[5].ap(F32, 1), crs, identf.ap(F32, 1, pn=1)))(),
                 rd=[crk, identf.key()], wr=[PS[5].key()], tag=f"l2b_DT{g}_{ch}")
            b.op("scalar", (lambda: lambda e: e.activation(out=ssc.ap(F32, 1), in_=PS[5].ap(F32, 1, pn=128), func=AF.Copy, bias=0.0, scale=1.0))(),
                 rd=[PS[5].key()], wr=[ssc.key(F32, 1)], tag=f"l2b_Dn{g}_{ch}")
            b.op("scalar", (lambda: lambda e: e.activation(out=Dtmp.ap(F32, 128), in_=PS[4].ap(F32, 128), func=AF.Relu, bias=ssc.ap(F32, 1), scale=fc(env, -1.0)))(),
                 rd=[PS[4].key(), ssc.key(F32, 1), fck(env, -1.0)], wr=[Dtmp.key()], tag=f"l2b_Di{g}_{ch}")
            b.op("scalar", (lambda ch=ch: lambda e: e.activation(out=Dall.ap(BF16, 128, coff=ch * 128), in_=Dtmp.ap(F32, 128), func=AF.Exp, bias=0.0, scale=fc(env, -1.0)))(),
                 rd=[Dtmp.key(), fck(env, -1.0)], wr=[Dall.key(BF16, 128, coff=ch * 128)], tag=f"l2b_De{g}_{ch}")
            b.op("tensor", (lambda ch=ch: lambda e: e.transpose(PS[5].ap(F32, 1), dtrow.ap(F32, 128, coff=ch * 128, pn=1), identf.ap(F32, 1, pn=1)))(),
                 rd=[dtrow.key(F32, 128, coff=ch * 128), identf.key()], wr=[PS[5].key()], tag=f"l2b_dT{g}_{ch}")
            b.op("scalar", (lambda: lambda e: e.activation(out=ssc.ap(F32, 1, coff=2), in_=PS[5].ap(F32, 1, pn=128), func=AF.Copy, bias=0.0, scale=1.0))(),
                 rd=[PS[5].key()], wr=[ssc.key(F32, 1, coff=2)], tag=f"l2b_dTe{g}_{ch}")
            b.op("vector", (lambda ch=ch: lambda e: e.tensor_scalar(out=Dall.ap(BF16, 128, coff=ch * 128), in0=Dall.ap(BF16, 128, coff=ch * 128), scalar1=ssc.ap(F32, 1, coff=2), scalar2=None, op0=ALU.mult))(),
                 rd=[Dall.key(BF16, 128, coff=ch * 128), ssc.key(F32, 1, coff=2)], wr=[Dall.key(BF16, 128, coff=ch * 128)], tag=f"l2b_Dd{g}_{ch}")
            b.op("vector", (lambda ch=ch: lambda e: e.tensor_tensor(out=Dall.ap(BF16, 128, coff=ch * 128), in0=Dall.ap(BF16, 128, coff=ch * 128), in1=tric.ap(BF16, 128), op=ALU.mult))(),
                 rd=[Dall.key(BF16, 128, coff=ch * 128), tric.key()], wr=[Dall.key(BF16, 128, coff=ch * 128)], tag=f"l2b_Dm{g}_{ch}")
            if ch > 0:
                b.op("vector", (lambda ch=ch: lambda e: e.tensor_scalar(out=ssc.ap(F32, 1, coff=3, pn=1), in0=crow.ap(F32, 1, coff=ch * 128 - 1, pn=1), scalar1=-1.0, scalar2=None, op0=ALU.mult))(),
                     rd=[crow.key(F32, 1, coff=ch * 128 - 1)], wr=[ssc.key(F32, 1, coff=3)], tag=f"l2b_cs{g}_{ch}")
                b.op("scalar", (lambda crs=crs: lambda e: e.activation(out=ssc.ap(F32, 128, coff=8, pn=1), in_=crs, func=AF.Exp, bias=ssc.ap(F32, 1, coff=3, pn=1), scale=1.0))(),
                     rd=[crk, ssc.key(F32, 1, coff=3)], wr=[ssc.key(F32, 128, coff=8)], tag=f"l2b_id{g}_{ch}")
            else:
                b.op("vector", (lambda: lambda e: e.memset(ssc.ap(F32, 128, coff=8, pn=1), 0.0))(),
                     rd=[], wr=[ssc.key(F32, 128, coff=8)], tag=f"l2b_id{g}_{ch}")
                b.op("vector", (lambda: lambda e: e.memset(ssc.ap(F32, 1, coff=3, pn=1), 0.0))(),
                     rd=[], wr=[ssc.key(F32, 1, coff=3)], tag=f"l2b_cs{g}_{ch}")
            b.op("tensor", (lambda: lambda e: e.matmul(PS[4].ap(F32, 128), ones_row_f32(env), ssc.ap(F32, 128, coff=8, pn=1), start=True, stop=True))(),
                 rd=[ssc.key(F32, 128, coff=8), onesrow_key(env)], wr=[PS[4].key()], tag=f"l2b_idb{g}_{ch}")
            b.op("scalar", (lambda: lambda e: e.activation(out=ssc.ap(F32, 128, coff=136), in_=PS[4].ap(F32, 128), func=AF.Copy, bias=0.0, scale=1.0))(),
                 rd=[PS[4].key()], wr=[ssc.key(F32, 128, coff=136)], tag=f"l2b_idbe{g}_{ch}")
            for rp in range(2):
                srcC = Cd0 if rp == 0 else Cd1
                b.op("vector", (lambda rp=rp, ch=ch, srcC=srcC: lambda e: e.tensor_tensor(out=Cdec.ap(BF16, 128, coff=rp * 2048 + ch * 128), in0=srcC.ap(BF16, 128, coff=ch * 128), in1=ssc.ap(F32, 128, coff=136), op=ALU.mult))(),
                     rd=[srcC.key(BF16, 128, coff=ch * 128), ssc.key(F32, 128, coff=136)], wr=[Cdec.key(BF16, 128, coff=rp * 2048 + ch * 128)], tag=f"l2b_cd{g}_{ch}_{rp}")
            b.op("vector", (lambda ch=ch: lambda e: e.tensor_scalar(out=ssc.ap(F32, 1, coff=4, pn=1), in0=crow.ap(F32, 1, coff=ch * 128 + 127, pn=1), scalar1=1.0, scalar2=None, op0=ALU.mult))(),
                 rd=[crow.key(F32, 128, coff=ch * 128)], wr=[ssc.key(F32, 1, coff=4)], tag=f"l2b_ce{g}_{ch}")
            b.op("scalar", (lambda crs=crs: lambda e: e.activation(out=ssc.ap(F32, 128, coff=328, pn=1), in_=crs, func=AF.Exp, bias=ssc.ap(F32, 1, coff=4, pn=1), scale=fc(env, -1.0, pn=1)))(),
                 rd=[crk, ssc.key(F32, 1, coff=4), fck(env, -1.0)], wr=[ssc.key(F32, 128, coff=328)], tag=f"l2b_wd{g}_{ch}")
            b.op("vector", (lambda ch=ch: lambda e: e.tensor_tensor(out=ssc.ap(F32, 128, coff=328, pn=1), in0=ssc.ap(F32, 128, coff=328, pn=1), in1=dtrow.ap(F32, 128, coff=ch * 128, pn=1), op=ALU.mult))(),
                 rd=[ssc.key(F32, 128, coff=328), dtrow.key(F32, 128, coff=ch * 128)], wr=[ssc.key(F32, 128, coff=328)], tag=f"l2b_wdd{g}_{ch}")
            b.op("tensor", (lambda: lambda e: e.transpose(PS[4].ap(F32, 1), ssc.ap(F32, 128, coff=328, pn=1), identf.ap(F32, 1, pn=1)))(),
                 rd=[ssc.key(F32, 128, coff=328), identf.key()], wr=[PS[4].key()], tag=f"l2b_wdT{g}_{ch}")
            b.op("scalar", (lambda ch=ch: lambda e: e.activation(out=wdall.ap(F32, 1, coff=64 + ch), in_=PS[4].ap(F32, 1, pn=128), func=AF.Copy, bias=0.0, scale=1.0))(),
                 rd=[PS[4].key()], wr=[wdall.key(F32, 1, coff=64 + ch)], tag=f"l2b_wdTe{g}_{ch}")
            b.op("scalar", (lambda ch=ch: lambda e: e.activation(out=wdall.ap(F32, 1, coff=16 + ch, pn=1), in_=ssc.ap(F32, 1, coff=4, pn=1), func=AF.Exp, bias=ssc.ap(F32, 1, coff=3, pn=1), scale=1.0))(),
                 rd=[ssc.key(F32, 1, coff=4), ssc.key(F32, 1, coff=3)], wr=[wdall.key(F32, 1, coff=16 + ch)], tag=f"l2b_al{g}_{ch}")
        b.op("tensor", lambda e: e.matmul(PS[4].ap(F32, 16, pn=64), ones_row_f32(env)[:, 0:64], wdall.ap(F32, 16, coff=16, pn=1), start=True, stop=True),
             rd=[wdall.key(F32, 16, coff=16), onesrow_key(env)], wr=[PS[4].key()], tag=f"l2b_alb{g}")
        b.op("scalar", (lambda: lambda e: e.activation(out=wdall.ap(F32, 16, coff=32, pn=64), in_=PS[4].ap(F32, 16, pn=64), func=AF.Copy, bias=0.0, scale=1.0))(),
             rd=[PS[4].key()], wr=[wdall.key(F32, 16, coff=32)], tag=f"l2b_albe{g}")
        for hh in range(4):
            h = g * 4 + hh
            for ch in range(16):
                for rp in range(2):
                    ob = 2 + rp
                    srcC = Cd0 if rp == 0 else Cd1
                    for r in range(2):
                        sb = r
                        b.op("tensor", (lambda ch=ch, r=r, sb=sb, srcC=srcC: lambda e: e.matmul(PS[sb].ap(F32, 128), Bfm.ap(BF16, 128, coff=ch * 128, p0=r * 64, pn=64), srcC.ap(BF16, 128, coff=ch * 128, p0=r * 64, pn=64), start=True, stop=True, tile_position=(r * 64, 0)))(),
                             rd=[Bfm.key(BF16, 128, coff=ch * 128), srcC.key(BF16, 128, coff=ch * 128)], wr=[PS[sb].key()], tag=f"l2b_S{g}_{h}_{ch}_{rp}_{r}")
                        b.op("vector", (lambda sb=sb, ch=ch: lambda e: e.tensor_tensor(out=ssc.ap(BF16, 128, coff=528), in0=PS[sb].ap(F32, 128), in1=Dall.ap(BF16, 128, coff=ch * 128), op=ALU.mult))(),
                             rd=[PS[sb].key(), Dall.key(BF16, 128, coff=ch * 128)], wr=[ssc.key(BF16, 128, coff=528)], tag=f"l2b_P{g}_{h}_{ch}_{rp}_{r}")
                        b.op("tensor", (lambda ch=ch, h=h, r=r, rp=rp, ob=ob: lambda e: e.matmul(PS[ob].ap(F32, 128, p0=rp * 64, pn=64), Xvg.ap(BF16, 64, coff=ch * 512 + (h % 4) * 128 + r * 64), ssc.ap(BF16, 128, coff=528), start=(r == 0), stop=False))(),
                             rd=[Xvg.key(BF16, 64, coff=ch * 512 + (h % 4) * 128 + r * 64), ssc.key(BF16, 128, coff=528)], wr=[PS[ob].key()], tag=f"l2b_y{g}_{h}_{ch}_{rp}_{r}")
                    b.op("tensor", (lambda h=h, rp=rp, ob=ob, ch=ch: lambda e: e.matmul(PS[ob].ap(F32, 128, p0=rp * 64, pn=64), Hb.ap(BF16, 64, coff=h * 64, pn=64), Cdec.ap(BF16, 128, coff=rp * 2048 + ch * 128, pn=64), start=False, stop=True))(),
                         rd=[Hb.key(BF16, 64, coff=h * 64), Cdec.key(BF16, 128, coff=rp * 2048 + ch * 128)], wr=[PS[ob].key()], tag=f"l2b_yi{g}_{h}_{ch}_{rp}")
                    b.op("scalar", (lambda ch=ch, rp=rp, ob=ob: lambda e: e.activation(out=yfmh.ap(BF16, 128, coff=ch * 128, p0=rp * 64, pn=64), in_=PS[ob].ap(F32, 128, p0=rp * 64, pn=64), func=AF.Copy, bias=0.0, scale=1.0))(),
                         rd=[PS[ob].key()], wr=[yfmh.key(BF16, 128, coff=ch * 128)], tag=f"l2b_ye{g}_{h}_{ch}_{rp}")
                ob = 4
                for r in range(2):
                    b.op("vector", (lambda ch=ch, r=r: lambda e: e.tensor_scalar(out=ssc.ap(BF16, 64, coff=912), in0=Btokg.ap(BF16, 64, coff=ch * 128 + r * 64), scalar1=wdall.ap(F32, 1, coff=64 + ch), scalar2=None, op0=ALU.mult))(),
                         rd=[Btokg.key(BF16, 64, coff=ch * 128 + r * 64), wdall.key(F32, 1, coff=64 + ch)], wr=[ssc.key(BF16, 64, coff=912)], tag=f"l2b_kw{g}_{h}_{ch}_{r}")
                    b.op("tensor", (lambda ch=ch, h=h, r=r, ob=ob: lambda e: e.matmul(PS[ob].ap(F32, 64, pn=64), ssc.ap(BF16, 64, coff=912), Xvg.ap(BF16, 64, coff=ch * 512 + (h % 4) * 128 + r * 64), start=(r == 0), stop=(r == 1)))(),
                         rd=[ssc.key(BF16, 64, coff=912), Xvg.key(BF16, 64, coff=ch * 512 + (h % 4) * 128 + r * 64)], wr=[PS[ob].key()], tag=f"l2b_dH{g}_{h}_{ch}_{r}")
                b.op("vector", (lambda h=h, ch=ch: lambda e: e.tensor_scalar(out=Hs.ap(F32, 64, coff=h * 64, pn=64), in0=Hs.ap(F32, 64, coff=h * 64, pn=64), scalar1=wdall.ap(F32, 1, coff=32 + ch, pn=64), scalar2=None, op0=ALU.mult))(),
                     rd=[Hs.key(F32, 64, coff=h * 64), wdall.key(F32, 16, coff=32)], wr=[Hs.key(F32, 64, coff=h * 64)], tag=f"l2b_Hd{g}_{h}_{ch}")
                b.op("vector", (lambda h=h, ob=ob: lambda e: e.tensor_tensor(out=Hs.ap(F32, 64, coff=h * 64, pn=64), in0=PS[ob].ap(F32, 64, pn=64), in1=Hs.ap(F32, 64, coff=h * 64, pn=64), op=ALU.add))(),
                     rd=[PS[ob].key(), Hs.key(F32, 64, coff=h * 64)], wr=[Hs.key(F32, 64, coff=h * 64)], tag=f"l2b_Ha{g}_{h}_{ch}")
                b.op("scalar", (lambda h=h: lambda e: e.activation(out=Hb.ap(BF16, 64, coff=h * 64, pn=64), in_=Hs.ap(F32, 64, coff=h * 64, pn=64), func=AF.Copy, bias=0.0, scale=1.0))(),
                     rd=[Hs.key(F32, 64, coff=h * 64)], wr=[Hb.key(BF16, 64, coff=h * 64)], tag=f"l2b_Hc{g}_{h}_{ch}")
            for ch4 in range(4):
                bk = 5
                b.op("tensor", (lambda h=h, ch4=ch4, bk=bk: lambda e: e.matmul(PS[bk].ap(F32, 512, p0=(h % 2) * 64, pn=64), ydw.ap(BF16, 64), yfmh.ap(BF16, 512, coff=ch4 * 512), start=True, stop=True))(),
                     rd=[ydw.key(), yfmh.key(BF16, 512, coff=ch4 * 512)], wr=[PS[bk].key()], tag=f"l2b_yd{g}_{h}_{ch4}")
                b.op("scalar", (lambda h=h, ch4=ch4, bk=bk: lambda e: e.activation(out=ydfm.ap(BF16, 512, coff=(h // 2) * 2048 + ch4 * 512, p0=(h % 2) * 64, pn=64), in_=PS[bk].ap(F32, 512, p0=(h % 2) * 64, pn=64), func=AF.Copy, bias=0.0, scale=1.0))(),
                     rd=[PS[bk].key()], wr=[ydfm.key(BF16, 512, coff=(h // 2) * 2048 + ch4 * 512)], tag=f"l2b_yde{g}_{h}_{ch4}")
    A.free("l2b_crow", "l2b_dtrow", "l2b_Bfm", "l2b_Cd0", "l2b_Cd1", "l2b_Btokg", "l2b_Ctokg", "l2b_yfmh",
           "l2b_Dall", "l2b_Dtmp", "l2b_Cdec", "l2b_wdall", "l2b_tri", "l2b_dupm", "l2b_H", "l2b_Hb", "l2b_ydw", "l2b_Xvg")
    emit_l2a_post(b, env, identf)
    A.reset(m1)


def emit_l2a_post(b, env, identf):
    A, d, PS = env.arena, env.dram, env.PS
    env.ST.free("l2b_ssc")
    ydfm = env.arena.allocs["l2b_ydfm"]
    rowb = A.alloc("l2p_row", 4096, top=True)
    b.dma("gpsimd", lambda e: e.dma_start(out=rowb.ap(BF16, 1024, pn=1), in_=d["l2a_Drow"][:, :]),
          rd=[("l2a_Drow", 0, 1)], wr=[(rowb.name, rowb.off, rowb.off + 2048)], tag="l2p_ldD")
    b.dma("gpsimd", lambda e: e.dma_start(out=rowb.ap(BF16, 1024, coff=1024, pn=1), in_=d["l2a_preg"][:, :]),
          rd=[("l2a_preg", 0, 1)], wr=[(rowb.name, rowb.off + 2048, rowb.off + 4096)], tag="l2p_ldpg")
    Dbc = A.alloc("l2p_Dbc", 2048)
    pgbc = A.alloc("l2p_pgbc", 2048)
    for i, dst in enumerate((Dbc, pgbc)):
        for c in range(2):
            b.op("tensor", (lambda i=i, c=c: lambda e: e.matmul(PS[6].ap(F32, 512), ones_row_bf(env), rowb.ap(BF16, 512, coff=i * 1024 + c * 512, pn=1), start=True, stop=True))(),
                 rd=[onesrowbf_key(env), (rowb.name, rowb.off, rowb.off + 4096)], wr=[PS[6].key()], tag=f"l2p_bc{i}_{c}")
            b.op("vector", (lambda dst=dst, c=c: lambda e: e.tensor_copy(out=dst.ap(BF16, 512, coff=c * 512), in_=PS[6].ap(F32, 512)))(),
                 rd=[PS[6].key()], wr=[dst.key(BF16, 512, coff=c * 512)], tag=f"l2p_bce{i}_{c}")
    y2fm = A.alloc("l2p_y2fm", 32768)
    ztok = A.alloc("l2p_ztok", 2048, top=True)
    xptok = A.alloc("l2p_xptok", 2048, top=True)
    ygt = A.alloc("l2p_ygt", 2048, top=True)
    sgt = A.alloc("l2p_sgt", 2048, top=True)
    dump = A.alloc("l2p_dump", 2048, top=True)
    gsc = env.ST.alloc("l2p_gsc", 512)
    for tt in range(16):
        b.dma("sync", (lambda tt=tt: lambda e: e.dma_start(out=ztok.ap(BF16, 1024), in_=d["l2a_spill"][0, :, tt * 1024:(tt + 1) * 1024]))(),
              rd=[("l2a_spill", 0, 1)], wr=[ztok.key()], tag=f"l2p_ldz{tt}")
        b.dma("sync", (lambda tt=tt: lambda e: e.dma_start(out=xptok.ap(BF16, 1024), in_=d["l2a_spill"][2, :, tt * 1024:(tt + 1) * 1024]))(),
              rd=[("l2a_spill", 2, 3)], wr=[xptok.key()], tag=f"l2p_ldxp{tt}")
        for ft in range(8):
            pt = PS[6 + ft % 2]
            b.op("tensor", (lambda tt=tt, ft=ft, pt=pt: lambda e: e.transpose(pt.ap(BF16, 128), ydfm.ap(BF16, 128, coff=ft * 2048 + tt * 128), ident_bf(env)))(),
                 rd=[ydfm.key(BF16, 128, coff=ft * 2048 + tt * 128), env.ident.key()], wr=[pt.key()], tag=f"l2p_yT{tt}_{ft}")
            b.op("scalar", (lambda ft=ft, pt=pt: lambda e: e.activation(out=ygt.ap(BF16, 128, coff=ft * 128), in_=pt.ap(BF16, 128), func=AF.Copy, bias=0.0, scale=1.0))(),
                 rd=[pt.key()], wr=[ygt.key(BF16, 128, coff=ft * 128)], tag=f"l2p_yTe{tt}_{ft}")
        b.op("vector", lambda e: e.tensor_tensor(out=dump.ap(BF16, 1024), in0=xptok.ap(BF16, 1024), in1=Dbc.ap(BF16, 1024), op=ALU.mult),
             rd=[xptok.key(), Dbc.key()], wr=[dump.key()], tag=f"l2p_sk{tt}")
        b.op("vector", lambda e: e.tensor_tensor(out=ygt.ap(BF16, 1024), in0=ygt.ap(BF16, 1024), in1=dump.ap(BF16, 1024), op=ALU.add),
             rd=[ygt.key(), dump.key()], wr=[ygt.key()], tag=f"l2p_ska{tt}")
        b.op("vector", lambda e: e.scalar_tensor_tensor(out=dump.ap(BF16, 1024), in0=ygt.ap(BF16, 1024), scalar=1.0, in1=ygt.ap(BF16, 1024), op0=ALU.mult, op1=ALU.mult, accum_out=gsc.ap(F32, 1)),
             rd=[ygt.key()], wr=[dump.key(), gsc.key(F32, 1)], tag=f"l2p_sq{tt}")
        b.op("scalar", lambda e: e.activation(out=gsc.ap(F32, 1, coff=1), in_=gsc.ap(F32, 1), func=AF.Ln, bias=fc(env, 1e-5), scale=fc(env, 1.0 / 1024)),
             rd=[gsc.key(F32, 1), fck(env, 1e-5), fck(env, 1.0 / 1024)], wr=[gsc.key(F32, 1, coff=1)], tag=f"l2p_ln{tt}")
        b.op("scalar", lambda e: e.activation(out=gsc.ap(F32, 1, coff=1), in_=gsc.ap(F32, 1, coff=1), func=AF.Exp, bias=0.0, scale=fc(env, -0.5)),
             rd=[gsc.key(F32, 1, coff=1), fck(env, -0.5)], wr=[gsc.key(F32, 1, coff=1)], tag=f"l2p_rs{tt}")
        b.op("scalar", lambda e: e.activation(out=sgt.ap(BF16, 1024), in_=ztok.ap(BF16, 1024), func=AF.Sigmoid, bias=0.0, scale=1.0),
             rd=[ztok.key()], wr=[sgt.key()], tag=f"l2p_sg{tt}")
        b.op("vector", lambda e: e.tensor_scalar(out=ygt.ap(BF16, 1024), in0=ygt.ap(BF16, 1024), scalar1=gsc.ap(F32, 1, coff=1), scalar2=None, op0=ALU.mult),
             rd=[ygt.key(), gsc.key(F32, 1, coff=1)], wr=[ygt.key()], tag=f"l2p_ap1{tt}")
        b.op("vector", lambda e: e.tensor_tensor(out=ygt.ap(BF16, 1024), in0=ygt.ap(BF16, 1024), in1=pgbc.ap(BF16, 1024), op=ALU.mult),
             rd=[ygt.key(), pgbc.key()], wr=[ygt.key()], tag=f"l2p_ap2{tt}")
        b.op("vector", lambda e: e.tensor_tensor(out=ygt.ap(BF16, 1024), in0=ygt.ap(BF16, 1024), in1=sgt.ap(BF16, 1024), op=ALU.mult),
             rd=[ygt.key(), sgt.key()], wr=[ygt.key()], tag=f"l2p_ap3{tt}")
        for ft in range(8):
            pt = PS[6 + ft % 2]
            b.op("tensor", (lambda ft=ft, pt=pt: lambda e: e.transpose(pt.ap(BF16, 128), ygt.ap(BF16, 128, coff=ft * 128), ident_bf(env)))(),
                 rd=[ygt.key(BF16, 128, coff=ft * 128), env.ident.key()], wr=[pt.key()], tag=f"l2p_y2T{tt}_{ft}")
            b.op("scalar", (lambda tt=tt, ft=ft, pt=pt: lambda e: e.activation(out=y2fm.ap(BF16, 128, coff=ft * 2048 + tt * 128), in_=pt.ap(BF16, 128), func=AF.Copy, bias=0.0, scale=1.0))(),
                 rd=[pt.key()], wr=[y2fm.key(BF16, 128, coff=ft * 2048 + tt * 128)], tag=f"l2p_y2Te{tt}_{ft}")
    if "l2d_ydfm" in env.dram:
        emit_dbg(b, env, "l2d_ydfm", ydfm, dtype=BF16)
    if "l2d_y2fm" in env.dram:
        emit_dbg(b, env, "l2d_y2fm", y2fm, dtype=BF16)
    A.free("l2b_ydfm", "l2p_ztok", "l2p_xptok", "l2p_sgt", "l2p_dump", "l2p_Dbc", "l2p_pgbc", "l2p_row")
    accfm = A.alloc("l2p_acc", 32768)
    Me = A.alloc("l2p_Me", 16384)
    rtb = A.alloc("l2p_rtb", 1024)
    b.dma("gpsimd", lambda e: e.dma_start(out=rtb.ap(BF16, 512).rearrange("p (k n) -> p k n", n=64), in_=d["l2a_ouprt"].rearrange("(k p) n -> p k n", p=128)),
          rd=[("l2a_ouprt", 0, 1)], wr=[rtb.key()], tag="l2p_ldrt")
    wem = A.alloc("l2p_wem", 8192)
    wsc = env.ST.alloc("l2p_wsc", 2048)
    for tl in range(16):
        for kt in range(8):
            b.op("tensor", (lambda tl=tl, kt=kt: lambda e: e.matmul(PS[4].ap(F32, 64), y2fm.ap(BF16, 128, coff=kt * 2048 + tl * 128), rtb.ap(BF16, 64, coff=kt * 64), start=(kt == 0), stop=(kt == 7)))(),
                 rd=[y2fm.key(BF16, 128, coff=kt * 2048 + tl * 128), rtb.key()], wr=[PS[4].key()], tag=f"l2p_rt{tl}_{kt}")
        lg = wsc.ap(F32, 64); lgk = wsc.key(F32, 64)
        b.op("vector", (lambda lg=lg: lambda e: e.tensor_copy(out=lg, in_=PS[4].ap(F32, 64)))(),
             rd=[PS[4].key()], wr=[lgk], tag=f"l2p_lg{tl}")
        m1_ = wsc.ap(F32, 1, coff=64); m1k = wsc.key(F32, 1, coff=64)
        b.op("vector", (lambda lg=lg, m1_=m1_: lambda e: e.tensor_reduce(out=m1_, in_=lg, axis=AX.X, op=ALU.max))(),
             rd=[lgk], wr=[m1k], tag=f"l2p_m1{tl}")
        oh1 = wsc.ap(F32, 64, coff=72); oh1k = wsc.key(F32, 64, coff=72)
        b.op("vector", (lambda lg=lg, oh1=oh1, m1_=m1_: lambda e: e.tensor_scalar(out=oh1, in0=lg, scalar1=m1_, scalar2=None, op0=ALU.is_ge))(),
             rd=[lgk, m1k], wr=[oh1k], tag=f"l2p_oh1{tl}")
        p2_ = wsc.ap(F32, 64, coff=136); p2k = wsc.key(F32, 64, coff=136)
        b.op("vector", (lambda lg=lg, p2_=p2_, oh1=oh1: lambda e: e.scalar_tensor_tensor(out=p2_, in0=oh1, scalar=-1e30, in1=lg, op0=ALU.mult, op1=ALU.add))(),
             rd=[oh1k, lgk], wr=[p2k], tag=f"l2p_p2{tl}")
        m2_ = wsc.ap(F32, 1, coff=65); m2k = wsc.key(F32, 1, coff=65)
        b.op("vector", (lambda p2_=p2_, m2_=m2_: lambda e: e.tensor_reduce(out=m2_, in_=p2_, axis=AX.X, op=ALU.max))(),
             rd=[p2k], wr=[m2k], tag=f"l2p_m2{tl}")
        oh2 = wsc.ap(F32, 64, coff=200); oh2k = wsc.key(F32, 64, coff=200)
        b.op("vector", (lambda p2_=p2_, oh2=oh2, m2_=m2_: lambda e: e.tensor_scalar(out=oh2, in0=p2_, scalar1=m2_, scalar2=None, op0=ALU.is_ge))(),
             rd=[p2k, m2k], wr=[oh2k], tag=f"l2p_oh2{tl}")
        dd = wsc.ap(F32, 1, coff=66); ddk = wsc.key(F32, 1, coff=66)
        b.op("vector", (lambda m1_=m1_, m2_=m2_, dd=dd: lambda e: e.tensor_tensor(out=dd, in0=m2_, in1=m1_, op=ALU.subtract))(),
             rd=[m1k, m2k], wr=[ddk], tag=f"l2p_dd{tl}")
        b.op("scalar", (lambda dd=dd: lambda e: e.activation(out=dd, in_=dd, func=AF.Sigmoid, bias=0.0, scale=fc(env, -1.0)))(),
             rd=[ddk, fck(env, -1.0)], wr=[ddk], tag=f"l2p_p1{tl}")
        d2 = wsc.ap(F32, 1, coff=67); d2k = wsc.key(F32, 1, coff=67)
        b.op("vector", (lambda dd=dd, d2=d2: lambda e: e.tensor_scalar(out=d2, in0=dd, scalar1=-1.0, scalar2=1.0, op0=ALU.mult, op1=ALU.add))(),
             rd=[ddk], wr=[d2k], tag=f"l2p_p2v{tl}")
        b.op("vector", (lambda oh1=oh1, dd=dd: lambda e: e.tensor_scalar(out=oh1, in0=oh1, scalar1=dd, scalar2=None, op0=ALU.mult))(),
             rd=[oh1k, ddk], wr=[oh1k], tag=f"l2p_w1{tl}")
        b.op("vector", (lambda oh2=oh2, d2=d2: lambda e: e.tensor_scalar(out=oh2, in0=oh2, scalar1=d2, scalar2=None, op0=ALU.mult))(),
             rd=[oh2k, d2k], wr=[oh2k], tag=f"l2p_w2{tl}")
        wvb = wsc.ap(BF16, 64, coff=536)
        wvbk = wsc.key(BF16, 64, coff=536)
        b.op("vector", (lambda oh1=oh1, oh2=oh2, wvb=wvb: lambda e: e.tensor_tensor(out=wvb, in0=oh1, in1=oh2, op=ALU.add))(),
             rd=[oh1k, oh2k], wr=[wvbk], tag=f"l2p_wf{tl}")
        b.op("tensor", (lambda wvb=wvb: lambda e: e.transpose(PS[5].ap(BF16, 128, pn=64), wvb, ident_bf(env)))(),
             rd=[wvbk, env.ident.key()], wr=[PS[5].key()], tag=f"l2p_wT{tl}")
        b.op("scalar", (lambda tl=tl: lambda e: e.activation(out=wem.ap(F32, 128, coff=tl * 128, pn=64), in_=PS[5].ap(BF16, 128, pn=64), func=AF.Copy, bias=0.0, scale=1.0))(),
             rd=[PS[5].key()], wr=[wem.key(F32, 128, coff=tl * 128)], tag=f"l2p_wTe{tl}")
    b.dma("sync", lambda e: e.dma_start(out=d["l2a_wsp"][:, :], in_=wem.ap(F32, 2048, pn=64)),
          rd=[wem.key()], wr=[("l2a_wsp", 0, 1)], tag="l2p_wspill")
    wrow = [A.alloc(f"l2p_wrow{i}", 8192, top=True) for i in range(2)]
    Aef = [A.alloc(f"l2p_Aef{i}", 1024, top=True) for i in range(2)]
    Bef = [A.alloc(f"l2p_Bef{i}", 1024, top=True) for i in range(2)]
    wbcs = A.alloc("l2p_wbcs", 1024, top=True)
    tmps = A.alloc("l2p_tmps", 1024, top=True)
    b.op("vector", lambda e: e.memset(accfm.ap(BF16, 16384), 0.0), rd=[], wr=[accfm.key()], tag="l2p_acc0")
    for ee in range(64):
        sl = ee % 2
        b.dma("sync", (lambda ee=ee, sl=sl: lambda e: e.dma_start(out=Aef[sl].ap(F32, 256).rearrange("p (k o) -> p k o", o=32), in_=d["l2a_oupAe"][ee].rearrange("(k p) o -> p k o", p=128)))(),
              rd=[("l2a_oupAe", 0, 1)], wr=[Aef[sl].key()], tag=f"l2p_ldA{ee}")
        b.dma("sync", (lambda ee=ee, sl=sl: lambda e: e.dma_start(out=Bef[sl].ap(F32, 256).rearrange("p (k o) -> p k o", o=32), in_=d["l2a_oupBe"][ee].rearrange("(k p) o -> p k o", p=128)))(),
              rd=[("l2a_oupBe", 0, 1)], wr=[Bef[sl].key()], tag=f"l2p_ldB{ee}")
        for kt in range(8):
            outv = Me.ap(BF16, 1024, coff=kt * 1024).rearrange("p (o q) -> p o q", o=32)
            in0 = Aef[sl].ap(F32, 32, coff=kt * 32).unsqueeze(2).to_broadcast([128, 32, 32])
            in1 = Bef[sl].ap(F32, 32, coff=kt * 32).unsqueeze(1).to_broadcast([128, 32, 32])
            b.op("vector", (lambda outv=outv, in0=in0, in1=in1, sl=sl, kt=kt: lambda e: e.tensor_tensor(out=outv, in0=in0, in1=in1, op=ALU.mult))(),
                 rd=[Aef[sl].key(), Bef[sl].key()], wr=[Me.key(BF16, 1024, coff=kt * 1024)], tag=f"l2p_Mb{ee}_{kt}")
        b.dma("sync", (lambda ee=ee, sl=sl: lambda e: e.dma_start(out=wrow[sl].ap(F32, 2048, pn=1), in_=d["l2a_wsp"][ee:ee + 1, :]))(),
              rd=[("l2a_wsp", 0, 1)], wr=[wrow[sl].key()], tag=f"l2p_ldw{ee}")
        for tc in range(4):
            b.op("tensor", (lambda ee=ee, tc=tc, sl=sl: lambda e: e.matmul(PS[6].ap(F32, 512), ones_row_f32(env), wrow[sl].ap(F32, 512, coff=tc * 512, pn=1), start=True, stop=True))(),
                 rd=[onesrow_key(env), wrow[sl].key()], wr=[PS[6].key()], tag=f"l2p_wb{ee}_{tc}")
            b.op("vector", (lambda: lambda e: e.tensor_copy(out=wbcs.ap(BF16, 512), in_=PS[6].ap(F32, 512)))(),
                 rd=[PS[6].key()], wr=[wbcs.key()], tag=f"l2p_wbe{ee}_{tc}")
            for jt in range(8):
                bk = jt % 4
                for kt in range(8):
                    b.op("tensor", (lambda kt=kt, jt=jt, tc=tc, bk=bk: lambda e: e.matmul(PS[bk].ap(F32, 512), Me.ap(BF16, 128, coff=kt * 1024 + jt * 128), y2fm.ap(BF16, 512, coff=kt * 2048 + tc * 512), start=(kt == 0), stop=(kt == 7)))(),
                         rd=[Me.key(BF16, 128, coff=kt * 1024 + jt * 128), y2fm.key(BF16, 512, coff=kt * 2048 + tc * 512)], wr=[PS[bk].key()], tag=f"l2p_mm{ee}_{tc}_{jt}_{kt}")
                b.op("vector", (lambda bk=bk: lambda e: e.tensor_tensor(out=tmps.ap(BF16, 512), in0=PS[bk].ap(F32, 512), in1=wbcs.ap(BF16, 512), op=ALU.mult))(),
                     rd=[PS[bk].key(), wbcs.key()], wr=[tmps.key()], tag=f"l2p_cw{ee}_{tc}_{jt}")
                b.op("vector", (lambda jt=jt, tc=tc: lambda e: e.tensor_tensor(out=accfm.ap(BF16, 512, coff=jt * 2048 + tc * 512), in0=accfm.ap(BF16, 512, coff=jt * 2048 + tc * 512), in1=tmps.ap(BF16, 512), op=ALU.add))(),
                     rd=[accfm.key(BF16, 512, coff=jt * 2048 + tc * 512), tmps.key()], wr=[accfm.key(BF16, 512, coff=jt * 2048 + tc * 512)], tag=f"l2p_ca{ee}_{tc}_{jt}")
    osc = env.ST.alloc("l2p_osc", 512)
    bcast_scalar(b, env, "l2a_oupsc", osc, 0)
    obc = A.alloc("l2p_obc", 512, top=True)
    b.dma("sync", lambda e: e.dma_start(out=obc.ap(F32, 8), in_=d["l2a_oupbif"][:, :]),
          rd=[("l2a_oupbif", 0, 1)], wr=[obc.key()], tag="l2p_ldob")
    for ft in range(8):
        b.op("vector", (lambda ft=ft: lambda e: e.tensor_scalar(out=accfm.ap(BF16, 2048, coff=ft * 2048), in0=accfm.ap(BF16, 2048, coff=ft * 2048), scalar1=osc.ap(F32, 1), scalar2=obc.ap(F32, 1, coff=ft), op0=ALU.mult, op1=ALU.add))(),
             rd=[accfm.key(BF16, 2048, coff=ft * 2048), osc.key(F32, 1), obc.key()], wr=[accfm.key(BF16, 2048, coff=ft * 2048)], tag=f"l2p_sb{ft}")
        b.op("vector", (lambda ft=ft: lambda e: e.tensor_tensor(out=env.XR.ap(F32, 2048, coff=ft * 2048), in0=env.XR.ap(F32, 2048, coff=ft * 2048), in1=accfm.ap(BF16, 2048, coff=ft * 2048), op=ALU.add))(),
             rd=[env.XR.key(F32, 2048, coff=ft * 2048), accfm.key(BF16, 2048, coff=ft * 2048)], wr=[env.XR.key(F32, 2048, coff=ft * 2048)], tag=f"l2p_res{ft}")
    if "l2d_acc" in env.dram:
        emit_dbg(b, env, "l2d_acc", accfm, dtype=BF16)


BLOCKS["l2a"] = (emit_l2a, dram_l2a, host_l2a)

